# version 07
# baseline (speedup 1.0000x reference)
import numpy as np
import ml_dtypes as mld
from contextlib import ExitStack

import concourse.bass as bass
import concourse.mybir as mybir
from concourse.alu_op_type import AluOpType as OP
from concourse.bass_utils import run_bass_kernel_spmd

F32 = mybir.dt.float32
BF16 = mybir.dt.bfloat16
U32 = mybir.dt.uint32
AF = mybir.ActivationFunctionType
AXX = mybir.AxisListType.X
IOA = bass.IndirectOffsetOnAxis

NCORE = 8
D = 2048
T = 2048
S = 4096
NDT = D // 128
NTC = T // 512
ROT = 1024
NEG = -1e9
QSCALE = 192.0 ** -0.5



def _rope_perm(n):
    return np.concatenate([np.arange(0, ROT, 2), np.arange(1, ROT, 2),
                           np.arange(ROT, n)]).astype(np.int64)


def host_prep(inp):
    p = {}
    perm = _rope_perm(D)
    perm_all = np.concatenate([perm, perm + D, perm + 2 * D])
    W_in = np.asarray(inp["L0_attn_W_in"])
    p["winT"] = np.ascontiguousarray(W_in[perm_all].T)
    p["binp"] = np.ascontiguousarray(
        np.asarray(inp["L0_attn_b_in"])[perm_all].reshape(48, 128).T)
    W_out = np.asarray(inp["L0_attn_W_out"])
    p["woutT"] = np.ascontiguousarray(W_out[:, perm].T)
    p["bout"] = np.ascontiguousarray(
        np.asarray(inp["L0_attn_b_out"]).reshape(16, 128).T)
    p["wq"] = np.ascontiguousarray(np.asarray(inp["L0_ffn_Wq"]))
    keys = np.asarray(inp["L0_ffn_keys"])
    p["keysT"] = np.ascontiguousarray(
        keys.transpose(0, 2, 3, 1).reshape(8, 64, 128))
    p["e_cat"] = np.ascontiguousarray(
        np.concatenate([np.asarray(inp["L0_ffn_e_down"]),
                        np.asarray(inp["L0_ffn_e_up"])], axis=1))

    p["qprojT"] = np.ascontiguousarray(np.asarray(inp["L1_attn_q_proj_w"]).T)
    p["kvaT"] = np.ascontiguousarray(np.asarray(inp["L1_attn_kv_a_proj_w"]).T)
    p["lnw"] = np.ascontiguousarray(
        np.asarray(inp["L1_attn_kv_a_ln_w"]).reshape(4, 128).T)
    kvbT = np.asarray(inp["L1_attn_kv_b_proj_w"]).T
    nope_cols = (np.arange(16)[:, None] * 256 + np.arange(128)[None]).reshape(-1)
    p["kvbT_nope"] = np.ascontiguousarray(kvbT[:, nope_cols])
    p["kvbT_v"] = np.ascontiguousarray(kvbT[:, nope_cols + 128])
    p["oprojT"] = np.ascontiguousarray(np.asarray(inp["L1_attn_o_proj_w"]).T)
    p["wgu"] = np.ascontiguousarray(np.asarray(inp["L1_ffn_w_gate_up"]))
    p["wproj"] = np.ascontiguousarray(np.asarray(inp["L1_ffn_w_proj"]))

    X = np.asarray(inp["L0_attn_X"])
    p["xT"] = [np.ascontiguousarray(X[c // 2, (c % 2) * T:(c % 2 + 1) * T, :].T)
               for c in range(NCORE)]

    invf = 10000.0 ** (-(np.arange(512, dtype=np.float64) * 2.0 / ROT))
    ang = np.arange(S, dtype=np.float64)[None, :] * invf[:, None]
    p["cosT"] = np.cos(ang).astype(mld.bfloat16)
    p["sinT"] = np.sin(ang).astype(mld.bfloat16)
    p["ones1"] = np.ones((1, 128), dtype=mld.bfloat16)
    p["onescol"] = np.ones((128, 1), dtype=mld.bfloat16)
    p["ident"] = np.eye(128, dtype=np.float32).astype(mld.bfloat16)
    p["iota64"] = np.broadcast_to(np.arange(64, dtype=np.float32), (128, 64)).copy()
    masks = np.zeros((4, 128, 512), dtype=np.float32)
    for j in range(4):
        masks[j] = np.where(np.arange(128)[:, None] + 128 * j > np.arange(512)[None, :],
                            NEG, 0.0)
    p["masks"] = masks
    return p



class Sem:
    def __init__(self, nc, ctx, name):
        self.h = ctx.enter_context(nc.semaphore(name))
        self.v = 0

    def inc(self, n=1):
        self.v += n
        return self.v


class Prog:
    def __init__(self, name):
        self.nc = bass.Bass("TRN2", target_bir_lowering=False)
        self.name = name
        self.ctx = ExitStack()
        self.pctx = None
        self.prefix = ""
        self.block = None

    def sem(self, name):
        return Sem(self.nc, self.ctx, self.prefix + name)

    def sb(self, name, shape, dt, ctx=None):
        return (ctx or self.pctx or self.ctx).enter_context(
            self.nc.sbuf_tensor(self.prefix + name, list(shape), dt))

    def ps(self, name, shape, dt=F32, ctx=None):
        return (ctx or self.pctx or self.ctx).enter_context(
            self.nc.psum_tensor(self.prefix + name, list(shape), dt))

    def din(self, name, shape, dt):
        return self.nc.dram_tensor(name, list(shape), dt, kind="ExternalInput").ap()

    def dout(self, name, shape, dt):
        return self.nc.dram_tensor(name, list(shape), dt, kind="ExternalOutput").ap()

    def dtmp(self, name, shape, dt):
        return self.nc.dram_tensor(self.prefix + name, list(shape), dt,
                                   kind="Internal").ap()


def mark(eng, msem):
    eng.drain()
    eng.nop().then_inc(msem.h, 1)
    msem.inc()
    return msem.v


def tmark(t, msem):
    t.nop().then_inc(msem.h, 1)
    msem.inc()
    return msem.v


def get_io(pr, io, name, shape, dt, out=False):
    if io is not None and name in io:
        return io[name]
    return (pr.dout if out else pr.din)(name, list(shape), dt)


def gate_all(block, gate):
    if gate is None:
        return
    for emit in (block.sync, block.tensor, block.vector, block.scalar,
                 block.gpsimd):
        emit(lambda e, g=gate: e.wait_ge(g[0], g[1]))


class Stream:

    def __init__(self, pr, name, nslot):
        self.sems = [pr.sem(f"{name}{i}") for i in range(nslot)]
        self.n = nslot

    def issue(self, eng, slot, dmas):
        sem = self.sems[slot % self.n]
        for out, in_, kw in dmas:
            eng.dma_start(out=out, in_=in_, **kw).then_inc(sem.h, 16)
            sem.inc(16)
        return (sem.h, sem.v)

    def totals(self):
        return [(s.h, s.v) for s in self.sems]


def wait_all(eng, targets):
    for h, v in targets:
        if v > 0:
            eng.wait_ge(h, v)



def emit_ln(pr, b, src_dram, xnT, ms, eps=1e-5, tag="ln"):
    block = pr.block
    vm, sm, pe = ms["vm"], ms["sm"], ms["pe"]
    ctx = ExitStack()
    xf = [pr.sb(f"{tag}_xf{i}", [128, T], F32, ctx) for i in range(2)]
    xb = pr.sb(f"{tag}_xb", [128, T], BF16, ctx)
    xsq = pr.sb(f"{tag}_xsq", [128, T], BF16, ctx)
    m_row = pr.sb(f"{tag}_m", [1, T], F32, ctx)
    r_row = pr.sb(f"{tag}_r", [1, T], F32, ctx)
    sc_row = pr.sb(f"{tag}_sc", [1, T], F32, ctx)
    rb16 = pr.sb(f"{tag}_rb16", [1, T], BF16, ctx)
    mrb16 = pr.sb(f"{tag}_mrb16", [1, T], BF16, ctx)
    Rb = pr.sb(f"{tag}_Rb", [128, T], BF16, ctx)
    MRb = pr.sb(f"{tag}_MRb", [128, T], BF16, ctx)
    tmp = pr.sb(f"{tag}_tmp", [128, T], F32, ctx)
    psA = pr.ps(f"{tag}_psA", [128, 2048], F32, ctx)
    psB = pr.ps(f"{tag}_psB", [128, 2048], F32, ctx)
    s_x = Stream(pr, f"{tag}sx", 2)
    ocol, o1 = b["ocol"], b["o1"]

    cons = []
    for dt in range(NDT):
        def qf(q, dt=dt):
            if dt >= 2:
                q.wait_ge(vm.h, cons[dt - 2][0])
                q.wait_ge(sm.h, cons[dt - 2][1])
            s_x.issue(q, dt, [(xf[dt % 2][:],
                               src_dram[128 * dt:128 * (dt + 1), :], {})])
        block.sync(qf)
        lt = (s_x.sems[dt % 2].h, s_x.sems[dt % 2].v)

        def vf(v, dt=dt, lt=lt):
            v.wait_ge(*lt)
            if dt >= 1:
                v.wait_ge(pe.h, pe.v)
            v.tensor_copy(xb[:], xf[dt % 2][:])
            mark(v, vm)
        block.vector(vf)

        def sf(s, dt=dt, lt=lt):
            s.wait_ge(*lt)
            if dt >= 1:
                s.wait_ge(pe.h, pe.v)
            s.activation(xsq[:], xf[dt % 2][:], AF.Square)
            mark(s, sm)
        block.scalar(sf)
        cons.append((vm.v, sm.v))

        def tf(t, dt=dt, vv=vm.v, sv=sm.v):
            t.wait_ge(vm.h, vv)
            t.wait_ge(sm.h, sv)
            for tc in range(NTC):
                t.matmul(psA[0:1, 512 * tc:512 * (tc + 1)], ocol[:],
                         xb[:, 512 * tc:512 * (tc + 1)],
                         start=(dt == 0), stop=(dt == NDT - 1))
                t.matmul(psB[0:1, 512 * tc:512 * (tc + 1)], ocol[:],
                         xsq[:, 512 * tc:512 * (tc + 1)],
                         start=(dt == 0), stop=(dt == NDT - 1))
            mark(t, pe)
        block.tensor(tf)

    def vrows(v, pv=pe.v):
        v.wait_ge(pe.h, pv)
        v.tensor_scalar(m_row[:], psA[0:1, :], 1.0 / D, None, OP.mult)
        v.tensor_scalar(sc_row[:], psB[0:1, :], 1.0 / D, None, OP.mult)
        v.drain()
        v.tensor_tensor(r_row[:], m_row[:], m_row[:], OP.mult)
        v.drain()
        v.tensor_tensor(sc_row[:], sc_row[:], r_row[:], OP.subtract)
        v.drain()
        v.tensor_scalar(sc_row[:], sc_row[:], eps, None, OP.add)
        mark(v, vm)
    block.vector(vrows)

    def srows(s, vv=vm.v):
        s.wait_ge(vm.h, vv)
        s.activation(r_row[:], sc_row[:], AF.Ln)
        s.drain()
        s.activation(r_row[:], r_row[:], AF.Exp, scale=-0.5)
        mark(s, sm)
    block.scalar(srows)

    def vmr(v, sv=sm.v):
        v.wait_ge(sm.h, sv)
        v.tensor_tensor(sc_row[:], m_row[:], r_row[:], OP.mult)
        v.drain()
        v.tensor_copy(rb16[:], r_row[:])
        v.tensor_copy(mrb16[:], sc_row[:])
        mark(v, vm)
    block.vector(vmr)

    def tbc(t, vv=vm.v):
        t.wait_ge(vm.h, vv)
        for tc in range(NTC):
            t.matmul(psA[:, 512 * tc:512 * (tc + 1)], o1[:],
                     rb16[:, 512 * tc:512 * (tc + 1)], start=True, stop=True)
            t.matmul(psB[:, 512 * tc:512 * (tc + 1)], o1[:],
                     mrb16[:, 512 * tc:512 * (tc + 1)], start=True, stop=True)
        mark(t, pe)
    block.tensor(tbc)

    def vbc(v, pv=pe.v):
        v.wait_ge(pe.h, pv)
        v.tensor_copy(Rb[:], psA[:])
        v.tensor_copy(MRb[:], psB[:])
        mark(v, vm)
    block.vector(vbc)
    bc = vm.v

    marks = []
    for dt in range(NDT):
        def qf2(q, dt=dt):
            q.wait_ge(vm.h, bc if dt < 2 else marks[dt - 2])
            s_x.issue(q, dt, [(xf[dt % 2][:],
                               src_dram[128 * dt:128 * (dt + 1), :], {})])
        block.sync(qf2)
        lt = (s_x.sems[dt % 2].h, s_x.sems[dt % 2].v)

        def vf2(v, dt=dt, lt=lt):
            v.wait_ge(*lt)
            v.tensor_tensor(tmp[:], xf[dt % 2][:], Rb[:], OP.mult)
            v.drain()
            v.tensor_tensor(xnT[dt][:], tmp[:], MRb[:], OP.subtract)
            mark(v, vm)
        block.vector(vf2)
        marks.append(vm.v)

    ctx.close()
    return vm.v



def build_l0a(cut="all", pr=None, io=None, gate=None):
    if pr is None:
        pr = Prog("l0a")
    nc = pr.nc
    xT = get_io(pr, io, "xT", [D, T], F32)
    winT = get_io(pr, io, "winT", [D, 3 * D], F32)
    binp = get_io(pr, io, "binp", [128, 48], F32)
    cosT_d = get_io(pr, io, "cosT", [512, T], BF16)
    sinT_d = get_io(pr, io, "sinT", [512, T], BF16)
    wq_d = get_io(pr, io, "wq", [D, 512], F32)
    keysT = get_io(pr, io, "keysT", [8, 64, 128], F32)
    ones1 = get_io(pr, io, "ones1", [1, 128], BF16)
    onescol = get_io(pr, io, "onescol", [128, 1], BF16)
    iota64 = get_io(pr, io, "iota64", [128, 64], F32)

    xnT_o = get_io(pr, io, "xnT_o", [D, T], BF16, out=True)
    qT_o = get_io(pr, io, "qT_o", [D, T], BF16, out=True)
    apart_o = get_io(pr, io, "apart_o", [128, 16], F32, out=True)
    fi_o = get_io(pr, io, "fi_o", [16, 128, 32], U32, out=True)
    fs_o = get_io(pr, io, "fs_o", [16, 128, 32], F32, out=True)

    with pr.ctx:
        xnT = [pr.sb(f"xnT{i}", [128, T], BF16) for i in range(NDT)]
        o1 = pr.sb("o1", [1, 128], BF16)
        ocol = pr.sb("ocol", [128, 1], BF16)
        bcol = pr.sb("bcol", [128, 48], F32)
        acol = pr.sb("acol", [128, 16], F32)

        pe = pr.sem("pe")
        vm = pr.sem("vm")
        sm = pr.sem("sm")
        st = pr.sem("st")
        ms = {"pe": pe, "vm": vm, "sm": sm, "st": st}
        s_const = Stream(pr, "cst", 1)
        s_w = Stream(pr, "sw", 2)
        s_cs = Stream(pr, "scs", 2)
        s_wq = Stream(pr, "swq", 1)
        s_qst = Stream(pr, "sqst", 1)
        s_tk = Stream(pr, "stk", 1)

        if pr.block is None:
            pr.block = pr.ctx.enter_context(nc.Block())
        block = pr.block
        gate_all(block, gate)
        b = {"o1": o1, "ocol": ocol}

        def qconst(q):
            s_const.issue(q, 0, [
                (o1[:], ones1[:], {}),
                (ocol[:], onescol[:], {}),
                (bcol[:], binp[:], {}),
            ])
        block.sync(qconst)
        const_t = s_const.totals()[0]

        def vconst(v):
            v.wait_ge(*const_t)
            mark(v, vm)
        block.vector(vconst)

        emit_ln(pr, b, xT, xnT, ms)
        xn_done = vm.v

        def qxn(q):
            q.wait_ge(vm.h, xn_done)
            for dt in range(NDT):
                q.dma_start(out=xnT_o[128 * dt:128 * (dt + 1), :],
                            in_=xnT[dt][:]).then_inc(st.h, 16)
                st.inc(16)
        block.sync(qxn)

        if cut == "ln":
            def qfin0(q):
                q.wait_ge(st.h, st.v)
            block.sync(qfin0)
            pr.ctx.pop_all()
            return pr

        ctx2 = ExitStack()
        ktT = [pr.sb(f"ktT{i}", [128, T], BF16, ctx2) for i in range(NDT)]
        cosb = [pr.sb(f"cosb{i}", [128, T], BF16, ctx2) for i in range(2)]
        sinb = [pr.sb(f"sinb{i}", [128, T], BF16, ctx2) for i in range(2)]
        p1 = pr.sb("p1", [128, T], F32, ctx2)
        p2 = pr.sb("p2", [128, T], F32, ctx2)
        sqt = pr.sb("sqt", [128, T], BF16, ctx2)
        vtile = pr.sb("vtile", [128, T], BF16, ctx2)
        fkt = pr.sb("fkt", [128, T], BF16, ctx2)
        rrow = pr.sb("rrow", [1, T], F32, ctx2)
        rn16 = pr.sb("rn16", [1, T], BF16, ctx2)
        RNb = pr.sb("RNb", [128, T], BF16, ctx2)
        scr = pr.sb("scr", [128, T], BF16, ctx2)
        qtile = pr.sb("qtile", [128, T], BF16, ctx2)
        wbuf = [pr.sb(f"wb{i}", [128, NDT * 128], BF16, ctx2) for i in range(2)]
        psA = pr.ps("psA", [128, 2048], F32, ctx2)
        psB = pr.ps("psB", [128, 2048], F32, ctx2)

        cs_t = {}

        def csload(q, i, buf):
            cs_t[i] = s_cs.issue(q, buf, [
                (cosb[buf][:], cosT_d[128 * i:128 * (i + 1), :], {}),
                (sinb[buf][:], sinT_d[128 * i:128 * (i + 1), :], {}),
            ])

        def qcs(q, xv=xn_done):
            q.wait_ge(vm.h, xv)
            csload(q, 0, 0)
            csload(q, 1, 1)
        block.sync(qcs)

        w_t = {}

        def wload(g, e, buf):
            w_t[e] = s_w.issue(g, buf, [(
                wbuf[buf][:].rearrange("p (dt c) -> p dt c", c=128),
                winT[:, 128 * e:128 * (e + 1)].rearrange("(dt p) c -> p dt c", p=128),
                {})])

        def mm_echunk(t, e, psX, buf):
            t.wait_ge(*w_t[e])
            for dt in range(NDT):
                for tc in range(NTC):
                    t.matmul(psX[:, 512 * tc:512 * (tc + 1)],
                             wbuf[buf][:, dt * 128:dt * 128 + 128],
                             xnT[dt][:, 512 * tc:512 * (tc + 1)],
                             start=(dt == 0), stop=(dt == NDT - 1))
            return mark(t, pe)

        def gk0(g, xv=xn_done):
            g.wait_ge(vm.h, xv)
            wload(g, 16, 0)
            wload(g, 20, 1)
        block.gpsimd(gk0)

        for i in range(4):
            def tk(t, i=i):
                if i > 0:
                    t.wait_ge(vm.h, vm.v)
                mm_echunk(t, 16 + i, psA, 0)
                mm_echunk(t, 20 + i, psB, 1)
            block.tensor(tk)
            pair_mm = pe.v

            if i < 3:
                def gk(g, i=i, pm=pair_mm):
                    g.wait_ge(pe.h, pm)
                    wload(g, 16 + i + 1, 0)
                    wload(g, 20 + i + 1, 1)
                block.gpsimd(gk)

            def vk(v, i=i, pm=pair_mm):
                v.wait_ge(pe.h, pm)
                v.wait_ge(*cs_t[i])
                cb, sb_ = cosb[i % 2], sinb[i % 2]
                v.scalar_tensor_tensor(p1[:], psA[:], bcol[:, 16 + i:17 + i], cb[:],
                                       OP.add, OP.mult)
                v.scalar_tensor_tensor(p2[:], psB[:], bcol[:, 20 + i:21 + i], sb_[:],
                                       OP.add, OP.mult)
                v.drain()
                v.tensor_tensor(ktT[i][:], p1[:], p2[:], OP.subtract)
                v.drain()
                v.scalar_tensor_tensor(p1[:], psB[:], bcol[:, 20 + i:21 + i], cb[:],
                                       OP.add, OP.mult)
                v.scalar_tensor_tensor(p2[:], psA[:], bcol[:, 16 + i:17 + i], sb_[:],
                                       OP.add, OP.mult)
                v.drain()
                v.tensor_tensor(ktT[4 + i][:], p1[:], p2[:], OP.add)
                mark(v, vm)
            block.vector(vk)

            if 0 < i < 3:
                def qcs2(q, i=i, vv=vm.v):
                    q.wait_ge(vm.h, vv)
                    csload(q, i + 1, (i + 1) % 2)
                block.sync(qcs2)

        if cut == "kpair":
            def qfin2(q, vv=vm.v):
                q.wait_ge(vm.h, vv)
                q.wait_ge(st.h, st.v)
            block.sync(qfin2)
            pr.ctx.pop_all()
            return pr

        def gkp(g, pv=pe.v):
            g.wait_ge(pe.h, pv)
            wload(g, 24, 0)
            wload(g, 25, 1)
        block.gpsimd(gkp)

        for j in range(8):
            def tkp(t, j=j, vv=vm.v):
                t.wait_ge(vm.h, vv if j == 0 else vm.v)
                mm_echunk(t, 24 + j, psA if j % 2 == 0 else psB, j % 2)
            block.tensor(tkp)
            mmv = pe.v
            if j < 6:
                def gkp2(g, j=j, mmv=mmv):
                    g.wait_ge(pe.h, mmv)
                    wload(g, 26 + j, j % 2)
                block.gpsimd(gkp2)

            def vkp(v, j=j, mmv=mmv):
                v.wait_ge(pe.h, mmv)
                v.tensor_scalar(ktT[8 + j][:], (psA if j % 2 == 0 else psB)[:],
                                bcol[:, 24 + j:25 + j], None, OP.add)
                mark(v, vm)
            block.vector(vkp)
        k_done = vm.v

        for dt in range(NDT):
            def sq(s, dt=dt, kd=k_done):
                s.wait_ge(vm.h, kd)
                if dt > 0:
                    s.wait_ge(pe.h, pe.v)
                s.activation(sqt[:], ktT[dt][:], AF.Square)
                mark(s, sm)
            block.scalar(sq)

            def tn(t, dt=dt, sv=sm.v):
                t.wait_ge(sm.h, sv)
                for tc in range(NTC):
                    t.matmul(psA[0:1, 512 * tc:512 * (tc + 1)], ocol[:],
                             sqt[:, 512 * tc:512 * (tc + 1)],
                             start=(dt == 0), stop=(dt == NDT - 1))
                mark(t, pe)
            block.tensor(tn)

        def vnorm(v, pv=pe.v):
            v.wait_ge(pe.h, pv)
            v.tensor_scalar(rrow[:], psA[0:1, :], 1.0, 1e-24, OP.mult, OP.max)
            mark(v, vm)
        block.vector(vnorm)

        def snorm(s, vv=vm.v):
            s.wait_ge(vm.h, vv)
            s.activation(rrow[:], rrow[:], AF.Ln)
            s.drain()
            s.activation(rrow[:], rrow[:], AF.Exp, scale=-0.5)
            mark(s, sm)
        block.scalar(snorm)

        def vnorm2(v, sv=sm.v):
            v.wait_ge(sm.h, sv)
            v.tensor_scalar(rrow[:], rrow[:], 1e12, None, OP.min)
            v.drain()
            v.tensor_copy(rn16[:], rrow[:])
            mark(v, vm)
        block.vector(vnorm2)

        def trb(t, vv=vm.v):
            t.wait_ge(vm.h, vv)
            for tc in range(NTC):
                t.matmul(psA[:, 512 * tc:512 * (tc + 1)], o1[:],
                         rn16[:, 512 * tc:512 * (tc + 1)], start=True, stop=True)
            mark(t, pe)
        block.tensor(trb)

        def vrb(v, pv=pe.v):
            v.wait_ge(pe.h, pv)
            v.tensor_copy(RNb[:], psA[:])
            v.memset(acol[:], 0.0)
            mark(v, vm)
        block.vector(vrb)
        rn_done = vm.v

        def gv0(g, pv=pe.v):
            g.wait_ge(pe.h, pv)
            wload(g, 32, 0)
            wload(g, 33, 1)
        block.gpsimd(gv0)

        for j in range(NDT):
            def tv(t, j=j, rd=rn_done):
                t.wait_ge(vm.h, rd if j == 0 else vm.v)
                mm_echunk(t, 32 + j, psB, j % 2)
            block.tensor(tv)
            mmv = pe.v
            if j < NDT - 2:
                def gv(g, j=j, mmv=mmv):
                    g.wait_ge(pe.h, mmv)
                    wload(g, 34 + j, j % 2)
                block.gpsimd(gv)

            def vv(v, j=j, mmv=mmv):
                v.wait_ge(pe.h, mmv)
                v.tensor_scalar(vtile[:], psB[:], bcol[:, 32 + j:33 + j], None, OP.add)
                v.drain()
                v.tensor_tensor(fkt[:], ktT[j][:], RNb[:], OP.mult)
                v.drain()
                v.scalar_tensor_tensor(scr[:], fkt[:], 1.0, vtile[:],
                                       OP.mult, OP.mult,
                                       accum_out=acol[:, j:j + 1])
                mark(v, vm)
            block.vector(vv)
        a_done = vm.v

        def qa(q):
            q.wait_ge(vm.h, a_done)
            q.dma_start(out=apart_o[:], in_=acol[:]).then_inc(st.h, 16)
            st.inc(16)
        block.sync(qa)

        def gq0(g, pv=pe.v):
            g.wait_ge(pe.h, pv)
            wload(g, 0, 0)
            wload(g, 4, 1)
        block.gpsimd(gq0)

        def qcsq(q, av=a_done):
            q.wait_ge(vm.h, av)
            csload(q, 0, 0)
            csload(q, 1, 1)
        block.sync(qcsq)

        for i in range(4):
            def tq(t, i=i, av=a_done):
                t.wait_ge(vm.h, av if i == 0 else vm.v)
                mm_echunk(t, i, psA, 0)
                mm_echunk(t, 4 + i, psB, 1)
            block.tensor(tq)
            pair_mm = pe.v
            if i < 3:
                def gq(g, i=i, pm=pair_mm):
                    g.wait_ge(pe.h, pm)
                    wload(g, i + 1, 0)
                    wload(g, 4 + i + 1, 1)
                block.gpsimd(gq)

            def vq(v, i=i, pm=pair_mm):
                v.wait_ge(pe.h, pm)
                v.wait_ge(*cs_t[i])
                cb, sb_ = cosb[i % 2], sinb[i % 2]
                v.scalar_tensor_tensor(p1[:], psA[:], bcol[:, i:i + 1], cb[:],
                                       OP.add, OP.mult)
                v.scalar_tensor_tensor(p2[:], psB[:], bcol[:, 4 + i:5 + i], sb_[:],
                                       OP.add, OP.mult)
                v.drain()
                v.tensor_tensor(qtile[:], p1[:], p2[:], OP.subtract)
                mark(v, vm)
            block.vector(vq)

            def qq(q, i=i, ev=vm.v):
                q.wait_ge(vm.h, ev)
                s_qst.issue(q, 0, [(qT_o[128 * i:128 * (i + 1), :], qtile[:], {})])
            block.sync(qq)
            qst_t = s_qst.totals()[0]

            def vq2(v, i=i, qt=qst_t):
                v.wait_ge(*qt)
                cb, sb_ = cosb[i % 2], sinb[i % 2]
                v.scalar_tensor_tensor(p1[:], psB[:], bcol[:, 4 + i:5 + i], cb[:],
                                       OP.add, OP.mult)
                v.scalar_tensor_tensor(p2[:], psA[:], bcol[:, i:i + 1], sb_[:],
                                       OP.add, OP.mult)
                v.drain()
                v.tensor_tensor(qtile[:], p1[:], p2[:], OP.add)
                mark(v, vm)
            block.vector(vq2)

            def qq2(q, i=i, ev=vm.v):
                q.wait_ge(vm.h, ev)
                s_qst.issue(q, 0, [(qT_o[128 * (4 + i):128 * (5 + i), :], qtile[:], {})])
            block.sync(qq2)
            qst_t2 = s_qst.totals()[0]

            def vq3(v, qt=qst_t2):
                v.wait_ge(*qt)
                mark(v, vm)
            block.vector(vq3)

            if 0 < i < 3:
                def qcsq2(q, i=i, vv=vm.v):
                    q.wait_ge(vm.h, vv)
                    csload(q, i + 1, (i + 1) % 2)
                block.sync(qcsq2)

        def gqp(g, pv=pe.v):
            g.wait_ge(pe.h, pv)
            wload(g, 8, 0)
            wload(g, 9, 1)
        block.gpsimd(gqp)

        for j in range(8):
            def tqp(t, j=j):
                t.wait_ge(vm.h, vm.v)
                mm_echunk(t, 8 + j, psA if j % 2 == 0 else psB, j % 2)
            block.tensor(tqp)
            mmv = pe.v
            if j < 6:
                def gqp2(g, j=j, mmv=mmv):
                    g.wait_ge(pe.h, mmv)
                    wload(g, 10 + j, j % 2)
                block.gpsimd(gqp2)

            def vqp(v, j=j, mmv=mmv):
                v.wait_ge(pe.h, mmv)
                v.tensor_scalar(qtile[:], (psA if j % 2 == 0 else psB)[:],
                                bcol[:, 8 + j:9 + j], None, OP.add)
                mark(v, vm)
            block.vector(vqp)

            def qqp(q, j=j, ev=vm.v):
                q.wait_ge(vm.h, ev)
                s_qst.issue(q, 0, [(qT_o[128 * (8 + j):128 * (9 + j), :], qtile[:], {})])
            block.sync(qqp)
            qst_t = s_qst.totals()[0]

            def vqp2(v, qt=qst_t):
                v.wait_ge(*qt)
                mark(v, vm)
            block.vector(vqp2)

        ctx2.close()
        qkv_end = vm.v

        if cut == "qkv":
            def qfin1(q):
                q.wait_ge(vm.h, qkv_end)
                q.wait_ge(st.h, st.v)
                wait_all(q, s_qst.totals())
            block.sync(qfin1)
            pr.ctx.pop_all()
            return pr

        ctx3 = ExitStack()
        qfT = [pr.sb(f"qfT{i}", [128, T], BF16, ctx3) for i in range(4)]
        wqb = pr.sb("wqb", [128, NDT * 512], BF16, ctx3)
        keysb = pr.sb("keysb", [128, 8 * 128], BF16, ctx3)
        iob = pr.sb("iob", [128, 64], F32, ctx3)
        s1 = pr.sb("s1", [128, 8], F32, ctx3)
        s2 = pr.sb("s2", [128, 8], F32, ctx3)
        i1 = pr.sb("i1", [128, 8], U32, ctx3)
        i2 = pr.sb("i2", [128, 8], U32, ctx3)
        i1f = pr.sb("i1f", [128, 8], F32, ctx3)
        i2f = pr.sb("i2f", [128, 8], F32, ctx3)
        i1m = pr.sb("i1m", [128, 8], F32, ctx3)
        asc = pr.sb("asc", [128, 64], F32, ctx3)
        aidx = pr.sb("aidx", [128, 64], F32, ctx3)
        fs_t = pr.sb("fs_t", [128, 32], F32, ctx3)
        fsh = pr.sb("fsh", [128, 8], F32, ctx3)
        pk_t = pr.sb("pk_t", [128, 8], U32, ctx3)
        pkf = pr.sb("pkf", [128, 8], F32, ctx3)
        fi_f = pr.sb("fi_f", [128, 32], F32, ctx3)
        fi_u = pr.sb("fi_u", [128, 32], U32, ctx3)
        eqm = pr.sb("eqm", [128, 64], F32, ctx3)
        prodm = pr.sb("prodm", [128, 64], F32, ctx3)
        psF = pr.ps("psF", [128, 2048], F32, ctx3)
        psS = pr.ps("psS", [128, 2048], F32, ctx3)

        def gw(g, vv=qkv_end):
            g.wait_ge(vm.h, vv)
            s_wq.issue(g, 0, [
                (wqb[:].rearrange("p (dt c) -> p dt c", c=512),
                 wq_d.rearrange("(dt p) c -> p dt c", p=128), {}),
                (keysb[0:64, :].rearrange("p (a c) -> p a c", c=128),
                 keysT.rearrange("a b c -> b a c"), {}),
                (keysb[64:128, :].rearrange("p (a c) -> p a c", c=128),
                 keysT.rearrange("a b c -> b a c"), {}),
                (iob[:], iota64[:], {}),
            ])
        block.gpsimd(gw)
        wq_t = s_wq.totals()[0]

        for cc in range(4):
            def tqf(t, cc=cc, wt=wq_t, vv=qkv_end):
                t.wait_ge(*wt)
                t.wait_ge(vm.h, vv)
                if cc > 0:
                    t.wait_ge(sm.h, sm.v)
                for dt in range(NDT):
                    for tc in range(NTC):
                        t.matmul(psF[:, 512 * tc:512 * (tc + 1)],
                                 wqb[:, dt * 512 + cc * 128:dt * 512 + cc * 128 + 128],
                                 xnT[dt][:, 512 * tc:512 * (tc + 1)],
                                 start=(dt == 0), stop=(dt == NDT - 1))
                mark(t, pe)
            block.tensor(tqf)

            def sqf(s, cc=cc, pv=pe.v):
                s.wait_ge(pe.h, pv)
                s.copy(qfT[cc][:], psF[:])
                mark(s, sm)
            block.scalar(sqf)
        qf_done = sm.v

        if cut == "qf":
            def qfinq(q):
                q.wait_ge(sm.h, qf_done)
                q.wait_ge(st.h, st.v)
            block.sync(qfinq)
            pr.ctx.pop_all()
            return pr

        for tb in range(NDT):
            def tsim(t, tb=tb, qd=qf_done, wt=wq_t):
                t.wait_ge(sm.h, qd)
                t.wait_ge(*wt)
                if tb > 0:
                    t.wait_ge(vm.h, vm.v)
                for h in range(4):
                    for w in range(2):
                        b_ = 2 * h + w
                        psX = psF if b_ < 4 else psS
                        t.matmul(psX[:, 512 * (b_ % 4):512 * (b_ % 4) + 128],
                                 qfT[h][64 * w:64 * (w + 1), 128 * tb:128 * (tb + 1)],
                                 keysb[64 * w:64 * (w + 1),
                                       (2 * h + w) * 128:(2 * h + w + 1) * 128],
                                 start=True, stop=True)
                mark(t, pe)
            block.tensor(tsim)

            def vtk(v, tb=tb, pv=pe.v):
                v.wait_ge(pe.h, pv)
                for h in range(4):
                    b1, b2 = 2 * h, 2 * h + 1
                    sl1 = (psF if b1 < 4 else psS)[:, 512 * (b1 % 4):512 * (b1 % 4) + 128]
                    sl2 = (psF if b2 < 4 else psS)[:, 512 * (b2 % 4):512 * (b2 % 4) + 128]
                    v.max(s1[:], sl1)
                    v.max(s2[:], sl2)
                    v.drain()
                    v.max_index(i1[:], s1[:], sl1)
                    v.max_index(i2[:], s2[:], sl2)
                    v.drain()
                    v.tensor_copy(i1f[:], i1[:])
                    v.tensor_copy(i2f[:], i2[:])
                    v.drain()
                    v.tensor_scalar(i1m[:], i1f[:], 128.0, None, OP.mult)
                    v.drain()
                    for j in range(8):
                        v.tensor_scalar(asc[:, 8 * j:8 * j + 8], s2[:],
                                        s1[:, j:j + 1], None, OP.add)
                        v.tensor_scalar(aidx[:, 8 * j:8 * j + 8], i2f[:],
                                        i1m[:, j:j + 1], None, OP.add)
                    v.drain()
                    v.max(fsh[:], asc[:])
                    v.drain()
                    v.max_index(pk_t[:], fsh[:], asc[:])
                    v.tensor_copy(fs_t[:, 8 * h:8 * h + 8], fsh[:])
                    v.drain()
                    v.tensor_copy(pkf[:], pk_t[:])
                    v.drain()
                    for j in range(8):
                        v.tensor_scalar(eqm[:], iob[:], pkf[:, j:j + 1],
                                        None, OP.is_equal)
                        v.drain()
                        v.tensor_tensor(prodm[:], eqm[:], aidx[:], OP.mult)
                        v.drain()
                        v.tensor_reduce(fi_f[:, 8 * h + j:8 * h + j + 1], prodm[:],
                                        axis=AXX, op=OP.add)
                        v.drain()
                v.tensor_copy(fi_u[:], fi_f[:])
                mark(v, vm)
            block.vector(vtk)

            def qtk(q, tb=tb, ev=vm.v):
                q.wait_ge(vm.h, ev)
                s_tk.issue(q, 0, [(fi_o[tb], fi_u[:], {}),
                                  (fs_o[tb], fs_t[:], {})])
            block.sync(qtk)
            tk_t = s_tk.totals()[0]

            def vtk2(v, tt=tk_t):
                v.wait_ge(*tt)
                mark(v, vm)
            block.vector(vtk2)

        ctx3.close()

        def qfin(q):
            q.wait_ge(st.h, st.v)
            wait_all(q, s_qst.totals())
            wait_all(q, s_tk.totals())
        block.sync(qfin)
        pr.finish = [(pe.h, pe.v), (vm.h, vm.v), (sm.h, sm.v), (st.h, st.v)]
        pr.ctx.pop_all()

    return pr



def l0a_inmaps(prep):
    cos, sin = prep["cosT"], prep["sinT"]
    maps = []
    for c in range(NCORE):
        r = c % 2
        maps.append({
            "xT": prep["xT"][c],
            "winT": prep["winT"],
            "binp": prep["binp"],
            "cosT": np.ascontiguousarray(cos[:, r * T:(r + 1) * T]),
            "sinT": np.ascontiguousarray(sin[:, r * T:(r + 1) * T]),
            "wq": prep["wq"],
            "keysT": prep["keysT"],
            "ones1": prep["ones1"],
            "onescol": prep["onescol"],
            "iota64": prep["iota64"],
        })
    return maps





def build_l0b(pr=None, io=None, gate=None):
    if pr is None:
        pr = Prog("l0b")
    nc = pr.nc
    summed = io is not None and "ap1" not in io
    xT = get_io(pr, io, "xT", [D, T], F32)
    xnT_i = get_io(pr, io, "xnT_i", [D, T], BF16)
    qT_i = get_io(pr, io, "qT_i", [D, T], BF16)
    ap0 = get_io(pr, io, "ap0", [128, 16], F32)
    ap1 = None if summed else get_io(pr, io, "ap1", [128, 16], F32)
    fi_i = get_io(pr, io, "fi_i", [16, 128, 32], U32)
    fs_i = get_io(pr, io, "fs_i", [16, 128, 32], F32)
    ecat = get_io(pr, io, "ecat", [16384, 4096], F32)
    woutT = get_io(pr, io, "woutT", [D, D], F32)
    bout_d = get_io(pr, io, "bout_d", [128, 16], F32)
    ident_d = get_io(pr, io, "ident_d", [128, 128], BF16)
    ones1 = get_io(pr, io, "ones1", [1, 128], BF16)
    onescol = get_io(pr, io, "onescol", [128, 1], BF16)

    x1T_o = get_io(pr, io, "x1T_o", [D, T], F32, out=True)
    xn2T_o = get_io(pr, io, "xn2T_o", [D, T], BF16, out=True)
    x1s = io["x1s"] if (io is not None and "x1s" in io) else pr.dtmp("x1s", [D, T], F32)

    with pr.ctx:
        xnT = [pr.sb(f"xnT{i}", [128, T], BF16) for i in range(NDT)]
        fT = [pr.sb(f"fT{i}", [128, T], BF16) for i in range(NDT)]
        o1 = pr.sb("o1", [1, 128], BF16)
        ocol = pr.sb("ocol", [128, 1], BF16)
        ident = pr.sb("ident", [128, 128], BF16)
        bout = pr.sb("bout", [128, 16], F32)
        acol = pr.sb("acol", [128, 16], F32)
        aps = [pr.sb(f"aps{i}", [128, 16], F32) for i in range(2)]

        pe = pr.sem("pe")
        vm = pr.sem("vm")
        sm = pr.sem("sm")
        st = pr.sem("st")
        ms = {"pe": pe, "vm": vm, "sm": sm, "st": st}
        s_const = Stream(pr, "cst", 1)
        s_xn = Stream(pr, "sxn", 1)
        s_g = Stream(pr, "sg", 2)
        s_tk = Stream(pr, "stk", 2)
        s_w = Stream(pr, "sw", 2)
        s_q = Stream(pr, "sq", 2)
        s_xf = Stream(pr, "sxf", 2)

        if pr.block is None:
            pr.block = pr.ctx.enter_context(nc.Block())
        block = pr.block
        gate_all(block, gate)
        b = {"o1": o1, "ocol": ocol}

        def qc(q):
            dmas = [
                (o1[:], ones1[:], {}),
                (ocol[:], onescol[:], {}),
                (ident[:], ident_d[:], {}),
                (bout[:], bout_d[:], {}),
                (aps[0][:], ap0[:], {}),
            ]
            if not summed:
                dmas.append((aps[1][:], ap1[:], {}))
            s_const.issue(q, 0, dmas)
            for dt in range(NDT):
                s_xn.issue(q, 0, [(xnT[dt][:], xnT_i[128 * dt:128 * (dt + 1), :], {})])
        block.sync(qc)
        const_t = s_const.totals()[0]
        xn_t = s_xn.totals()[0]

        def va(v):
            v.wait_ge(*const_t)
            if summed:
                v.tensor_copy(acol[:], aps[0][:])
            else:
                v.tensor_tensor(acol[:], aps[0][:], aps[1][:], OP.add)
            mark(v, vm)
        block.vector(va)
        a_done = vm.v

        ctx1 = ExitStack()
        xnat = pr.sb("xnat", [128, D], BF16, ctx1)
        gbuf = [pr.sb(f"gbuf{i}", [128, 4096], BF16, ctx1) for i in range(2)]
        dscr = [pr.sb(f"dscr{i}", [128, D], BF16, ctx1) for i in range(2)]
        diag = [pr.sb(f"diag{i}", [128, 128], BF16, ctx1) for i in range(2)]
        idxb = pr.sb("idxb", [128, 32], U32, ctx1)
        fsb = pr.sb("fsb", [128, 32], F32, ctx1)
        dcol = pr.sb("dcol", [128, 32], F32, ctx1)
        sil = pr.sb("sil", [128, 32], F32, ctx1)
        hcol = pr.sb("hcol", [128, 32], F32, ctx1)
        fnat = pr.sb("fnat", [128, D], BF16, ctx1)
        psU = pr.ps("psU", [128, 2048], F32, ctx1)
        psT = pr.ps("psT", [128, 128], BF16, ctx1)

        for tb in range(NDT):
            def qtk(q, tb=tb, vv=vm.v):
                q.wait_ge(vm.h, vv)
                s_tk.issue(q, tb, [(idxb[:], fi_i[tb], {}),
                                   (fsb[:], fs_i[tb], {})])
            block.sync(qtk)
            tk_t = (s_tk.sems[tb % 2].h, s_tk.sems[tb % 2].v)

            def tx(t, tb=tb, xv=xn_t, cv=const_t, pv=vm.v):
                t.wait_ge(*xv)
                t.wait_ge(*cv)
                t.wait_ge(vm.h, pv)
                for dt in range(NDT):
                    t.transpose(psT[:], xnT[dt][:, 128 * tb:128 * (tb + 1)], ident[:])
                    t.drain()
                    t.nop().then_inc(pe.h, 1)
                    pe.inc()
                    t.wait_ge(sm.h, sm.v + 1)
                    sm.inc()
            block.tensor(tx)

            def sx(s, tb=tb, pv0=pe.v - NDT, sv0=sm.v - NDT):
                for dt in range(NDT):
                    s.wait_ge(pe.h, pv0 + dt + 1)
                    s.copy(xnat[:, 128 * dt:128 * (dt + 1)], psT[:])
                    s.drain()
                    s.nop().then_inc(sm.h, 1)
            block.scalar(sx)
            xnat_done = sm.v

            for k in range(32):
                def gk(g, tb=tb, k=k, tkt=tk_t, pv=pe.v):
                    if tb == 0 and k < 2:
                        g.wait_ge(*tkt)
                    if k >= 2:
                        g.wait_ge(pe.h, pe.v)
                    else:
                        g.wait_ge(*tkt)
                    g.indirect_dma_start(
                        out=gbuf[k % 2][:], out_offset=None,
                        in_=ecat[:, :],
                        in_offset=IOA(ap=idxb[:, k:k + 1], axis=0),
                    ).then_inc(s_g.sems[k % 2].h, 16)
                    s_g.sems[k % 2].inc(16)
                block.gpsimd(gk)
                g_t = (s_g.sems[k % 2].h, s_g.sems[k % 2].v)

                def vd(v, tb=tb, k=k, gt=g_t, xd=xnat_done):
                    v.wait_ge(*gt)
                    v.wait_ge(sm.h, xd)
                    v.scalar_tensor_tensor(dscr[k % 2][:], gbuf[k % 2][:, 0:2048],
                                           1.0, xnat[:], OP.mult, OP.mult,
                                           accum_out=dcol[:, k:k + 1])
                    mark(v, vm)
                block.vector(vd)

                def sk(s, k=k, vv=vm.v):
                    s.wait_ge(vm.h, vv)
                    s.activation(sil[:, k:k + 1], dcol[:, k:k + 1], AF.Silu)
                    mark(s, sm)
                block.scalar(sk)

                def vh(v, k=k, sv=sm.v):
                    v.wait_ge(sm.h, sv)
                    v.tensor_scalar(hcol[:, k:k + 1], fsb[:, k:k + 1], 0.0, None,
                                    OP.max)
                    v.drain()
                    v.tensor_tensor(hcol[:, k:k + 1], hcol[:, k:k + 1],
                                    sil[:, k:k + 1], OP.mult)
                    v.drain()
                    v.tensor_scalar(diag[k % 2][:], ident[:], hcol[:, k:k + 1],
                                    None, OP.mult)
                    mark(v, vm)
                block.vector(vh)

                def tu(t, tb=tb, k=k, vv=vm.v, gt=g_t):
                    t.wait_ge(vm.h, vv)
                    t.wait_ge(*gt)
                    for tc in range(4):
                        t.matmul(psU[:, 512 * tc:512 * (tc + 1)], diag[k % 2][:],
                                 gbuf[k % 2][:, 2048 + 512 * tc:2048 + 512 * (tc + 1)],
                                 start=(k == 0), stop=(k == 31))
                    mark(t, pe)
                block.tensor(tu)

            def sf(s, tb=tb, pv=pe.v):
                s.wait_ge(pe.h, pv)
                s.copy(fnat[:], psU[:])
                mark(s, sm)
            block.scalar(sf)

            def tf2(t, tb=tb, sv=sm.v):
                t.wait_ge(sm.h, sv)
                for dt in range(NDT):
                    t.transpose(psT[:], fnat[:, 128 * dt:128 * (dt + 1)], ident[:])
                    t.drain()
                    t.nop().then_inc(pe.h, 1)
                    pe.inc()
                    t.wait_ge(sm.h, sm.v + 1)
                    sm.inc()
            block.tensor(tf2)

            def sf2(s, tb=tb, pv0=pe.v - NDT, sv0=sm.v - NDT):
                for dt in range(NDT):
                    s.wait_ge(pe.h, pv0 + dt + 1)
                    s.copy(fT[dt][:, 128 * tb:128 * (tb + 1)], psT[:])
                    s.drain()
                    s.nop().then_inc(sm.h, 1)
            block.scalar(sf2)

            def vnext(v, sv=sm.v):
                v.wait_ge(sm.h, sv)
                mark(v, vm)
            block.vector(vnext)

        ctx1.close()
        ffn_done = vm.v

        ctx2 = ExitStack()
        ytile = [pr.sb(f"yt{i}", [128, 1024], BF16, ctx2) for i in range(NDT)]
        qbuf = [pr.sb(f"qbuf{i}", [128, 1024], BF16, ctx2) for i in range(2)]
        wbuf = [pr.sb(f"wb{i}", [128, NDT * 128], BF16, ctx2) for i in range(2)]
        xf = [pr.sb(f"xf{i}", [128, 1024], F32, ctx2) for i in range(2)]
        t1 = pr.sb("t1", [128, 1024], F32, ctx2)
        t2 = pr.sb("t2", [128, 1024], F32, ctx2)
        psW = pr.ps("psW", [128, 1024], F32, ctx2)

        w_t = {}

        def wload(g, e, buf):
            w_t[e] = s_w.issue(g, buf, [(
                wbuf[buf][:].rearrange("p (dt c) -> p dt c", c=128),
                woutT[:, 128 * e:128 * (e + 1)].rearrange("(dt p) c -> p dt c", p=128),
                {})])

        for half in range(2):
            hs = slice(1024 * half, 1024 * (half + 1))
            for dt in range(NDT):
                def qy(q, dt=dt, half=half, fv=ffn_done, vv=vm.v):
                    q.wait_ge(vm.h, vv if (half > 0 or dt >= 2) else fv)
                    s_q.issue(q, dt, [(qbuf[dt % 2][:], qT_i[128 * dt:128 * (dt + 1), hs], {})])
                block.sync(qy)
                q_t = (s_q.sems[dt % 2].h, s_q.sems[dt % 2].v)

                def vy(v, dt=dt, qt=q_t, av=a_done):
                    v.wait_ge(*qt)
                    v.wait_ge(vm.h, av)
                    v.tensor_scalar(ytile[dt][:], qbuf[dt % 2][:], acol[:, dt:dt + 1],
                                    None, OP.mult)
                    mark(v, vm)
                block.vector(vy)
            y_done = vm.v

            def g0(g, half=half, yv=y_done):
                g.wait_ge(vm.h, yv)
                wload(g, 0, 0)
                wload(g, 1, 1)
            block.gpsimd(g0)

            for d2 in range(NDT):
                def tw(t, d2=d2, yv=y_done):
                    t.wait_ge(vm.h, yv)
                    t.wait_ge(*w_t[d2])
                    if d2 > 0:
                        t.wait_ge(vm.h, vm.v)
                    for d1 in range(NDT):
                        for tc in range(2):
                            t.matmul(psW[:, 512 * tc:512 * (tc + 1)],
                                     wbuf[d2 % 2][:, d1 * 128:d1 * 128 + 128],
                                     ytile[d1][:, 512 * tc:512 * (tc + 1)],
                                     start=(d1 == 0), stop=(d1 == NDT - 1))
                    mark(t, pe)
                block.tensor(tw)
                mmv = pe.v
                if d2 < NDT - 2:
                    def g2(g, d2=d2, mmv=mmv):
                        g.wait_ge(pe.h, mmv)
                        wload(g, d2 + 2, d2 % 2)
                    block.gpsimd(g2)

                def qx(q, d2=d2, half=half, vv=vm.v):
                    q.wait_ge(vm.h, vv)
                    s_xf.issue(q, d2, [(xf[d2 % 2][:],
                                        xT[128 * d2:128 * (d2 + 1), hs], {})])
                block.sync(qx)
                xf_t = (s_xf.sems[d2 % 2].h, s_xf.sems[d2 % 2].v)

                def vx(v, d2=d2, half=half, mmv=mmv, xt_=xf_t):
                    v.wait_ge(pe.h, mmv)
                    v.wait_ge(*xt_)
                    v.tensor_scalar(t1[:], psW[:], bout[:, d2:d2 + 1], None, OP.add)
                    v.drain()
                    v.tensor_tensor(t2[:], t1[:], xnT[d2][:, hs], OP.add)
                    v.drain()
                    v.tensor_tensor(t1[:], t2[:], fT[d2][:, hs], OP.add)
                    v.drain()
                    v.tensor_tensor(t2[:], t1[:], xf[d2 % 2][:], OP.add)
                    mark(v, vm)
                block.vector(vx)

                def qo(q, d2=d2, half=half, vv=vm.v):
                    q.wait_ge(vm.h, vv)
                    q.dma_start(out=x1T_o[128 * d2:128 * (d2 + 1), hs],
                                in_=t2[:]).then_inc(st.h, 16)
                    st.inc(16)
                    q.dma_start(out=x1s[128 * d2:128 * (d2 + 1), hs],
                                in_=t2[:]).then_inc(st.h, 16)
                    st.inc(16)
                block.sync(qo)
                stv = st.v

                def vnx(v, sv=stv):
                    v.wait_ge(st.h, sv)
                    mark(v, vm)
                block.vector(vnx)

        ctx2.close()

        emit_ln(pr, b, x1s, xnT, ms, eps=1e-5, tag="ln2")
        xn2_done = vm.v

        def qxn2(q):
            q.wait_ge(vm.h, xn2_done)
            for dt in range(NDT):
                q.dma_start(out=xn2T_o[128 * dt:128 * (dt + 1), :],
                            in_=xnT[dt][:]).then_inc(st.h, 16)
                st.inc(16)
        block.sync(qxn2)

        def qfin(q):
            q.wait_ge(st.h, st.v)
        block.sync(qfin)
        pr.finish = [(pe.h, pe.v), (vm.h, vm.v), (sm.h, sm.v), (st.h, st.v)]
        pr.ctx.pop_all()

    return pr


def l0b_inmaps(prep, l0a_res):
    maps = []
    for c in range(NCORE):
        part = c + 1 if c % 2 == 0 else c - 1
        maps.append({
            "xT": prep["xT"][c],
            "xnT_i": l0a_res[c]["xnT_o"],
            "qT_i": l0a_res[c]["qT_o"],
            "ap0": l0a_res[c]["apart_o"],
            "ap1": l0a_res[part]["apart_o"],
            "fi_i": l0a_res[c]["fi_o"],
            "fs_i": l0a_res[c]["fs_o"],
            "ecat": prep["e_cat"],
            "woutT": prep["woutT"],
            "bout_d": prep["bout"],
            "ident_d": prep["ident"],
            "ones1": prep["ones1"],
            "onescol": prep["onescol"],
        })
    return maps





def build_l1a(pr=None, io=None, gate=None):
    if pr is None:
        pr = Prog("l1a")
    nc = pr.nc
    ag = io is not None and "xn2f_ag" in io
    xn2f_d = io["xn2f_ag"] if ag else get_io(pr, io, "xn2f", [D, S], BF16)
    kvaT_d = get_io(pr, io, "kvaT", [D, 576], F32)
    lnw_d = get_io(pr, io, "lnw_d", [128, 4], F32)
    kvbn_d = get_io(pr, io, "kvbn", [512, 2048], F32)
    kvbv_d = get_io(pr, io, "kvbv", [512, 2048], F32)
    qpT_d = get_io(pr, io, "qpT", [D, 1536], F32)
    ones1 = get_io(pr, io, "ones1", [1, 128], BF16)
    onescol = get_io(pr, io, "onescol", [128, 1], BF16)

    kT_o = get_io(pr, io, "kT_o", [2048, S], BF16, out=True)
    krotT_o = get_io(pr, io, "krotT_o", [64, S], BF16, out=True)
    v_o = get_io(pr, io, "v_o", [S, 2048], BF16, out=True)
    q2T_o = get_io(pr, io, "q2T_o", [1536, S], BF16, out=True)

    with pr.ctx:
        o1 = pr.sb("o1", [1, 128], BF16)
        ocol = pr.sb("ocol", [128, 1], BF16)
        lnw = pr.sb("lnw", [128, 4], F32)
        cn = [pr.sb(f"cn{i}", [128, S], BF16) for i in range(4)]
        RNb = pr.sb("RNb", [128, S], BF16)
        ssq = pr.sb("ssq", [1, S], F32)
        rn16 = pr.sb("rn16", [1, S], BF16)

        pe = pr.sem("pe")
        vm = pr.sem("vm")
        sm = pr.sem("sm")
        st = pr.sem("st")
        s_const = Stream(pr, "cst", 1)
        s_xn = Stream(pr, "sxn", 1)
        s_w = Stream(pr, "sw", 2)

        if pr.block is None:
            pr.block = pr.ctx.enter_context(nc.Block())
        block = pr.block
        gate_all(block, gate)

        def qc(q):
            s_const.issue(q, 0, [
                (o1[:], ones1[:], {}),
                (ocol[:], onescol[:], {}),
                (lnw[:], lnw_d[:], {}),
            ])
        block.sync(qc)
        const_t = s_const.totals()[0]

        ctxA = ExitStack()
        xnf = [pr.sb(f"xnf{i}", [128, S], BF16, ctxA) for i in range(NDT)]
        wbuf = [pr.sb(f"wb{i}", [128, NDT * 128], BF16, ctxA) for i in range(2)]
        evb = pr.sb("evb", [128, 2048], BF16, ctxA)
        csq = evb
        psA = pr.ps("psA", [128, 2048], F32, ctxA)
        psR = pr.ps("psR", [128, 512], F32, ctxA)

        def qxn(q):
            for dt in range(NDT):
                if ag:
                    s_xn.issue(q, 0, [
                        (xnf[dt][:, 0:T], xn2f_d[128 * dt:128 * (dt + 1), :], {}),
                        (xnf[dt][:, T:S],
                         xn2f_d[D + 128 * dt:D + 128 * (dt + 1), :], {}),
                    ])
                else:
                    s_xn.issue(q, 0, [(xnf[dt][:],
                                       xn2f_d[128 * dt:128 * (dt + 1), :], {})])
        block.sync(qxn)
        xn_t = s_xn.totals()[0]

        w_t = {}

        def wload(g, dram, e, buf, ncol=128):
            w_t[(dram is qpT_d, e)] = s_w.issue(g, buf, [(
                wbuf[buf][:, 0:NDT * ncol].rearrange("p (dt c) -> p dt c", c=ncol),
                dram[:, ncol * e:ncol * (e + 1)].rearrange("(dt p) c -> p dt c", p=128),
                {})])

        def gq0(g):
            wload(g, qpT_d, 0, 0)
            wload(g, qpT_d, 1, 1)
        block.gpsimd(gq0)

        for mc in range(12):
            def tq(t, mc=mc, xt=xn_t, ct=const_t):
                t.wait_ge(*xt)
                t.wait_ge(*ct)
                t.wait_ge(*w_t[(True, mc)])
                if mc > 0:
                    t.wait_ge(vm.h, vm.v)
                for th in range(2):
                    for dt in range(NDT):
                        for tc in range(4):
                            t.matmul(psA[:, 512 * tc:512 * (tc + 1)],
                                     wbuf[mc % 2][:, dt * 128:dt * 128 + 128],
                                     xnf[dt][:, 2048 * th + 512 * tc:
                                              2048 * th + 512 * (tc + 1)],
                                     start=(dt == 0), stop=(dt == NDT - 1))
                    t.drain()
                    t.nop().then_inc(pe.h, 1)
                    pe.inc()
                    t.wait_ge(vm.h, vm.v + th + 1)
            block.tensor(tq)
            if mc < 10:
                def gq(g, mc=mc, pv=pe.v):
                    g.wait_ge(pe.h, pv)
                    wload(g, qpT_d, mc + 2, mc % 2)
                block.gpsimd(gq)

            for th in range(2):
                def vq(v, mc=mc, th=th, pv=pe.v - 1 + th):
                    v.wait_ge(pe.h, pv)
                    v.tensor_scalar(evb[:], psA[:], QSCALE, None, OP.mult)
                    mark(v, vm)
                block.vector(vq)

                def qq(q, mc=mc, th=th, vv=vm.v):
                    q.wait_ge(vm.h, vv)
                    q.dma_start(out=q2T_o[128 * mc:128 * (mc + 1),
                                          2048 * th:2048 * (th + 1)],
                                in_=evb[:]).then_inc(st.h, 16)
                    st.inc(16)
                block.sync(qq)
                stv = st.v

                def vq2(v, sv=stv):
                    v.wait_ge(st.h, sv)
                    mark(v, vm)
                block.vector(vq2)
        q_done = vm.v

        def gc0(g, pv=pe.v):
            g.wait_ge(pe.h, pv)
            wload(g, kvaT_d, 0, 0)
            wload(g, kvaT_d, 1, 1)
        block.gpsimd(gc0)

        def vz(v, qd=q_done):
            v.wait_ge(vm.h, qd)
            v.memset(ssq[:], 0.0)
            mark(v, vm)
        block.vector(vz)

        for cc in range(5):
            ncol = 128 if cc < 4 else 64
            if cc >= 2 and cc < 4:
                pass
            for th in range(2):
                def tc_(t, cc=cc, th=th, ncol=ncol, qd=q_done):
                    t.wait_ge(vm.h, vm.v)
                    t.wait_ge(*w_t[(False, cc)])
                    for dt in range(NDT):
                        for tc in range(4):
                            t.matmul(psA[0:ncol, 512 * tc:512 * (tc + 1)],
                                     wbuf[cc % 2][:, dt * 128:dt * 128 + ncol],
                                     xnf[dt][:, 2048 * th + 512 * tc:
                                              2048 * th + 512 * (tc + 1)],
                                     start=(dt == 0), stop=(dt == NDT - 1))
                    mark(t, pe)
                block.tensor(tc_)
                mmv = pe.v
                if th == 1 and cc < 3:
                    def gc(g, cc=cc, mmv=mmv):
                        g.wait_ge(pe.h, mmv)
                        if cc + 2 < 4:
                            wload(g, kvaT_d, cc + 2, cc % 2)
                        else:
                            w_t[(False, 4)] = s_w.issue(g, cc % 2, [(
                                wbuf[cc % 2][:].rearrange(
                                    "p (dt c) -> p dt c", c=128)[:, :, 0:64],
                                kvaT_d[:, 512:576].rearrange(
                                    "(dt p) c -> p dt c", p=128), {})])
                    block.gpsimd(gc)

                if cc < 4:
                    def sc(s, cc=cc, th=th, mmv=mmv):
                        s.wait_ge(pe.h, mmv)
                        s.copy(cn[cc][:, 2048 * th:2048 * (th + 1)], psA[:])
                        s.drain()
                        s.activation(csq[:], psA[:], AF.Square)
                        mark(s, sm)
                    block.scalar(sc)

                    def tsq(t, sv=sm.v):
                        t.wait_ge(sm.h, sv)
                        for tc in range(4):
                            t.matmul(psR[0:1, 0:512], ocol[:],
                                     csq[:, 512 * tc:512 * (tc + 1)],
                                     start=True, stop=True)
                            t.drain()
                            t.nop().then_inc(pe.h, 1)
                            pe.inc()
                            t.wait_ge(vm.h, vm.v + tc + 1)
                    block.tensor(tsq)

                    for tc in range(4):
                        def vsq(v, th=th, tc=tc, pv=pe.v - 3 + tc):
                            v.wait_ge(pe.h, pv)
                            v.tensor_tensor(ssq[0:1, 2048 * th + 512 * tc:
                                                2048 * th + 512 * (tc + 1)],
                                            ssq[0:1, 2048 * th + 512 * tc:
                                                2048 * th + 512 * (tc + 1)],
                                            psR[0:1, 0:512], OP.add)
                            mark(v, vm)
                        block.vector(vsq)

                    def vcn(v, sv=sm.v):
                        v.wait_ge(sm.h, sv)
                        mark(v, vm)
                    block.vector(vcn)
                else:
                    def vrot(v, th=th, mmv=mmv):
                        v.wait_ge(pe.h, mmv)
                        v.tensor_copy(evb[0:64, 0:2048], psA[0:64, :])
                        mark(v, vm)
                    block.vector(vrot)

                    def qrot(q, th=th, vv=vm.v):
                        q.wait_ge(vm.h, vv)
                        q.dma_start(out=krotT_o[:, 2048 * th:2048 * (th + 1)],
                                    in_=evb[0:64, 0:2048]).then_inc(st.h, 16)
                        st.inc(16)
                    block.sync(qrot)
                    stv = st.v

                    def vrot2(v, sv=stv):
                        v.wait_ge(st.h, sv)
                        mark(v, vm)
                    block.vector(vrot2)
        c_done = vm.v

        def vrr(v, cd=c_done):
            v.wait_ge(vm.h, cd)
            v.tensor_scalar(ssq[:], ssq[:], 1.0 / KVR_, 1e-6, OP.mult, OP.add)
            mark(v, vm)
        block.vector(vrr)

        def srr(s, vv=vm.v):
            s.wait_ge(vm.h, vv)
            s.activation(ssq[:], ssq[:], AF.Ln)
            s.drain()
            s.activation(ssq[:], ssq[:], AF.Exp, scale=-0.5)
            mark(s, sm)
        block.scalar(srr)

        def vrr2(v, sv=sm.v):
            v.wait_ge(sm.h, sv)
            v.tensor_copy(rn16[:], ssq[:])
            mark(v, vm)
        block.vector(vrr2)

        def trb(t, vv=vm.v):
            t.wait_ge(vm.h, vv)
            for tc in range(8):
                t.matmul(psA[:, 512 * (tc % 4):512 * (tc % 4 + 1)], o1[:],
                         rn16[:, 512 * tc:512 * (tc + 1)], start=True, stop=True)
                t.drain()
                t.nop().then_inc(pe.h, 1)
                pe.inc()
                t.wait_ge(sm.h, sm.v + tc + 1)
        block.tensor(trb)

        for tc in range(8):
            def srb(s, tc=tc, pv=pe.v - 7 + tc):
                s.wait_ge(pe.h, pv)
                s.copy(RNb[:, 512 * tc:512 * (tc + 1)],
                       psA[:, 512 * (tc % 4):512 * (tc % 4 + 1)])
                s.drain()
                s.nop().then_inc(sm.h, 1)
                sm.inc()
            block.scalar(srb)
        rb_done = sm.v

        for cc in range(4):
            def vcs(v, cc=cc, rb=rb_done):
                v.wait_ge(sm.h, rb)
                v.tensor_tensor(cn[cc][:], cn[cc][:], RNb[:], OP.mult)
                v.drain()
                v.tensor_scalar(cn[cc][:], cn[cc][:], lnw[:, cc:cc + 1], None,
                                OP.mult)
                mark(v, vm)
            block.vector(vcs)
        cn_done = vm.v

        ctxA.close()

        ctxB = ExitStack()
        kwb = pr.sb("kwb", [128, 4 * 128], BF16, ctxB)
        vwb = pr.sb("vwb", [128, 4 * 2048], BF16, ctxB)
        ev2 = pr.sb("ev2", [128, 2048], BF16, ctxB)
        psK = pr.ps("psK", [128, 2048], F32, ctxB)
        s_kw = Stream(pr, "skw", 2)

        kw_t = {}

        def kwload(g, e, buf):
            kw_t[e] = s_kw.issue(g, buf, [(
                kwb[:, 0:4 * 128].rearrange("p (rt c) -> p rt c", c=128),
                kvbn_d[:, 128 * e:128 * (e + 1)].rearrange("(rt p) c -> p rt c", p=128),
                {})])

        def gk0(g, cd=cn_done):
            g.wait_ge(vm.h, cd)
            kwload(g, 0, 0)
            g.dma_start(out=vwb[:].rearrange("p (rt c) -> p rt c", c=2048),
                        in_=kvbv_d.rearrange("(rt p) c -> p rt c", p=128),
                        ).then_inc(s_kw.sems[1].h, 16)
            s_kw.sems[1].inc(16)
        block.gpsimd(gk0)
        vw_t = (s_kw.sems[1].h, s_kw.sems[1].v)

        for mc in range(16):
            for th in range(2):
                def tk(t, mc=mc, th=th, cd=cn_done):
                    t.wait_ge(vm.h, vm.v if (mc or th) else cd)
                    t.wait_ge(*kw_t[mc])
                    for rt in range(4):
                        for tc in range(4):
                            t.matmul(psK[:, 512 * tc:512 * (tc + 1)],
                                     kwb[:, rt * 128:rt * 128 + 128],
                                     cn[rt][:, 2048 * th + 512 * tc:
                                             2048 * th + 512 * (tc + 1)],
                                     start=(rt == 0), stop=(rt == 3))
                    mark(t, pe)
                block.tensor(tk)
                mmv = pe.v

                def vk(v, mmv=mmv):
                    v.wait_ge(pe.h, mmv)
                    v.tensor_copy(ev2[:], psK[:])
                    mark(v, vm)
                block.vector(vk)

                def qk(q, mc=mc, th=th, vv=vm.v):
                    q.wait_ge(vm.h, vv)
                    q.dma_start(out=kT_o[128 * mc:128 * (mc + 1),
                                         2048 * th:2048 * (th + 1)],
                                in_=ev2[:]).then_inc(st.h, 16)
                    st.inc(16)
                block.sync(qk)
                stv = st.v

                def vk2(v, sv=stv):
                    v.wait_ge(st.h, sv)
                    mark(v, vm)
                block.vector(vk2)
            if mc < 15:
                def gkn(g, mc=mc, pv=pe.v):
                    g.wait_ge(pe.h, pv)
                    kwload(g, mc + 1, 0)
                block.gpsimd(gkn)

        for kc in range(32):
            def tv(t, kc=kc, vt=vw_t):
                t.wait_ge(*vt)
                t.wait_ge(vm.h, vm.v)
                for rt in range(4):
                    for tc in range(4):
                        t.matmul(psK[:, 512 * tc:512 * (tc + 1)],
                                 cn[rt][:, 128 * kc:128 * (kc + 1)],
                                 vwb[:, rt * 2048 + 512 * tc:
                                     rt * 2048 + 512 * (tc + 1)],
                                 start=(rt == 0), stop=(rt == 3))
                mark(t, pe)
            block.tensor(tv)
            mmv = pe.v

            def vv2(v, mmv=mmv):
                v.wait_ge(pe.h, mmv)
                v.tensor_copy(ev2[:], psK[:])
                mark(v, vm)
            block.vector(vv2)

            def qv(q, kc=kc, vv=vm.v):
                q.wait_ge(vm.h, vv)
                q.dma_start(out=v_o[128 * kc:128 * (kc + 1), :],
                            in_=ev2[:]).then_inc(st.h, 16)
                st.inc(16)
            block.sync(qv)
            stv = st.v

            def vv3(v, sv=stv):
                v.wait_ge(st.h, sv)
                mark(v, vm)
            block.vector(vv3)

        ctxB.close()

        def qfin(q):
            q.wait_ge(st.h, st.v)
        block.sync(qfin)
        pr.finish = [(pe.h, pe.v), (vm.h, vm.v), (sm.h, sm.v), (st.h, st.v)]
        pr.ctx.pop_all()

    return pr


KVR_ = 512.0



def build_l1b(pr=None, io=None, gate=None):
    if pr is None:
        pr = Prog("l1b")
    nc = pr.nc
    kT_d = get_io(pr, io, "kT", [1024, S], BF16)
    krotT_d = get_io(pr, io, "krotT", [64, S], BF16)
    v_d = get_io(pr, io, "v", [S, 1024], BF16)
    qT_d = get_io(pr, io, "qT", [1536, S], BF16)
    opT_d = get_io(pr, io, "opT", [1024, D], F32)
    masks_d = get_io(pr, io, "masks", [4, 128, 512], F32)
    ones1 = get_io(pr, io, "ones1", [1, 128], BF16)
    onescol = get_io(pr, io, "onescol", [128, 1], BF16)

    op_o = get_io(pr, io, "op_o", [S, D], BF16, out=True)

    with pr.ctx:
        o1 = pr.sb("o1", [1, 128], BF16)
        ocol = pr.sb("ocol", [128, 1], BF16)
        maskb = pr.sb("maskb", [128, 4 * 512], F32)
        ob = pr.sb("ob", [128, 8 * 2048], BF16)
        krot = pr.sb("krot", [64, S], BF16)
        pvt = [pr.sb(f"pvt{i}", [128, 2048], BF16) for i in range(8)]

        pe = pr.sem("pe")
        vm = pr.sem("vm")
        sm = pr.sem("sm")
        st = pr.sem("st")
        s_const = Stream(pr, "cst", 1)
        s_kh = Stream(pr, "skh", 2)
        s_vh = Stream(pr, "svh", 2)
        s_qh = Stream(pr, "sqh", 2)

        if pr.block is None:
            pr.block = pr.ctx.enter_context(nc.Block())
        block = pr.block
        gate_all(block, gate)

        def qc(q):
            s_const.issue(q, 0, [
                (o1[:], ones1[:], {}),
                (ocol[:], onescol[:], {}),
                (maskb[:].rearrange("p (a c) -> p a c", c=512),
                 masks_d.rearrange("a b c -> b a c"), {}),
                (krot[:], krotT_d[:], {}),
            ])
        block.sync(qc)
        const_t = s_const.totals()[0]

        s_ob = Stream(pr, "sob", 1)

        def gob(g):
            s_ob.issue(g, 0, [(ob[:].rearrange("p (a c) -> p a c", c=2048),
                               opT_d.rearrange("(a p) c -> p a c", p=128), {})])
        block.gpsimd(gob)
        ob_t = s_ob.totals()[0]

        ctx1 = ExitStack()
        kh = [pr.sb(f"kh{i}", [128, S], BF16, ctx1) for i in range(2)]
        vh = [pr.sb(f"vh{i}", [128, 32 * 128], BF16, ctx1) for i in range(2)]
        qhn = [pr.sb(f"qhn{i}", [128, 2048], BF16, ctx1) for i in range(2)]
        qhr = [pr.sb(f"qhr{i}", [64, 2048], BF16, ctx1) for i in range(2)]
        pbuf = pr.sb("pbuf", [128, 512], BF16, ctx1)
        zrow = pr.sb("zrow", [1, 512], F32, ctx1)
        rz16 = pr.sb("rz16", [1, 512], BF16, ctx1)
        rzb = pr.sb("rzb", [128, 512], BF16, ctx1)
        onat = pr.sb("onat", [128, 2048], BF16, ctx1)
        psS = pr.ps("psS", [128, 512], F32, ctx1)
        psZ = pr.ps("psZ", [128, 512], F32, ctx1)
        psP = pr.ps("psP", [128, 512], F32, ctx1)
        psB2 = pr.ps("psB2", [128, 512], F32, ctx1)
        psO = pr.ps("psO", [128, 2048], F32, ctx1)

        for qhalf in range(2):
            for hl in range(8):
                def qh_(q, hl=hl, qhalf=qhalf, vv=vm.v):
                    q.wait_ge(vm.h, vv)
                    s_kh.issue(q, hl, [(kh[hl % 2][:],
                                        kT_d[128 * hl:128 * (hl + 1), :], {})])
                    s_vh.issue(q, hl, [(
                        vh[hl % 2][:].rearrange("p (a c) -> p a c", c=128),
                        v_d[:, 128 * hl:128 * (hl + 1)].rearrange(
                            "(a p) c -> p a c", p=128), {})])
                    s_qh.issue(q, hl, [
                        (qhn[hl % 2][:],
                         qT_d[192 * hl:192 * hl + 128,
                              2048 * qhalf:2048 * (qhalf + 1)], {}),
                        (qhr[hl % 2][0:64, :],
                         qT_d[192 * hl + 128:192 * hl + 192,
                              2048 * qhalf:2048 * (qhalf + 1)], {}),
                    ])
                block.sync(qh_)
                kh_t = (s_kh.sems[hl % 2].h, s_kh.sems[hl % 2].v)
                vh_t = (s_vh.sems[hl % 2].h, s_vh.sems[hl % 2].v)
                qh_t = (s_qh.sems[hl % 2].h, s_qh.sems[hl % 2].v)

                for g_ in range(4):
                    qg = qhalf * 4 + g_
                    nkt = 4 * qg + 4
                    for kt in range(nkt):
                        def ts_(t, hl=hl, g_=g_, qg=qg, kt=kt, kh_t=kh_t,
                                qh_t=qh_t, ct=const_t, vv=vm.v):
                            t.wait_ge(*kh_t)
                            t.wait_ge(*qh_t)
                            t.wait_ge(*ct)
                            if kt == 0:
                                t.wait_ge(vm.h, vv)
                            else:
                                t.wait_ge(sm.h, sm.v)
                            t.matmul(psS[:], kh[hl % 2][:, 128 * kt:128 * (kt + 1)],
                                     qhn[hl % 2][:, 512 * g_:512 * (g_ + 1)],
                                     start=True, stop=False)
                            t.matmul(psS[:], krot[:, 128 * kt:128 * (kt + 1)],
                                     qhr[hl % 2][0:64, 512 * g_:512 * (g_ + 1)],
                                     start=False, stop=True)
                            mark(t, pe)
                        block.tensor(ts_)
                        mmv = pe.v

                        diag_j = kt - 4 * qg
                        if diag_j >= 0:
                            def vmsk(v, j=diag_j, mmv=mmv):
                                v.wait_ge(pe.h, mmv)
                                v.tensor_tensor(psS[:], psS[:],
                                                maskb[:, 512 * j:512 * (j + 1)],
                                                OP.add)
                                mark(v, vm)
                            block.vector(vmsk)

                        def se_(s, kt=kt, mmv=mmv, masked=(diag_j >= 0), vv=vm.v):
                            if masked:
                                s.wait_ge(vm.h, vv)
                            else:
                                s.wait_ge(pe.h, mmv)
                            s.activation(pbuf[:], psS[:], AF.Exp)
                            mark(s, sm)
                        block.scalar(se_)

                        def tz(t, kt=kt, nkt=nkt, hl=hl, vh_t=vh_t, sv=sm.v):
                            t.wait_ge(sm.h, sv)
                            t.wait_ge(*vh_t)
                            t.matmul(psZ[0:1, :], ocol[:], pbuf[:],
                                     start=(kt == 0), stop=(kt == nkt - 1))
                            t.matmul(psP[:], vh[hl % 2][:, 128 * kt:128 * (kt + 1)],
                                     pbuf[:], start=(kt == 0), stop=(kt == nkt - 1))
                            mark(t, pe)
                        block.tensor(tz)

                    def vz_(v, pv=pe.v):
                        v.wait_ge(pe.h, pv)
                        v.tensor_copy(zrow[:], psZ[0:1, :])
                        v.drain()
                        v.reciprocal(zrow[:], zrow[:])
                        v.drain()
                        v.tensor_copy(rz16[:], zrow[:])
                        mark(v, vm)
                    block.vector(vz_)

                    def trz(t, vv=vm.v):
                        t.wait_ge(vm.h, vv)
                        t.matmul(psB2[:], o1[:], rz16[:], start=True, stop=True)
                        mark(t, pe)
                    block.tensor(trz)

                    def srz(s, pv=pe.v):
                        s.wait_ge(pe.h, pv)
                        s.copy(rzb[:], psB2[:])
                        mark(s, sm)
                    block.scalar(srz)

                    def vpv(v, hl=hl, g_=g_, sv=sm.v):
                        v.wait_ge(sm.h, sv)
                        v.tensor_tensor(pvt[hl][:, 512 * g_:512 * (g_ + 1)],
                                        psP[:], rzb[:], OP.mult)
                        mark(v, vm)
                    block.vector(vpv)

            pv_done = vm.v
            for qc_ in range(16):
                def to(t, qc_=qc_, qhalf=qhalf, pv=pv_done, obt=ob_t, vv=vm.v):
                    t.wait_ge(vm.h, pv)
                    t.wait_ge(*obt)
                    if qc_ > 0:
                        t.wait_ge(vm.h, vv)
                    for hl in range(8):
                        for tc in range(4):
                            t.matmul(psO[:, 512 * tc:512 * (tc + 1)],
                                     pvt[hl][:, 128 * qc_:128 * (qc_ + 1)],
                                     ob[:, 2048 * hl + 512 * tc:
                                        2048 * hl + 512 * (tc + 1)],
                                     start=(hl == 0), stop=(hl == 7))
                    mark(t, pe)
                block.tensor(to)
                mmv = pe.v

                def vo(v, mmv=mmv):
                    v.wait_ge(pe.h, mmv)
                    v.tensor_copy(onat[:], psO[:])
                    mark(v, vm)
                block.vector(vo)

                def qo(q, qc_=qc_, qhalf=qhalf, vv=vm.v):
                    q.wait_ge(vm.h, vv)
                    q.dma_start(out=op_o[2048 * qhalf + 128 * qc_:
                                         2048 * qhalf + 128 * (qc_ + 1), :],
                                in_=onat[:]).then_inc(st.h, 16)
                    st.inc(16)
                block.sync(qo)
                stv = st.v

                def vo2(v, sv=stv):
                    v.wait_ge(st.h, sv)
                    mark(v, vm)
                block.vector(vo2)

        ctx1.close()

        def qfin(q):
            q.wait_ge(st.h, st.v)
        block.sync(qfin)
        pr.finish = [(pe.h, pe.v), (vm.h, vm.v), (sm.h, sm.v), (st.h, st.v)]
        pr.ctx.pop_all()

    return pr



def build_l1c(pr=None, io=None, gate=None):
    if pr is None:
        pr = Prog("l1c")
    nc = pr.nc
    summed = io is not None and "opb" not in io
    xn2T_d = get_io(pr, io, "xn2T", [D, T], BF16)
    x1T_d = get_io(pr, io, "x1T", [D, T], F32)
    opa_d = get_io(pr, io, "opa", [T, D], BF16)
    opb_d = None if summed else get_io(pr, io, "opb", [T, D], BF16)
    wgu_d = get_io(pr, io, "wgu", [8 * D, D], F32)
    wpr_d = get_io(pr, io, "wpr", [D, 4 * D], F32)
    ident_d = get_io(pr, io, "ident_d", [128, 128], BF16)
    ones1 = get_io(pr, io, "ones1", [1, 128], BF16)
    onescol = get_io(pr, io, "onescol", [128, 1], BF16)

    yT_o = get_io(pr, io, "yT_o", [D, T], F32, out=True)
    h_hbm = pr.dtmp("h_hbm", [4 * D, T], BF16)
    wqpT_hbm = pr.dtmp("wqpT_hbm", [4 * D, D], BF16)
    aT_hbm = pr.dtmp("aT_hbm", [D, T], BF16)

    NH = 4 * D
    with pr.ctx:
        o1 = pr.sb("o1", [1, 128], BF16)
        ocol = pr.sb("ocol", [128, 1], BF16)
        ident = pr.sb("ident", [128, 128], BF16)
        xn2T = [pr.sb(f"xn2T{i}", [128, T], BF16) for i in range(NDT)]
        ssq = pr.sb("ssq", [1, T], F32)
        rrow = pr.sb("rrow", [1, T], F32)
        rn16 = pr.sb("rn16", [1, T], BF16)
        RNb = pr.sb("RNb", [128, T], BF16)

        pe = pr.sem("pe")
        vm = pr.sem("vm")
        sm = pr.sem("sm")
        st = pr.sem("st")
        s_const = Stream(pr, "cst", 1)
        s_xn = Stream(pr, "sxn", 1)
        s_op = Stream(pr, "sop", 2)
        s_wg = Stream(pr, "swg", 2)
        s_hh = Stream(pr, "shh", 2)
        s_wp = Stream(pr, "swp", 2)
        s_x1 = Stream(pr, "sx1", 2)
        s_at = Stream(pr, "sat", 2)

        if pr.block is None:
            pr.block = pr.ctx.enter_context(nc.Block())
        block = pr.block
        gate_all(block, gate)

        def qc(q):
            s_const.issue(q, 0, [
                (o1[:], ones1[:], {}),
                (ocol[:], onescol[:], {}),
                (ident[:], ident_d[:], {}),
            ])
            for dt in range(NDT):
                s_xn.issue(q, 0, [(xn2T[dt][:],
                                   xn2T_d[128 * dt:128 * (dt + 1), :], {})])
        block.sync(qc)
        const_t = s_const.totals()[0]
        xn_t = s_xn.totals()[0]

        ctx0 = ExitStack()
        oa = [pr.sb(f"oa{i}", [128, D], BF16, ctx0) for i in range(2)]
        obb = [pr.sb(f"obb{i}", [128, D], BF16, ctx0) for i in range(2)]
        anat = pr.sb("anat", [128, D], BF16, ctx0)
        arow = [pr.sb(f"arow{i}", [128, 128], BF16, ctx0) for i in range(2)]
        psT = pr.ps("psT", [128, 128], BF16, ctx0)

        for tb in range(NDT):
            def qop(q, tb=tb, vv=vm.v):
                q.wait_ge(vm.h, vv)
                dmas = [(oa[tb % 2][:], opa_d[128 * tb:128 * (tb + 1), :], {})]
                if not summed:
                    dmas.append((obb[tb % 2][:],
                                 opb_d[128 * tb:128 * (tb + 1), :], {}))
                s_op.issue(q, tb, dmas)
            block.sync(qop)
            op_t = (s_op.sems[tb % 2].h, s_op.sems[tb % 2].v)

            def vad(v, tb=tb, ot=op_t):
                v.wait_ge(*ot)
                if summed:
                    v.tensor_copy(anat[:], oa[tb % 2][:])
                else:
                    v.tensor_tensor(anat[:], oa[tb % 2][:], obb[tb % 2][:], OP.add)
                mark(v, vm)
            block.vector(vad)
            ad_v = vm.v

            pe0, sm0 = pe.v, sm.v

            def tt_(t, tb=tb, vv=ad_v, ct=const_t, sm0=sm0):
                t.wait_ge(vm.h, vv)
                t.wait_ge(*ct)
                for dt in range(NDT):
                    if dt > 0:
                        t.wait_ge(sm.h, sm0 + dt)
                    t.transpose(psT[:], anat[:, 128 * dt:128 * (dt + 1)], ident[:])
                    t.drain()
                    t.nop().then_inc(pe.h, 1)
            block.tensor(tt_)
            pe.inc(NDT)

            def sev(s, tb=tb, pe0=pe0):
                for dt in range(NDT):
                    s.wait_ge(pe.h, pe0 + dt + 1)
                    if dt >= 2:
                        s.wait_ge(s_at.sems[dt % 2].h, 16 * (8 * tb + dt // 2))
                    elif tb > 0:
                        s.wait_ge(s_at.sems[dt % 2].h, 16 * 8 * tb)
                    s.copy(arow[dt % 2][:], psT[:])
                    s.drain()
                    s.nop().then_inc(sm.h, 1)
            block.scalar(sev)
            sm.inc(NDT)

            def qat(q, tb=tb, sm0=sm0):
                for dt in range(NDT):
                    q.wait_ge(sm.h, sm0 + dt + 1)
                    s_at.issue(q, dt, [(aT_hbm[128 * dt:128 * (dt + 1),
                                               128 * tb:128 * (tb + 1)],
                                        arow[dt % 2][:], {})])
            block.sync(qat)

            def vnx(v, at=s_at.totals()):
                wait_all(v, at)
                mark(v, vm)
            block.vector(vnx)
        ctx0.close()
        at_done = vm.v

        ctx1 = ExitStack()
        wt = [pr.sb(f"wt{i}", [128, NH], BF16, ctx1) for i in range(2)]
        sums = pr.sb("sums", [128, 128], F32, ctx1)
        scal = pr.sb("scal", [128, 128], F32, ctx1)
        thr = pr.sb("thr", [128, 128], F32, ctx1)
        nthr = pr.sb("nthr", [128, 128], F32, ctx1)
        sc16 = pr.sb("sc16", [128, 128], BF16, ctx1)
        mm1 = pr.sb("mm1", [128, NH], BF16, ctx1)
        mm2 = pr.sb("mm2", [128, NH], BF16, ctx1)
        wqt = pr.sb("wqt", [128, NH], BF16, ctx1)
        tev = [pr.sb(f"tev{i}", [128, 128], BF16, ctx1) for i in range(2)]
        psT1 = pr.ps("psT1", [128, 128], BF16, ctx1)
        s_wq2 = Stream(pr, "swq2", 2)

        def quant_tile(v, wtile, ng):
            v.tensor_reduce(sums[:, 0:ng],
                            wtile[:].rearrange("p (g k) -> p g k", k=64),
                            axis=AXX, op=OP.add, apply_absolute_value=True)
            v.drain()
            v.tensor_scalar(scal[:, 0:ng], sums[:, 0:ng], 1.0 / 64, None, OP.mult)
            v.drain()
            v.tensor_copy(sc16[:, 0:ng], scal[:, 0:ng])
            v.drain()
            v.tensor_copy(scal[:, 0:ng], sc16[:, 0:ng])
            v.drain()
            v.tensor_scalar(scal[:, 0:ng], scal[:, 0:ng], 1e-8, None, OP.max)
            v.drain()
            v.tensor_scalar(thr[:, 0:ng], scal[:, 0:ng], 0.5, None, OP.mult)
            v.tensor_scalar(nthr[:, 0:ng], scal[:, 0:ng], -0.5, None, OP.mult)
            v.drain()
            v.tensor_tensor(mm1[:, 0:ng * 64].rearrange("p (g k) -> p g k", k=64),
                            wtile[:].rearrange("p (g k) -> p g k", k=64),
                            thr[:, 0:ng, None].broadcast_to((128, ng, 64)),
                            OP.is_gt)
            v.tensor_tensor(mm2[:, 0:ng * 64].rearrange("p (g k) -> p g k", k=64),
                            wtile[:].rearrange("p (g k) -> p g k", k=64),
                            nthr[:, 0:ng, None].broadcast_to((128, ng, 64)),
                            OP.is_lt)
            v.drain()
            v.tensor_tensor(mm1[:, 0:ng * 64], mm1[:, 0:ng * 64],
                            mm2[:, 0:ng * 64], OP.subtract)
            v.drain()
            v.tensor_tensor(wqt[:, 0:ng * 64].rearrange("p (g k) -> p g k", k=64),
                            mm1[:, 0:ng * 64].rearrange("p (g k) -> p g k", k=64),
                            scal[:, 0:ng, None].broadcast_to((128, ng, 64)),
                            OP.mult)

        for wb_ in range(16):
            def gw(g, wb_=wb_, vv=vm.v):
                g.wait_ge(vm.h, vv)
                s_wq2.issue(g, wb_, [(
                    wt[wb_ % 2][:], wpr_d[128 * wb_:128 * (wb_ + 1), :], {})])
            block.gpsimd(gw)
            w_tt = (s_wq2.sems[wb_ % 2].h, s_wq2.sems[wb_ % 2].v)

            def vq_(v, wb_=wb_, wtt=w_tt):
                v.wait_ge(*wtt)
                quant_tile(v, wt[wb_ % 2], 128)
                mark(v, vm)
            block.vector(vq_)
            q_v = vm.v

            pe0, sm0 = pe.v, sm.v

            def tt2(t, wb_=wb_, vv=q_v, ct=const_t, sm0=sm0):
                t.wait_ge(vm.h, vv)
                t.wait_ge(*ct)
                for hb in range(64):
                    if hb > 0:
                        t.wait_ge(sm.h, sm0 + hb)
                    t.transpose(psT1[:], wqt[:, 128 * hb:128 * (hb + 1)], ident[:])
                    t.drain()
                    t.nop().then_inc(pe.h, 1)
            block.tensor(tt2)
            pe.inc(64)

            def se2(s, wb_=wb_, pe0=pe0):
                for hb in range(64):
                    s.wait_ge(pe.h, pe0 + hb + 1)
                    if hb >= 2:
                        s.wait_ge(s_wp.sems[hb % 2].h, 16 * (32 * wb_ + hb // 2))
                    elif wb_ > 0:
                        s.wait_ge(s_wp.sems[hb % 2].h, 16 * 32 * wb_)
                    s.copy(tev[hb % 2][:], psT1[:])
                    s.drain()
                    s.nop().then_inc(sm.h, 1)
            block.scalar(se2)
            sm.inc(64)

            def qw2(q, wb_=wb_, sm0=sm0):
                for hb in range(64):
                    q.wait_ge(sm.h, sm0 + hb + 1)
                    s_wp.issue(q, hb, [(wqpT_hbm[128 * hb:128 * (hb + 1),
                                                 128 * wb_:128 * (wb_ + 1)],
                                        tev[hb % 2][:], {})])
            block.sync(qw2)

            def vnx2(v, at=s_wp.totals()):
                wait_all(v, at)
                mark(v, vm)
            block.vector(vnx2)
        wp_done = vm.v

        psG = pr.ps("psG", [128, 1024], F32, ctx1)
        psU2 = pr.ps("psU2", [128, 1024], F32, ctx1)
        psR = pr.ps("psR", [128, 512], F32, ctx1)
        wqg = pr.sb("wqg", [128, D], BF16, ctx1)
        wqu = pr.sb("wqu", [128, D], BF16, ctx1)
        wqgT = pr.sb("wqgT", [128, NDT * 128], BF16, ctx1)
        wquT = pr.sb("wquT", [128, NDT * 128], BF16, ctx1)
        sg = pr.sb("sg", [128, 1024], F32, ctx1)
        hj = pr.sb("hj", [128, T], BF16, ctx1)
        hsq = pr.sb("hsq", [128, T], BF16, ctx1)

        def vz0(v, wv=wp_done):
            v.wait_ge(vm.h, wv)
            v.memset(ssq[:], 0.0)
            mark(v, vm)
        block.vector(vz0)

        for jp in range(64):
            def gg(g, jp=jp, vv=vm.v):
                g.wait_ge(vm.h, vv)
                s_wg.issue(g, 0, [(wt[0][:, 0:D],
                                   wgu_d[128 * jp:128 * (jp + 1), :], {})])
                s_wg.issue(g, 1, [(wt[1][:, 0:D],
                                   wgu_d[128 * (64 + jp):128 * (64 + jp + 1), :], {})])
            block.gpsimd(gg)
            g_t = s_wg.totals()

            def vqg(v, jp=jp, gt=g_t):
                v.wait_ge(*gt[0])
                quant_tile(v, wt[0][:, 0:D].rearrange("p c -> p c") if False else wt[0][:, 0:D], 32)
                v.drain()
                v.tensor_copy(wqg[:], wqt[:, 0:D])
                v.drain()
                v.wait_ge(*gt[1])
                quant_tile(v, wt[1][:, 0:D], 32)
                v.drain()
                v.tensor_copy(wqu[:], wqt[:, 0:D])
                mark(v, vm)
            block.vector(vqg)
            q_v = vm.v

            pe0, sm0 = pe.v, sm.v

            def tt3(t, jp=jp, vv=q_v, sm0=sm0):
                t.wait_ge(vm.h, vv)
                for z, src in enumerate([wqg, wqu]):
                    for dt in range(NDT):
                        idx = z * NDT + dt
                        if idx > 0:
                            t.wait_ge(sm.h, sm0 + idx)
                        t.transpose(psT1[:], src[:, 128 * dt:128 * (dt + 1)],
                                    ident[:])
                        t.drain()
                        t.nop().then_inc(pe.h, 1)
            block.tensor(tt3)
            pe.inc(2 * NDT)

            def se3(s, pe0=pe0):
                for z, dst in enumerate([wqgT, wquT]):
                    for dt in range(NDT):
                        idx = z * NDT + dt
                        s.wait_ge(pe.h, pe0 + idx + 1)
                        s.copy(dst[:, 128 * dt:128 * (dt + 1)], psT1[:])
                        s.drain()
                        s.nop().then_inc(sm.h, 1)
            block.scalar(se3)
            sm.inc(2 * NDT)
            tr_done = sm.v

            for th in range(2):
                def tg(t, jp=jp, th=th, tv=tr_done, xt=xn_t, vv=vm.v):
                    t.wait_ge(sm.h, tv)
                    t.wait_ge(*xt)
                    if th > 0 or jp > 0:
                        t.wait_ge(vm.h, vv)
                    for dt in range(NDT):
                        for tc in range(2):
                            sl = slice(1024 * th + 512 * tc, 1024 * th + 512 * (tc + 1))
                            t.matmul(psG[:, 512 * tc:512 * (tc + 1)],
                                     wqgT[:, 128 * dt:128 * (dt + 1)],
                                     xn2T[dt][:, sl],
                                     start=(dt == 0), stop=(dt == NDT - 1))
                            t.matmul(psU2[:, 512 * tc:512 * (tc + 1)],
                                     wquT[:, 128 * dt:128 * (dt + 1)],
                                     xn2T[dt][:, sl],
                                     start=(dt == 0), stop=(dt == NDT - 1))
                    mark(t, pe)
                block.tensor(tg)
                mmv = pe.v

                def sh(s, th=th, mmv=mmv):
                    s.wait_ge(pe.h, mmv)
                    s.activation(sg[:], psG[:], AF.Silu)
                    mark(s, sm)
                block.scalar(sh)

                def vh_(v, th=th, sv=sm.v):
                    v.wait_ge(sm.h, sv)
                    v.tensor_tensor(hj[:, 1024 * th:1024 * (th + 1)], sg[:],
                                    psU2[:], OP.mult)
                    mark(v, vm)
                block.vector(vh_)

                def sh2(s, th=th, vv=vm.v):
                    s.wait_ge(vm.h, vv)
                    s.activation(hsq[:, 1024 * th:1024 * (th + 1)],
                                 hj[:, 1024 * th:1024 * (th + 1)], AF.Square)
                    mark(s, sm)
                block.scalar(sh2)

                def tsq2(t, th=th, sv=sm.v):
                    t.wait_ge(sm.h, sv)
                    for tc in range(2):
                        t.matmul(psR[0:1, 0:512], ocol[:],
                                 hsq[:, 1024 * th + 512 * tc:
                                     1024 * th + 512 * (tc + 1)],
                                 start=True, stop=True)
                        t.drain()
                        t.nop().then_inc(pe.h, 1)
                        pe.inc()
                        t.wait_ge(vm.h, vm.v + tc + 1)
                block.tensor(tsq2)

                for tc in range(2):
                    def vsq2(v, th=th, tc=tc, pv=pe.v - 1 + tc):
                        v.wait_ge(pe.h, pv)
                        sl = slice(1024 * th + 512 * tc, 1024 * th + 512 * (tc + 1))
                        v.tensor_tensor(ssq[0:1, sl], ssq[0:1, sl],
                                        psR[0:1, 0:512], OP.add)
                        mark(v, vm)
                    block.vector(vsq2)

            def qh2(q, jp=jp, vv=vm.v):
                q.wait_ge(vm.h, vv)
                s_hh.issue(q, jp, [(h_hbm[128 * jp:128 * (jp + 1), :], hj[:], {})])
            block.sync(qh2)
            hst = (s_hh.sems[jp % 2].h, s_hh.sems[jp % 2].v)

            def vhf(v, ht=hst):
                v.wait_ge(*ht)
                mark(v, vm)
            block.vector(vhf)
        gu_done = vm.v

        def vrr(v, gd=gu_done):
            v.wait_ge(vm.h, gd)
            v.tensor_scalar(rrow[:], ssq[:], 1.0 / NH, 1.1920929e-7,
                            OP.mult, OP.add)
            mark(v, vm)
        block.vector(vrr)

        def srr(s, vv=vm.v):
            s.wait_ge(vm.h, vv)
            s.activation(rrow[:], rrow[:], AF.Ln)
            s.drain()
            s.activation(rrow[:], rrow[:], AF.Exp, scale=-0.5)
            mark(s, sm)
        block.scalar(srr)

        def vrr2(v, sv=sm.v):
            v.wait_ge(sm.h, sv)
            v.tensor_copy(rn16[:], rrow[:])
            mark(v, vm)
        block.vector(vrr2)

        def trb2(t, vv=vm.v):
            t.wait_ge(vm.h, vv)
            for tc in range(4):
                t.matmul(psG[:, 0:512], o1[:], rn16[:, 512 * tc:512 * (tc + 1)],
                         start=True, stop=True)
                t.drain()
                t.nop().then_inc(pe.h, 1)
                pe.inc()
                t.wait_ge(sm.h, sm.v + tc + 1)
        block.tensor(trb2)

        for tc in range(4):
            def srb2(s, tc=tc, pv=pe.v - 3 + tc):
                s.wait_ge(pe.h, pv)
                s.copy(RNb[:, 512 * tc:512 * (tc + 1)], psG[:, 0:512])
                s.drain()
                s.nop().then_inc(sm.h, 1)
                sm.inc()
            block.scalar(srb2)
        rnb_done = sm.v

        ctx1.close()

        ctx2 = ExitStack()
        hn = [pr.sb(f"hn{i}", [128, 512], BF16, ctx2) for i in range(64)]
        wpb = [pr.sb(f"wpb{i}", [128, 64 * 128], BF16, ctx2) for i in range(2)]
        x1f = [pr.sb(f"x1f{i}", [128, 512], F32, ctx2) for i in range(2)]
        atb = [pr.sb(f"atb{i}", [128, 512], BF16, ctx2) for i in range(2)]
        fo = pr.sb("fo", [128, 512], F32, ctx2)
        psO = pr.ps("psO", [128, 512], F32, ctx2)
        s_hn = Stream(pr, "shn", 4)
        s_wpb = Stream(pr, "swpb", 2)

        for tcol in range(4):
            ts_ = slice(512 * tcol, 512 * (tcol + 1))
            for ht in range(64):
                def qhn(q, ht=ht, tcol=tcol, vv=vm.v, rd=rnb_done):
                    q.wait_ge(sm.h, rd)
                    if ht >= 4 or tcol > 0:
                        q.wait_ge(vm.h, vv)
                    s_hn.issue(q, ht, [(hn[ht][:],
                                        h_hbm[128 * ht:128 * (ht + 1), ts_], {})])
                block.sync(qhn)
                h_t = (s_hn.sems[ht % 4].h, s_hn.sems[ht % 4].v)

                def vhn(v, ht=ht, tcol=tcol, htt=h_t):
                    v.wait_ge(*htt)
                    v.tensor_tensor(hn[ht][:], hn[ht][:], RNb[:, ts_], OP.mult)
                    mark(v, vm)
                block.vector(vhn)
            hn_done = vm.v

            def gwp0(g, tcol=tcol, hd=hn_done):
                g.wait_ge(vm.h, hd)
                s_wpb.issue(g, 0, [(
                    wpb[0][:].rearrange("p (a c) -> p a c", c=128),
                    wqpT_hbm[:, 0:128].rearrange("(a p) c -> p a c", p=128), {})])
                s_wpb.issue(g, 1, [(
                    wpb[1][:].rearrange("p (a c) -> p a c", c=128),
                    wqpT_hbm[:, 128:256].rearrange("(a p) c -> p a c", p=128), {})])
            block.gpsimd(gwp0)
            wpb_t = {0: s_wpb.totals()[0], 1: s_wpb.totals()[1]}

            for dc in range(16):
                def tp(t, dc=dc, tcol=tcol, hd=hn_done, wt_=wpb_t[dc % 2], vv=vm.v):
                    t.wait_ge(vm.h, hd)
                    t.wait_ge(*wt_)
                    if dc > 0:
                        t.wait_ge(vm.h, vv)
                    for ht in range(64):
                        t.matmul(psO[:], wpb[dc % 2][:, 128 * ht:128 * (ht + 1)],
                                 hn[ht][:], start=(ht == 0), stop=(ht == 63))
                    mark(t, pe)
                block.tensor(tp)
                mmv = pe.v
                if dc < 14:
                    def gwp(g, dc=dc, tcol=tcol, mmv=mmv):
                        g.wait_ge(pe.h, mmv)
                        tgt = s_wpb.issue(g, dc, [(
                            wpb[dc % 2][:].rearrange("p (a c) -> p a c", c=128),
                            wqpT_hbm[:, 128 * (dc + 2):128 * (dc + 3)].rearrange(
                                "(a p) c -> p a c", p=128), {})])
                        wpb_t[dc % 2] = tgt
                    block.gpsimd(gwp)

                def qx1(q, dc=dc, tcol=tcol, vv=vm.v):
                    q.wait_ge(vm.h, vv)
                    s_x1.issue(q, dc, [(x1f[dc % 2][:],
                                        x1T_d[128 * dc:128 * (dc + 1), ts_], {})])
                    s_at.issue(q, dc, [(atb[dc % 2][:],
                                        aT_hbm[128 * dc:128 * (dc + 1), ts_], {})])
                block.sync(qx1)
                x1_t = (s_x1.sems[dc % 2].h, s_x1.sems[dc % 2].v)
                at_t = (s_at.sems[dc % 2].h, s_at.sems[dc % 2].v)

                def vf(v, dc=dc, mmv=mmv, xt=x1_t, at=at_t):
                    v.wait_ge(pe.h, mmv)
                    v.wait_ge(*xt)
                    v.wait_ge(*at)
                    v.tensor_tensor(fo[:], psO[:], atb[dc % 2][:], OP.add)
                    v.drain()
                    v.tensor_tensor(fo[:], fo[:], x1f[dc % 2][:], OP.add)
                    mark(v, vm)
                block.vector(vf)

                def qy(q, dc=dc, tcol=tcol, vv=vm.v):
                    q.wait_ge(vm.h, vv)
                    q.dma_start(out=yT_o[128 * dc:128 * (dc + 1), ts_],
                                in_=fo[:]).then_inc(st.h, 16)
                    st.inc(16)
                block.sync(qy)
                stv = st.v

                def vf2(v, sv=stv):
                    v.wait_ge(st.h, sv)
                    mark(v, vm)
                block.vector(vf2)

        ctx2.close()

        def qfin(q):
            q.wait_ge(st.h, st.v)
        block.sync(qfin)
        pr.finish = [(pe.h, pe.v), (vm.h, vm.v), (sm.h, sm.v), (st.h, st.v)]
        pr.ctx.pop_all()

    return pr





RG_PAIR = [[0, 1], [2, 3], [4, 5], [6, 7]]


def build_fused(nphase=5, nocoll=False):
    pr = Prog("fz")
    nc = pr.nc
    io_sh = {
        "ones1": pr.din("ones1", [1, 128], BF16),
        "onescol": pr.din("onescol", [128, 1], BF16),
        "ident_d": pr.din("ident_d", [128, 128], BF16),
    }
    ins = {
        "xT": pr.din("xT", [D, T], F32),
        "winT": pr.din("winT", [D, 3 * D], F32),
        "binp": pr.din("binp", [128, 48], F32),
        "cosT": pr.din("cosT", [512, T], BF16),
        "sinT": pr.din("sinT", [512, T], BF16),
        "wq": pr.din("wq", [D, 512], F32),
        "keysT": pr.din("keysT", [8, 64, 128], F32),
        "iota64": pr.din("iota64", [128, 64], F32),
        "ecat": pr.din("ecat", [16384, 4096], F32),
        "woutT": pr.din("woutT", [D, D], F32),
        "bout_d": pr.din("bout_d", [128, 16], F32),
        "kvaT": pr.din("kvaT", [D, 576], F32),
        "lnw_d": pr.din("lnw_d", [128, 4], F32),
        "kvbn": pr.din("kvbn", [512, 2048], F32),
        "kvbv": pr.din("kvbv", [512, 2048], F32),
        "qpT": pr.din("qpT", [D, 1536], F32),
        "opT": pr.din("opT", [1024, D], F32),
        "masks": pr.din("masks", [4, 128, 512], F32),
        "wgu": pr.din("wgu", [8 * D, D], F32),
        "wpr": pr.din("wpr", [D, 4 * D], F32),
    }
    yT_o = pr.dout("yT_o", [D, T], F32)

    scr = {
        "xnT_s": pr.dtmp("xnT_s", [D, T], BF16),
        "qT_s": pr.dtmp("qT_s", [D, T], BF16),
        "apart_s": pr.dtmp("apart_s", [128, 16], F32),
        "apart_sum": pr.dtmp("apart_sum", [128, 16], F32),
        "fi_s": pr.dtmp("fi_s", [16, 128, 32], U32),
        "fs_s": pr.dtmp("fs_s", [16, 128, 32], F32),
        "x1T_s": pr.dtmp("x1T_s", [D, T], F32),
        "x1s2": pr.dtmp("x1s2", [D, T], F32),
        "xn2T_s": pr.dtmp("xn2T_s", [D, T], BF16),
        "xn2f_ag": pr.dtmp("xn2f_ag", [2 * D, T], BF16),
        "kT_s": pr.dtmp("kT_s", [2048, S], BF16),
        "krot_s": pr.dtmp("krot_s", [64, S], BF16),
        "v_s": pr.dtmp("v_s", [S, 2048], BF16),
        "q2T_s": pr.dtmp("q2T_s", [1536, S], BF16),
        "op_s": pr.dtmp("op_s", [S, D], BF16),
        "opsum_s": pr.dtmp("opsum_s", [T, D], BF16),
    }

    gbar = pr.sem("gbar")
    ccA = pr.sem("ccA")
    ccG = pr.sem("ccG")
    ccR = pr.sem("ccR")

    def phase(fn, io, coll=None):
        pr.pctx = ExitStack()
        fn(pr=pr, io=io, gate=(gbar.h, gbar.v) if gbar.v else None)
        pr.pctx.close()
        pr.pctx = None
        block = pr.block

        def qbar(q, fin=pr.finish):
            wait_all(q, fin)
            q.nop().then_inc(gbar.h, 1)
        block.sync(qbar)
        gbar.inc()

        if coll is not None and nocoll:
            coll = None
        if coll is not None:
            kind, op_, cin, cout, csem = coll

            def gcc(g, gv=gbar.v):
                g.wait_ge(gbar.h, gv)
                g.collective_compute(kind, op_, ins=[cin.opt()], outs=[cout.opt()],
                                     replica_groups=RG_PAIR).then_inc(csem.h, 1)
            block.gpsimd(gcc)
            csem.inc(1)

            def qcc(q, cs=(csem.h, csem.v)):
                q.wait_ge(*cs)
                q.nop().then_inc(gbar.h, 1)
            block.sync(qcc)
            gbar.inc()

    pr.prefix = "a_"
    phase(build_l0a,
          {**io_sh, **{k: ins[k] for k in ["xT", "winT", "binp", "cosT", "sinT",
                                           "wq", "keysT", "iota64"]},
           "xnT_o": scr["xnT_s"], "qT_o": scr["qT_s"], "apart_o": scr["apart_s"],
           "fi_o": scr["fi_s"], "fs_o": scr["fs_s"]},
          coll=("AllReduce", OP.add, scr["apart_s"], scr["apart_sum"], ccA))

    if nphase <= 1:
        return pr
    pr.prefix = "b_"
    phase(build_l0b,
          {**io_sh, "xT": ins["xT"], "xnT_i": scr["xnT_s"], "qT_i": scr["qT_s"],
           "ap0": scr["apart_sum"], "fi_i": scr["fi_s"], "fs_i": scr["fs_s"],
           "ecat": ins["ecat"], "woutT": ins["woutT"], "bout_d": ins["bout_d"],
           "x1T_o": scr["x1T_s"], "xn2T_o": scr["xn2T_s"], "x1s": scr["x1s2"]},
          coll=("AllGather", OP.bypass, scr["xn2T_s"], scr["xn2f_ag"], ccG))

    if nphase <= 2:
        return pr
    pr.prefix = "c_"
    phase(build_l1a,
          {**io_sh, "xn2f_ag": scr["xn2f_ag"],
           **{k: ins[k] for k in ["kvaT", "lnw_d", "kvbn", "kvbv", "qpT"]},
           "kT_o": scr["kT_s"], "krotT_o": scr["krot_s"], "v_o": scr["v_s"],
           "q2T_o": scr["q2T_s"]})

    if nphase <= 3:
        return pr
    pr.prefix = "d_"
    phase(build_l1b,
          {**io_sh, "kT": scr["kT_s"][0:1024, :], "krotT": scr["krot_s"],
           "v": scr["v_s"][:, 0:1024], "qT": scr["q2T_s"], "opT": ins["opT"],
           "masks": ins["masks"], "op_o": scr["op_s"]},
          coll=("ReduceScatter", OP.add, scr["op_s"], scr["opsum_s"], ccR))

    if nphase <= 4:
        return pr
    pr.prefix = "e_"
    phase(build_l1c,
          {**io_sh, "xn2T": scr["xn2T_s"], "x1T": scr["x1T_s"],
           "opa": scr["opsum_s"], "wgu": ins["wgu"], "wpr": ins["wpr"],
           "yT_o": yT_o})

    return pr


def fused_inmaps(prep):
    cos, sin = prep["cosT"], prep["sinT"]
    maps = []
    for c in range(NCORE):
        r = c % 2
        horder = list(range(r * 8, r * 8 + 8)) + list(range((1 - r) * 8,
                                                            (1 - r) * 8 + 8))
        cols = np.concatenate([np.arange(128) + 128 * h for h in horder])
        maps.append({
            "xT": prep["xT"][c],
            "winT": prep["winT"], "binp": prep["binp"],
            "cosT": np.ascontiguousarray(cos[:, r * T:(r + 1) * T]),
            "sinT": np.ascontiguousarray(sin[:, r * T:(r + 1) * T]),
            "wq": prep["wq"], "keysT": prep["keysT"], "iota64": prep["iota64"],
            "ecat": prep["e_cat"], "woutT": prep["woutT"], "bout_d": prep["bout"],
            "kvaT": prep["kvaT"], "lnw_d": prep["lnw"],
            "kvbn": np.ascontiguousarray(prep["kvbT_nope"][:, cols]),
            "kvbv": np.ascontiguousarray(prep["kvbT_v"][:, cols]),
            "qpT": np.ascontiguousarray(prep["qprojT"][:, r * 1536:(r + 1) * 1536]),
            "opT": np.ascontiguousarray(prep["oprojT"][r * 1024:(r + 1) * 1024]),
            "masks": prep["masks"],
            "wgu": prep["wgu"], "wpr": prep["wproj"],
            "ones1": prep["ones1"], "onescol": prep["onescol"],
            "ident_d": prep["ident"],
        })
    return maps


_PROGS = {}


def _prog(name):
    if name not in _PROGS:
        _PROGS[name] = {"l0a": build_l0a, "l0b": build_l0b, "l1a": build_l1a,
                        "l1b": build_l1b, "l1c": build_l1c}[name]()
    return _PROGS[name]


def _run(name, maps):
    pr = _prog(name)
    res = run_bass_kernel_spmd(pr.nc, maps, core_ids=list(range(NCORE)))
    return res.results


MODE = "fused"


def kernel(**inputs):
    prep = host_prep(inputs)
    if MODE == "fused":
        if "fused" not in _PROGS:
            _PROGS["fused"] = build_fused()
        pr = _PROGS["fused"]
        res = run_bass_kernel_spmd(pr.nc, fused_inmaps(prep),
                                   core_ids=list(range(NCORE)))
        out = np.empty((4, S, D), np.float32)
        for c in range(NCORE):
            b, r = c // 2, c % 2
            out[b, r * T:(r + 1) * T, :] = res.results[c]["yT_o"].T
        return out
    ra = _run("l0a", l0a_inmaps(prep))
    rb = _run("l0b", l0b_inmaps(prep, ra))

    maps = []
    for c in range(NCORE):
        base, r = (c // 2) * 2, c % 2
        maps.append({
            "xn2f": np.concatenate([rb[base]["xn2T_o"], rb[base + 1]["xn2T_o"]],
                                   axis=1),
            "kvaT": prep["kvaT"], "lnw_d": prep["lnw"],
            "kvbn": prep["kvbT_nope"], "kvbv": prep["kvbT_v"],
            "qpT": np.ascontiguousarray(prep["qprojT"][:, r * 1536:(r + 1) * 1536]),
            "ones1": prep["ones1"], "onescol": prep["onescol"]})
    r1a = _run("l1a", maps)

    maps = []
    for c in range(NCORE):
        r = c % 2
        maps.append({
            "kT": np.ascontiguousarray(r1a[c]["kT_o"][r * 1024:(r + 1) * 1024]),
            "krotT": r1a[c]["krotT_o"],
            "v": np.ascontiguousarray(r1a[c]["v_o"][:, r * 1024:(r + 1) * 1024]),
            "qT": r1a[c]["q2T_o"],
            "opT": np.ascontiguousarray(prep["oprojT"][r * 1024:(r + 1) * 1024]),
            "masks": prep["masks"],
            "ones1": prep["ones1"], "onescol": prep["onescol"]})
    r1b = _run("l1b", maps)

    maps = []
    for c in range(NCORE):
        r = c % 2
        part = c + 1 if r == 0 else c - 1
        maps.append({
            "xn2T": rb[c]["xn2T_o"], "x1T": rb[c]["x1T_o"],
            "opa": np.ascontiguousarray(r1b[c]["op_o"][r * T:(r + 1) * T]),
            "opb": np.ascontiguousarray(r1b[part]["op_o"][r * T:(r + 1) * T]),
            "wgu": prep["wgu"], "wpr": prep["wproj"],
            "ident_d": prep["ident"], "ones1": prep["ones1"],
            "onescol": prep["onescol"]})
    r1c = _run("l1c", maps)

    out = np.empty((4, S, D), np.float32)
    for c in range(NCORE):
        b, r = c // 2, c % 2
        out[b, r * T:(r + 1) * T, :] = r1c[c]["yT_o"].T
    return out


# version 08
# speedup vs baseline: 1.3789x; 1.3789x over previous
import numpy as np
import ml_dtypes as mld
from contextlib import ExitStack

import concourse.bass as bass
import concourse.mybir as mybir
from concourse.alu_op_type import AluOpType as OP
from concourse.bass_utils import run_bass_kernel_spmd

F32 = mybir.dt.float32
BF16 = mybir.dt.bfloat16
U32 = mybir.dt.uint32
AF = mybir.ActivationFunctionType
AXX = mybir.AxisListType.X
IOA = bass.IndirectOffsetOnAxis

NCORE = 8
D = 2048
T = 2048
S = 4096
NDT = D // 128
NTC = T // 512
ROT = 1024
NEG = -1e9
QSCALE = 192.0 ** -0.5



def _rope_perm(n):
    return np.concatenate([np.arange(0, ROT, 2), np.arange(1, ROT, 2),
                           np.arange(ROT, n)]).astype(np.int64)


def host_prep(inp):
    p = {}
    perm = _rope_perm(D)
    perm_all = np.concatenate([perm, perm + D, perm + 2 * D])
    W_in = np.asarray(inp["L0_attn_W_in"])
    p["winT"] = np.ascontiguousarray(W_in[perm_all].T)
    p["binp"] = np.ascontiguousarray(
        np.asarray(inp["L0_attn_b_in"])[perm_all].reshape(48, 128).T)
    W_out = np.asarray(inp["L0_attn_W_out"])
    p["woutT"] = np.ascontiguousarray(W_out[:, perm].T)
    p["bout"] = np.ascontiguousarray(
        np.asarray(inp["L0_attn_b_out"]).reshape(16, 128).T)
    p["wq"] = np.ascontiguousarray(np.asarray(inp["L0_ffn_Wq"]))
    keys = np.asarray(inp["L0_ffn_keys"])
    p["keysT"] = np.ascontiguousarray(
        keys.transpose(0, 2, 3, 1).reshape(8, 64, 128))
    p["e_cat"] = np.ascontiguousarray(
        np.concatenate([np.asarray(inp["L0_ffn_e_down"]),
                        np.asarray(inp["L0_ffn_e_up"])], axis=1))

    p["qprojT"] = np.ascontiguousarray(np.asarray(inp["L1_attn_q_proj_w"]).T)
    p["kvaT"] = np.ascontiguousarray(np.asarray(inp["L1_attn_kv_a_proj_w"]).T)
    p["lnw"] = np.ascontiguousarray(
        np.asarray(inp["L1_attn_kv_a_ln_w"]).reshape(4, 128).T)
    kvbT = np.asarray(inp["L1_attn_kv_b_proj_w"]).T
    nope_cols = (np.arange(16)[:, None] * 256 + np.arange(128)[None]).reshape(-1)
    p["kvbT_nope"] = np.ascontiguousarray(kvbT[:, nope_cols])
    p["kvbT_v"] = np.ascontiguousarray(kvbT[:, nope_cols + 128])
    p["oprojT"] = np.ascontiguousarray(np.asarray(inp["L1_attn_o_proj_w"]).T)
    p["wgu"] = np.ascontiguousarray(np.asarray(inp["L1_ffn_w_gate_up"]))
    p["wproj"] = np.ascontiguousarray(np.asarray(inp["L1_ffn_w_proj"]))

    X = np.asarray(inp["L0_attn_X"])
    p["xT"] = [np.ascontiguousarray(X[c // 2, (c % 2) * T:(c % 2 + 1) * T, :].T)
               for c in range(NCORE)]

    invf = 10000.0 ** (-(np.arange(512, dtype=np.float64) * 2.0 / ROT))
    ang = np.arange(S, dtype=np.float64)[None, :] * invf[:, None]
    p["cosT"] = np.cos(ang).astype(mld.bfloat16)
    p["sinT"] = np.sin(ang).astype(mld.bfloat16)
    p["ones1"] = np.ones((1, 128), dtype=mld.bfloat16)
    p["onescol"] = np.ones((128, 1), dtype=mld.bfloat16)
    p["ident"] = np.eye(128, dtype=np.float32).astype(mld.bfloat16)
    p["iota64"] = np.broadcast_to(np.arange(64, dtype=np.float32), (128, 64)).copy()
    masks = np.zeros((4, 128, 512), dtype=np.float32)
    for j in range(4):
        masks[j] = np.where(np.arange(128)[:, None] + 128 * j > np.arange(512)[None, :],
                            NEG, 0.0)
    p["masks"] = masks
    return p



class Sem:
    def __init__(self, nc, ctx, name):
        self.h = ctx.enter_context(nc.semaphore(name))
        self.v = 0

    def inc(self, n=1):
        self.v += n
        return self.v


class Prog:
    def __init__(self, name):
        self.nc = bass.Bass("TRN2", target_bir_lowering=False)
        self.name = name
        self.ctx = ExitStack()
        self.pctx = None
        self.prefix = ""
        self.block = None

    def sem(self, name):
        return Sem(self.nc, self.ctx, self.prefix + name)

    def sb(self, name, shape, dt, ctx=None):
        return (ctx or self.pctx or self.ctx).enter_context(
            self.nc.sbuf_tensor(self.prefix + name, list(shape), dt))

    def ps(self, name, shape, dt=F32, ctx=None):
        return (ctx or self.pctx or self.ctx).enter_context(
            self.nc.psum_tensor(self.prefix + name, list(shape), dt))

    def din(self, name, shape, dt):
        return self.nc.dram_tensor(name, list(shape), dt, kind="ExternalInput").ap()

    def dout(self, name, shape, dt):
        return self.nc.dram_tensor(name, list(shape), dt, kind="ExternalOutput").ap()

    def dtmp(self, name, shape, dt):
        return self.nc.dram_tensor(self.prefix + name, list(shape), dt,
                                   kind="Internal").ap()


def mark(eng, msem):
    eng.drain()
    eng.nop().then_inc(msem.h, 1)
    msem.inc()
    return msem.v


def tmark(t, msem, inst=None):
    (inst if inst is not None else t.nop()).then_inc(msem.h, 1)
    msem.inc()
    return msem.v


def get_io(pr, io, name, shape, dt, out=False):
    if io is not None and name in io:
        return io[name]
    return (pr.dout if out else pr.din)(name, list(shape), dt)


def gate_all(block, gate):
    if gate is None:
        return
    for emit in (block.sync, block.tensor, block.vector, block.scalar,
                 block.gpsimd):
        emit(lambda e, g=gate: e.wait_ge(g[0], g[1]))


class Stream:

    def __init__(self, pr, name, nslot):
        self.sems = [pr.sem(f"{name}{i}") for i in range(nslot)]
        self.n = nslot

    def issue(self, eng, slot, dmas):
        sem = self.sems[slot % self.n]
        for out, in_, kw in dmas:
            eng.dma_start(out=out, in_=in_, **kw).then_inc(sem.h, 16)
            sem.inc(16)
        return (sem.h, sem.v)

    def totals(self):
        return [(s.h, s.v) for s in self.sems]


def wait_all(eng, targets):
    for h, v in targets:
        if v > 0:
            eng.wait_ge(h, v)



def emit_ln(pr, b, src_dram, xnT, ms, eps=1e-5, tag="ln"):
    block = pr.block
    vm, sm, pe = ms["vm"], ms["sm"], ms["pe"]
    ctx = ExitStack()
    xf = [pr.sb(f"{tag}_xf{i}", [128, T], F32, ctx) for i in range(2)]
    xb = pr.sb(f"{tag}_xb", [128, T], BF16, ctx)
    xsq = pr.sb(f"{tag}_xsq", [128, T], BF16, ctx)
    m_row = pr.sb(f"{tag}_m", [1, T], F32, ctx)
    r_row = pr.sb(f"{tag}_r", [1, T], F32, ctx)
    sc_row = pr.sb(f"{tag}_sc", [1, T], F32, ctx)
    rb16 = pr.sb(f"{tag}_rb16", [1, T], BF16, ctx)
    mrb16 = pr.sb(f"{tag}_mrb16", [1, T], BF16, ctx)
    Rb = pr.sb(f"{tag}_Rb", [128, T], BF16, ctx)
    MRb = pr.sb(f"{tag}_MRb", [128, T], BF16, ctx)
    tmp = pr.sb(f"{tag}_tmp", [128, T], F32, ctx)
    psA = pr.ps(f"{tag}_psA", [128, 2048], F32, ctx)
    psB = pr.ps(f"{tag}_psB", [128, 2048], F32, ctx)
    s_x = Stream(pr, f"{tag}sx", 2)
    ocol, o1 = b["ocol"], b["o1"]

    cons = []
    for dt in range(NDT):
        def qf(q, dt=dt):
            if dt >= 2:
                q.wait_ge(vm.h, cons[dt - 2][0])
                q.wait_ge(sm.h, cons[dt - 2][1])
            s_x.issue(q, dt, [(xf[dt % 2][:],
                               src_dram[128 * dt:128 * (dt + 1), :], {})])
        block.sync(qf)
        lt = (s_x.sems[dt % 2].h, s_x.sems[dt % 2].v)

        def vf(v, dt=dt, lt=lt):
            v.wait_ge(*lt)
            if dt >= 1:
                v.wait_ge(pe.h, pe.v)
            v.tensor_copy(xb[:], xf[dt % 2][:])
            mark(v, vm)
        block.vector(vf)

        def sf(s, dt=dt, lt=lt):
            s.wait_ge(*lt)
            if dt >= 1:
                s.wait_ge(pe.h, pe.v)
            s.activation(xsq[:], xf[dt % 2][:], AF.Square)
            mark(s, sm)
        block.scalar(sf)
        cons.append((vm.v, sm.v))

        def tf(t, dt=dt, vv=vm.v, sv=sm.v):
            t.wait_ge(vm.h, vv)
            t.wait_ge(sm.h, sv)
            for tc in range(NTC):
                _mm = t.matmul(psA[0:1, 512 * tc:512 * (tc + 1)], ocol[:],
                         xb[:, 512 * tc:512 * (tc + 1)],
                         start=(dt == 0), stop=(dt == NDT - 1))
                _mm = t.matmul(psB[0:1, 512 * tc:512 * (tc + 1)], ocol[:],
                         xsq[:, 512 * tc:512 * (tc + 1)],
                         start=(dt == 0), stop=(dt == NDT - 1))
            tmark(t, pe, _mm)
        block.tensor(tf)

    def vrows(v, pv=pe.v):
        v.wait_ge(pe.h, pv)
        v.tensor_scalar(m_row[:], psA[0:1, :], 1.0 / D, None, OP.mult)
        v.tensor_scalar(sc_row[:], psB[0:1, :], 1.0 / D, None, OP.mult)
        v.drain()
        v.tensor_tensor(r_row[:], m_row[:], m_row[:], OP.mult)
        v.drain()
        v.tensor_tensor(sc_row[:], sc_row[:], r_row[:], OP.subtract)
        v.drain()
        v.tensor_scalar(sc_row[:], sc_row[:], eps, None, OP.add)
        mark(v, vm)
    block.vector(vrows)

    def srows(s, vv=vm.v):
        s.wait_ge(vm.h, vv)
        s.activation(r_row[:], sc_row[:], AF.Ln)
        s.drain()
        s.activation(r_row[:], r_row[:], AF.Exp, scale=-0.5)
        mark(s, sm)
    block.scalar(srows)

    def vmr(v, sv=sm.v):
        v.wait_ge(sm.h, sv)
        v.tensor_tensor(sc_row[:], m_row[:], r_row[:], OP.mult)
        v.drain()
        v.tensor_copy(rb16[:], r_row[:])
        v.tensor_copy(mrb16[:], sc_row[:])
        mark(v, vm)
    block.vector(vmr)

    def tbc(t, vv=vm.v):
        t.wait_ge(vm.h, vv)
        for tc in range(NTC):
            _mm = t.matmul(psA[:, 512 * tc:512 * (tc + 1)], o1[:],
                     rb16[:, 512 * tc:512 * (tc + 1)], start=True, stop=True)
            _mm = t.matmul(psB[:, 512 * tc:512 * (tc + 1)], o1[:],
                     mrb16[:, 512 * tc:512 * (tc + 1)], start=True, stop=True)
        tmark(t, pe, _mm)
    block.tensor(tbc)

    def vbc(v, pv=pe.v):
        v.wait_ge(pe.h, pv)
        v.tensor_copy(Rb[:], psA[:])
        v.tensor_copy(MRb[:], psB[:])
        mark(v, vm)
    block.vector(vbc)
    bc = vm.v

    marks = []
    for dt in range(NDT):
        def qf2(q, dt=dt):
            q.wait_ge(vm.h, bc if dt < 2 else marks[dt - 2])
            s_x.issue(q, dt, [(xf[dt % 2][:],
                               src_dram[128 * dt:128 * (dt + 1), :], {})])
        block.sync(qf2)
        lt = (s_x.sems[dt % 2].h, s_x.sems[dt % 2].v)

        def vf2(v, dt=dt, lt=lt):
            v.wait_ge(*lt)
            v.tensor_tensor(tmp[:], xf[dt % 2][:], Rb[:], OP.mult)
            v.drain()
            v.tensor_tensor(xnT[dt][:], tmp[:], MRb[:], OP.subtract)
            mark(v, vm)
        block.vector(vf2)
        marks.append(vm.v)

    ctx.close()
    return vm.v



def build_l0a(cut="all", pr=None, io=None, gate=None):
    if pr is None:
        pr = Prog("l0a")
    nc = pr.nc
    xT = get_io(pr, io, "xT", [D, T], F32)
    winT = get_io(pr, io, "winT", [D, 3 * D], F32)
    binp = get_io(pr, io, "binp", [128, 48], F32)
    cosT_d = get_io(pr, io, "cosT", [512, T], BF16)
    sinT_d = get_io(pr, io, "sinT", [512, T], BF16)
    wq_d = get_io(pr, io, "wq", [D, 512], F32)
    keysT = get_io(pr, io, "keysT", [8, 64, 128], F32)
    ones1 = get_io(pr, io, "ones1", [1, 128], BF16)
    onescol = get_io(pr, io, "onescol", [128, 1], BF16)
    iota64 = get_io(pr, io, "iota64", [128, 64], F32)

    xnT_o = get_io(pr, io, "xnT_o", [D, T], BF16, out=True)
    qT_o = get_io(pr, io, "qT_o", [D, T], BF16, out=True)
    apart_o = get_io(pr, io, "apart_o", [128, 16], F32, out=True)
    fi_o = get_io(pr, io, "fi_o", [16, 128, 32], U32, out=True)
    fs_o = get_io(pr, io, "fs_o", [16, 128, 32], F32, out=True)

    with pr.ctx:
        xnT = [pr.sb(f"xnT{i}", [128, T], BF16) for i in range(NDT)]
        o1 = pr.sb("o1", [1, 128], BF16)
        ocol = pr.sb("ocol", [128, 1], BF16)
        bcol = pr.sb("bcol", [128, 48], F32)
        acol = pr.sb("acol", [128, 16], F32)

        pe = pr.sem("pe")
        vm = pr.sem("vm")
        sm = pr.sem("sm")
        st = pr.sem("st")
        ms = {"pe": pe, "vm": vm, "sm": sm, "st": st}
        s_const = Stream(pr, "cst", 1)
        s_w = Stream(pr, "sw", 2)
        s_cs = Stream(pr, "scs", 2)
        s_wq = Stream(pr, "swq", 1)
        s_qst = Stream(pr, "sqst", 1)
        s_tk = Stream(pr, "stk", 1)

        if pr.block is None:
            pr.block = pr.ctx.enter_context(nc.Block())
        block = pr.block
        gate_all(block, gate)
        b = {"o1": o1, "ocol": ocol}

        def qconst(q):
            s_const.issue(q, 0, [
                (o1[:], ones1[:], {}),
                (ocol[:], onescol[:], {}),
                (bcol[:], binp[:], {}),
            ])
        block.sync(qconst)
        const_t = s_const.totals()[0]

        def vconst(v):
            v.wait_ge(*const_t)
            mark(v, vm)
        block.vector(vconst)

        emit_ln(pr, b, xT, xnT, ms)
        xn_done = vm.v

        def qxn(q):
            q.wait_ge(vm.h, xn_done)
            for dt in range(NDT):
                q.dma_start(out=xnT_o[128 * dt:128 * (dt + 1), :],
                            in_=xnT[dt][:]).then_inc(st.h, 16)
                st.inc(16)
        block.sync(qxn)

        if cut == "ln":
            def qfin0(q):
                q.wait_ge(st.h, st.v)
            block.sync(qfin0)
            pr.ctx.pop_all()
            return pr

        ctx2 = ExitStack()
        ktT = [pr.sb(f"ktT{i}", [128, T], BF16, ctx2) for i in range(NDT)]
        cosb = [pr.sb(f"cosb{i}", [128, T], BF16, ctx2) for i in range(2)]
        sinb = [pr.sb(f"sinb{i}", [128, T], BF16, ctx2) for i in range(2)]
        p1 = pr.sb("p1", [128, T], F32, ctx2)
        p2 = pr.sb("p2", [128, T], F32, ctx2)
        sqt = pr.sb("sqt", [128, T], BF16, ctx2)
        vtile = pr.sb("vtile", [128, T], BF16, ctx2)
        fkt = pr.sb("fkt", [128, T], BF16, ctx2)
        rrow = pr.sb("rrow", [1, T], F32, ctx2)
        rn16 = pr.sb("rn16", [1, T], BF16, ctx2)
        RNb = pr.sb("RNb", [128, T], BF16, ctx2)
        scr = pr.sb("scr", [128, T], BF16, ctx2)
        qtile = pr.sb("qtile", [128, T], BF16, ctx2)
        wbuf = [pr.sb(f"wb{i}", [128, NDT * 128], BF16, ctx2) for i in range(2)]
        psA = pr.ps("psA", [128, 2048], F32, ctx2)
        psB = pr.ps("psB", [128, 2048], F32, ctx2)

        cs_t = {}

        def csload(q, i, buf):
            cs_t[i] = s_cs.issue(q, buf, [
                (cosb[buf][:], cosT_d[128 * i:128 * (i + 1), :], {}),
                (sinb[buf][:], sinT_d[128 * i:128 * (i + 1), :], {}),
            ])

        def qcs(q, xv=xn_done):
            q.wait_ge(vm.h, xv)
            csload(q, 0, 0)
            csload(q, 1, 1)
        block.sync(qcs)

        w_t = {}

        def wload(g, e, buf):
            w_t[e] = s_w.issue(g, buf, [(
                wbuf[buf][:].rearrange("p (dt c) -> p dt c", c=128),
                winT[:, 128 * e:128 * (e + 1)].rearrange("(dt p) c -> p dt c", p=128),
                {})])

        def mm_echunk(t, e, psX, buf):
            t.wait_ge(*w_t[e])
            for dt in range(NDT):
                for tc in range(NTC):
                    _mm = t.matmul(psX[:, 512 * tc:512 * (tc + 1)],
                             wbuf[buf][:, dt * 128:dt * 128 + 128],
                             xnT[dt][:, 512 * tc:512 * (tc + 1)],
                             start=(dt == 0), stop=(dt == NDT - 1))
            return tmark(t, pe, _mm)

        def gk0(g, xv=xn_done):
            g.wait_ge(vm.h, xv)
            wload(g, 16, 0)
            wload(g, 20, 1)
        block.gpsimd(gk0)

        for i in range(4):
            def tk(t, i=i):
                if i > 0:
                    t.wait_ge(vm.h, vm.v)
                mm_echunk(t, 16 + i, psA, 0)
                mm_echunk(t, 20 + i, psB, 1)
            block.tensor(tk)
            pair_mm = pe.v

            if i < 3:
                def gk(g, i=i, pm=pair_mm):
                    g.wait_ge(pe.h, pm)
                    wload(g, 16 + i + 1, 0)
                    wload(g, 20 + i + 1, 1)
                block.gpsimd(gk)

            def vk(v, i=i, pm=pair_mm):
                v.wait_ge(pe.h, pm)
                v.wait_ge(*cs_t[i])
                cb, sb_ = cosb[i % 2], sinb[i % 2]
                v.scalar_tensor_tensor(p1[:], psA[:], bcol[:, 16 + i:17 + i], cb[:],
                                       OP.add, OP.mult)
                v.scalar_tensor_tensor(p2[:], psB[:], bcol[:, 20 + i:21 + i], sb_[:],
                                       OP.add, OP.mult)
                v.drain()
                v.tensor_tensor(ktT[i][:], p1[:], p2[:], OP.subtract)
                v.drain()
                v.scalar_tensor_tensor(p1[:], psB[:], bcol[:, 20 + i:21 + i], cb[:],
                                       OP.add, OP.mult)
                v.scalar_tensor_tensor(p2[:], psA[:], bcol[:, 16 + i:17 + i], sb_[:],
                                       OP.add, OP.mult)
                v.drain()
                v.tensor_tensor(ktT[4 + i][:], p1[:], p2[:], OP.add)
                mark(v, vm)
            block.vector(vk)

            if 0 < i < 3:
                def qcs2(q, i=i, vv=vm.v):
                    q.wait_ge(vm.h, vv)
                    csload(q, i + 1, (i + 1) % 2)
                block.sync(qcs2)

        if cut == "kpair":
            def qfin2(q, vv=vm.v):
                q.wait_ge(vm.h, vv)
                q.wait_ge(st.h, st.v)
            block.sync(qfin2)
            pr.ctx.pop_all()
            return pr

        def gkp(g, pv=pe.v):
            g.wait_ge(pe.h, pv)
            wload(g, 24, 0)
            wload(g, 25, 1)
        block.gpsimd(gkp)

        for j in range(8):
            def tkp(t, j=j, vv=vm.v):
                t.wait_ge(vm.h, vv if j == 0 else vm.v)
                mm_echunk(t, 24 + j, psA if j % 2 == 0 else psB, j % 2)
            block.tensor(tkp)
            mmv = pe.v
            if j < 6:
                def gkp2(g, j=j, mmv=mmv):
                    g.wait_ge(pe.h, mmv)
                    wload(g, 26 + j, j % 2)
                block.gpsimd(gkp2)

            def vkp(v, j=j, mmv=mmv):
                v.wait_ge(pe.h, mmv)
                v.tensor_scalar(ktT[8 + j][:], (psA if j % 2 == 0 else psB)[:],
                                bcol[:, 24 + j:25 + j], None, OP.add)
                mark(v, vm)
            block.vector(vkp)
        k_done = vm.v

        for dt in range(NDT):
            def sq(s, dt=dt, kd=k_done):
                s.wait_ge(vm.h, kd)
                if dt > 0:
                    s.wait_ge(pe.h, pe.v)
                s.activation(sqt[:], ktT[dt][:], AF.Square)
                mark(s, sm)
            block.scalar(sq)

            def tn(t, dt=dt, sv=sm.v):
                t.wait_ge(sm.h, sv)
                for tc in range(NTC):
                    _mm = t.matmul(psA[0:1, 512 * tc:512 * (tc + 1)], ocol[:],
                             sqt[:, 512 * tc:512 * (tc + 1)],
                             start=(dt == 0), stop=(dt == NDT - 1))
                tmark(t, pe, _mm)
            block.tensor(tn)

        def vnorm(v, pv=pe.v):
            v.wait_ge(pe.h, pv)
            v.tensor_scalar(rrow[:], psA[0:1, :], 1.0, 1e-24, OP.mult, OP.max)
            mark(v, vm)
        block.vector(vnorm)

        def snorm(s, vv=vm.v):
            s.wait_ge(vm.h, vv)
            s.activation(rrow[:], rrow[:], AF.Ln)
            s.drain()
            s.activation(rrow[:], rrow[:], AF.Exp, scale=-0.5)
            mark(s, sm)
        block.scalar(snorm)

        def vnorm2(v, sv=sm.v):
            v.wait_ge(sm.h, sv)
            v.tensor_scalar(rrow[:], rrow[:], 1e12, None, OP.min)
            v.drain()
            v.tensor_copy(rn16[:], rrow[:])
            mark(v, vm)
        block.vector(vnorm2)

        def trb(t, vv=vm.v):
            t.wait_ge(vm.h, vv)
            for tc in range(NTC):
                _mm = t.matmul(psA[:, 512 * tc:512 * (tc + 1)], o1[:],
                         rn16[:, 512 * tc:512 * (tc + 1)], start=True, stop=True)
            tmark(t, pe, _mm)
        block.tensor(trb)

        def vrb(v, pv=pe.v):
            v.wait_ge(pe.h, pv)
            v.tensor_copy(RNb[:], psA[:])
            v.memset(acol[:], 0.0)
            mark(v, vm)
        block.vector(vrb)
        rn_done = vm.v

        def gv0(g, pv=pe.v):
            g.wait_ge(pe.h, pv)
            wload(g, 32, 0)
            wload(g, 33, 1)
        block.gpsimd(gv0)

        for j in range(NDT):
            def tv(t, j=j, rd=rn_done):
                t.wait_ge(vm.h, rd if j == 0 else vm.v)
                mm_echunk(t, 32 + j, psB, j % 2)
            block.tensor(tv)
            mmv = pe.v
            if j < NDT - 2:
                def gv(g, j=j, mmv=mmv):
                    g.wait_ge(pe.h, mmv)
                    wload(g, 34 + j, j % 2)
                block.gpsimd(gv)

            def vv(v, j=j, mmv=mmv):
                v.wait_ge(pe.h, mmv)
                v.tensor_scalar(vtile[:], psB[:], bcol[:, 32 + j:33 + j], None, OP.add)
                v.drain()
                v.tensor_tensor(fkt[:], ktT[j][:], RNb[:], OP.mult)
                v.drain()
                v.scalar_tensor_tensor(scr[:], fkt[:], 1.0, vtile[:],
                                       OP.mult, OP.mult,
                                       accum_out=acol[:, j:j + 1])
                mark(v, vm)
            block.vector(vv)
        a_done = vm.v

        def qa(q):
            q.wait_ge(vm.h, a_done)
            q.dma_start(out=apart_o[:], in_=acol[:]).then_inc(st.h, 16)
            st.inc(16)
        block.sync(qa)

        def gq0(g, pv=pe.v):
            g.wait_ge(pe.h, pv)
            wload(g, 0, 0)
            wload(g, 4, 1)
        block.gpsimd(gq0)

        def qcsq(q, av=a_done):
            q.wait_ge(vm.h, av)
            csload(q, 0, 0)
            csload(q, 1, 1)
        block.sync(qcsq)

        for i in range(4):
            def tq(t, i=i, av=a_done):
                t.wait_ge(vm.h, av if i == 0 else vm.v)
                mm_echunk(t, i, psA, 0)
                mm_echunk(t, 4 + i, psB, 1)
            block.tensor(tq)
            pair_mm = pe.v
            if i < 3:
                def gq(g, i=i, pm=pair_mm):
                    g.wait_ge(pe.h, pm)
                    wload(g, i + 1, 0)
                    wload(g, 4 + i + 1, 1)
                block.gpsimd(gq)

            def vq(v, i=i, pm=pair_mm):
                v.wait_ge(pe.h, pm)
                v.wait_ge(*cs_t[i])
                cb, sb_ = cosb[i % 2], sinb[i % 2]
                v.scalar_tensor_tensor(p1[:], psA[:], bcol[:, i:i + 1], cb[:],
                                       OP.add, OP.mult)
                v.scalar_tensor_tensor(p2[:], psB[:], bcol[:, 4 + i:5 + i], sb_[:],
                                       OP.add, OP.mult)
                v.drain()
                v.tensor_tensor(qtile[:], p1[:], p2[:], OP.subtract)
                mark(v, vm)
            block.vector(vq)

            def qq(q, i=i, ev=vm.v):
                q.wait_ge(vm.h, ev)
                s_qst.issue(q, 0, [(qT_o[128 * i:128 * (i + 1), :], qtile[:], {})])
            block.sync(qq)
            qst_t = s_qst.totals()[0]

            def vq2(v, i=i, qt=qst_t):
                v.wait_ge(*qt)
                cb, sb_ = cosb[i % 2], sinb[i % 2]
                v.scalar_tensor_tensor(p1[:], psB[:], bcol[:, 4 + i:5 + i], cb[:],
                                       OP.add, OP.mult)
                v.scalar_tensor_tensor(p2[:], psA[:], bcol[:, i:i + 1], sb_[:],
                                       OP.add, OP.mult)
                v.drain()
                v.tensor_tensor(qtile[:], p1[:], p2[:], OP.add)
                mark(v, vm)
            block.vector(vq2)

            def qq2(q, i=i, ev=vm.v):
                q.wait_ge(vm.h, ev)
                s_qst.issue(q, 0, [(qT_o[128 * (4 + i):128 * (5 + i), :], qtile[:], {})])
            block.sync(qq2)
            qst_t2 = s_qst.totals()[0]

            def vq3(v, qt=qst_t2):
                v.wait_ge(*qt)
                mark(v, vm)
            block.vector(vq3)

            if 0 < i < 3:
                def qcsq2(q, i=i, vv=vm.v):
                    q.wait_ge(vm.h, vv)
                    csload(q, i + 1, (i + 1) % 2)
                block.sync(qcsq2)

        def gqp(g, pv=pe.v):
            g.wait_ge(pe.h, pv)
            wload(g, 8, 0)
            wload(g, 9, 1)
        block.gpsimd(gqp)

        for j in range(8):
            def tqp(t, j=j):
                t.wait_ge(vm.h, vm.v)
                mm_echunk(t, 8 + j, psA if j % 2 == 0 else psB, j % 2)
            block.tensor(tqp)
            mmv = pe.v
            if j < 6:
                def gqp2(g, j=j, mmv=mmv):
                    g.wait_ge(pe.h, mmv)
                    wload(g, 10 + j, j % 2)
                block.gpsimd(gqp2)

            def vqp(v, j=j, mmv=mmv):
                v.wait_ge(pe.h, mmv)
                v.tensor_scalar(qtile[:], (psA if j % 2 == 0 else psB)[:],
                                bcol[:, 8 + j:9 + j], None, OP.add)
                mark(v, vm)
            block.vector(vqp)

            def qqp(q, j=j, ev=vm.v):
                q.wait_ge(vm.h, ev)
                s_qst.issue(q, 0, [(qT_o[128 * (8 + j):128 * (9 + j), :], qtile[:], {})])
            block.sync(qqp)
            qst_t = s_qst.totals()[0]

            def vqp2(v, qt=qst_t):
                v.wait_ge(*qt)
                mark(v, vm)
            block.vector(vqp2)

        ctx2.close()
        qkv_end = vm.v

        if cut == "qkv":
            def qfin1(q):
                q.wait_ge(vm.h, qkv_end)
                q.wait_ge(st.h, st.v)
                wait_all(q, s_qst.totals())
            block.sync(qfin1)
            pr.ctx.pop_all()
            return pr

        ctx3 = ExitStack()
        qfT = [pr.sb(f"qfT{i}", [128, T], BF16, ctx3) for i in range(4)]
        wqb = pr.sb("wqb", [128, NDT * 512], BF16, ctx3)
        keysb = pr.sb("keysb", [128, 8 * 128], BF16, ctx3)
        iob = pr.sb("iob", [128, 64], F32, ctx3)
        s1 = pr.sb("s1", [128, 8], F32, ctx3)
        s2 = pr.sb("s2", [128, 8], F32, ctx3)
        i1 = pr.sb("i1", [128, 8], U32, ctx3)
        i2 = pr.sb("i2", [128, 8], U32, ctx3)
        i1f = pr.sb("i1f", [128, 8], F32, ctx3)
        i2f = pr.sb("i2f", [128, 8], F32, ctx3)
        i1m = pr.sb("i1m", [128, 8], F32, ctx3)
        asc = pr.sb("asc", [128, 64], F32, ctx3)
        aidx = pr.sb("aidx", [128, 64], F32, ctx3)
        fs_t = pr.sb("fs_t", [128, 32], F32, ctx3)
        fsh = pr.sb("fsh", [128, 8], F32, ctx3)
        pk_t = pr.sb("pk_t", [128, 8], U32, ctx3)
        pkf = pr.sb("pkf", [128, 8], F32, ctx3)
        fi_f = pr.sb("fi_f", [128, 32], F32, ctx3)
        fi_u = pr.sb("fi_u", [128, 32], U32, ctx3)
        eqm = pr.sb("eqm", [128, 64], F32, ctx3)
        prodm = pr.sb("prodm", [128, 64], F32, ctx3)
        psF = pr.ps("psF", [128, 2048], F32, ctx3)
        psS = pr.ps("psS", [128, 2048], F32, ctx3)

        def gw(g, vv=qkv_end):
            g.wait_ge(vm.h, vv)
            s_wq.issue(g, 0, [
                (wqb[:].rearrange("p (dt c) -> p dt c", c=512),
                 wq_d.rearrange("(dt p) c -> p dt c", p=128), {}),
                (keysb[0:64, :].rearrange("p (a c) -> p a c", c=128),
                 keysT.rearrange("a b c -> b a c"), {}),
                (keysb[64:128, :].rearrange("p (a c) -> p a c", c=128),
                 keysT.rearrange("a b c -> b a c"), {}),
                (iob[:], iota64[:], {}),
            ])
        block.gpsimd(gw)
        wq_t = s_wq.totals()[0]

        for cc in range(4):
            def tqf(t, cc=cc, wt=wq_t, vv=qkv_end):
                t.wait_ge(*wt)
                t.wait_ge(vm.h, vv)
                if cc > 0:
                    t.wait_ge(sm.h, sm.v)
                for dt in range(NDT):
                    for tc in range(NTC):
                        _mm = t.matmul(psF[:, 512 * tc:512 * (tc + 1)],
                                 wqb[:, dt * 512 + cc * 128:dt * 512 + cc * 128 + 128],
                                 xnT[dt][:, 512 * tc:512 * (tc + 1)],
                                 start=(dt == 0), stop=(dt == NDT - 1))
                tmark(t, pe, _mm)
            block.tensor(tqf)

            def sqf(s, cc=cc, pv=pe.v):
                s.wait_ge(pe.h, pv)
                s.copy(qfT[cc][:], psF[:])
                mark(s, sm)
            block.scalar(sqf)
        qf_done = sm.v

        if cut == "qf":
            def qfinq(q):
                q.wait_ge(sm.h, qf_done)
                q.wait_ge(st.h, st.v)
            block.sync(qfinq)
            pr.ctx.pop_all()
            return pr

        for tb in range(NDT):
            def tsim(t, tb=tb, qd=qf_done, wt=wq_t):
                t.wait_ge(sm.h, qd)
                t.wait_ge(*wt)
                if tb > 0:
                    t.wait_ge(vm.h, vm.v)
                for h in range(4):
                    for w in range(2):
                        b_ = 2 * h + w
                        psX = psF if b_ < 4 else psS
                        _mm = t.matmul(psX[:, 512 * (b_ % 4):512 * (b_ % 4) + 128],
                                 qfT[h][64 * w:64 * (w + 1), 128 * tb:128 * (tb + 1)],
                                 keysb[64 * w:64 * (w + 1),
                                       (2 * h + w) * 128:(2 * h + w + 1) * 128],
                                 start=True, stop=True)
                tmark(t, pe, _mm)
            block.tensor(tsim)

            def vtk(v, tb=tb, pv=pe.v):
                v.wait_ge(pe.h, pv)
                for h in range(4):
                    b1, b2 = 2 * h, 2 * h + 1
                    sl1 = (psF if b1 < 4 else psS)[:, 512 * (b1 % 4):512 * (b1 % 4) + 128]
                    sl2 = (psF if b2 < 4 else psS)[:, 512 * (b2 % 4):512 * (b2 % 4) + 128]
                    v.max(s1[:], sl1)
                    v.max(s2[:], sl2)
                    v.drain()
                    v.max_index(i1[:], s1[:], sl1)
                    v.max_index(i2[:], s2[:], sl2)
                    v.drain()
                    v.tensor_copy(i1f[:], i1[:])
                    v.tensor_copy(i2f[:], i2[:])
                    v.drain()
                    v.tensor_scalar(i1m[:], i1f[:], 128.0, None, OP.mult)
                    v.drain()
                    for j in range(8):
                        v.tensor_scalar(asc[:, 8 * j:8 * j + 8], s2[:],
                                        s1[:, j:j + 1], None, OP.add)
                        v.tensor_scalar(aidx[:, 8 * j:8 * j + 8], i2f[:],
                                        i1m[:, j:j + 1], None, OP.add)
                    v.drain()
                    v.max(fsh[:], asc[:])
                    v.drain()
                    v.max_index(pk_t[:], fsh[:], asc[:])
                    v.tensor_copy(fs_t[:, 8 * h:8 * h + 8], fsh[:])
                    v.drain()
                    v.tensor_copy(pkf[:], pk_t[:])
                    v.drain()
                    for j in range(8):
                        v.tensor_scalar(eqm[:], iob[:], pkf[:, j:j + 1],
                                        None, OP.is_equal)
                        v.drain()
                        v.tensor_tensor(prodm[:], eqm[:], aidx[:], OP.mult)
                        v.drain()
                        v.tensor_reduce(fi_f[:, 8 * h + j:8 * h + j + 1], prodm[:],
                                        axis=AXX, op=OP.add)
                        v.drain()
                v.tensor_copy(fi_u[:], fi_f[:])
                mark(v, vm)
            block.vector(vtk)

            def qtk(q, tb=tb, ev=vm.v):
                q.wait_ge(vm.h, ev)
                s_tk.issue(q, 0, [(fi_o[tb], fi_u[:], {}),
                                  (fs_o[tb], fs_t[:], {})])
            block.sync(qtk)
            tk_t = s_tk.totals()[0]

            def vtk2(v, tt=tk_t):
                v.wait_ge(*tt)
                mark(v, vm)
            block.vector(vtk2)

        ctx3.close()

        def qfin(q):
            q.wait_ge(st.h, st.v)
            wait_all(q, s_qst.totals())
            wait_all(q, s_tk.totals())
        block.sync(qfin)
        pr.finish = [(pe.h, pe.v), (vm.h, vm.v), (sm.h, sm.v), (st.h, st.v)]
        pr.ctx.pop_all()

    return pr



def l0a_inmaps(prep):
    cos, sin = prep["cosT"], prep["sinT"]
    maps = []
    for c in range(NCORE):
        r = c % 2
        maps.append({
            "xT": prep["xT"][c],
            "winT": prep["winT"],
            "binp": prep["binp"],
            "cosT": np.ascontiguousarray(cos[:, r * T:(r + 1) * T]),
            "sinT": np.ascontiguousarray(sin[:, r * T:(r + 1) * T]),
            "wq": prep["wq"],
            "keysT": prep["keysT"],
            "ones1": prep["ones1"],
            "onescol": prep["onescol"],
            "iota64": prep["iota64"],
        })
    return maps





def build_l0b(pr=None, io=None, gate=None):
    if pr is None:
        pr = Prog("l0b")
    nc = pr.nc
    summed = io is not None and "ap1" not in io
    xT = get_io(pr, io, "xT", [D, T], F32)
    xnT_i = get_io(pr, io, "xnT_i", [D, T], BF16)
    qT_i = get_io(pr, io, "qT_i", [D, T], BF16)
    ap0 = get_io(pr, io, "ap0", [128, 16], F32)
    ap1 = None if summed else get_io(pr, io, "ap1", [128, 16], F32)
    fi_i = get_io(pr, io, "fi_i", [16, 128, 32], U32)
    fs_i = get_io(pr, io, "fs_i", [16, 128, 32], F32)
    ecat = get_io(pr, io, "ecat", [16384, 4096], F32)
    woutT = get_io(pr, io, "woutT", [D, D], F32)
    bout_d = get_io(pr, io, "bout_d", [128, 16], F32)
    ident_d = get_io(pr, io, "ident_d", [128, 128], BF16)
    ones1 = get_io(pr, io, "ones1", [1, 128], BF16)
    onescol = get_io(pr, io, "onescol", [128, 1], BF16)

    x1T_o = get_io(pr, io, "x1T_o", [D, T], F32, out=True)
    xn2T_o = get_io(pr, io, "xn2T_o", [D, T], BF16, out=True)
    x1s = io["x1s"] if (io is not None and "x1s" in io) else pr.dtmp("x1s", [D, T], F32)

    with pr.ctx:
        xnT = [pr.sb(f"xnT{i}", [128, T], BF16) for i in range(NDT)]
        fT = [pr.sb(f"fT{i}", [128, T], BF16) for i in range(NDT)]
        o1 = pr.sb("o1", [1, 128], BF16)
        ocol = pr.sb("ocol", [128, 1], BF16)
        ident = pr.sb("ident", [128, 128], BF16)
        bout = pr.sb("bout", [128, 16], F32)
        acol = pr.sb("acol", [128, 16], F32)
        aps = [pr.sb(f"aps{i}", [128, 16], F32) for i in range(2)]

        pe = pr.sem("pe")
        vm = pr.sem("vm")
        sm = pr.sem("sm")
        st = pr.sem("st")
        ms = {"pe": pe, "vm": vm, "sm": sm, "st": st}
        s_const = Stream(pr, "cst", 1)
        s_xn = Stream(pr, "sxn", 1)
        s_g = Stream(pr, "sg", 4)
        s_tk = Stream(pr, "stk", 2)
        s_w = Stream(pr, "sw", 2)
        s_q = Stream(pr, "sq", 2)
        s_xf = Stream(pr, "sxf", 2)

        if pr.block is None:
            pr.block = pr.ctx.enter_context(nc.Block())
        block = pr.block
        gate_all(block, gate)
        b = {"o1": o1, "ocol": ocol}

        def qc(q):
            dmas = [
                (o1[:], ones1[:], {}),
                (ocol[:], onescol[:], {}),
                (ident[:], ident_d[:], {}),
                (bout[:], bout_d[:], {}),
                (aps[0][:], ap0[:], {}),
            ]
            if not summed:
                dmas.append((aps[1][:], ap1[:], {}))
            s_const.issue(q, 0, dmas)
            for dt in range(NDT):
                s_xn.issue(q, 0, [(xnT[dt][:], xnT_i[128 * dt:128 * (dt + 1), :], {})])
        block.sync(qc)
        const_t = s_const.totals()[0]
        xn_t = s_xn.totals()[0]

        def va(v):
            v.wait_ge(*const_t)
            if summed:
                v.tensor_copy(acol[:], aps[0][:])
            else:
                v.tensor_tensor(acol[:], aps[0][:], aps[1][:], OP.add)
            mark(v, vm)
        block.vector(va)
        a_done = vm.v

        ctx1 = ExitStack()
        xnat = pr.sb("xnat", [128, D], BF16, ctx1)
        gbuf = [pr.sb(f"gbuf{i}", [128, 4096], BF16, ctx1) for i in range(4)]
        dscr = [pr.sb(f"dscr{i}", [128, D], BF16, ctx1) for i in range(2)]
        diag = [pr.sb(f"diag{i}", [128, 128], BF16, ctx1) for i in range(2)]
        idxb = pr.sb("idxb", [128, 32], U32, ctx1)
        fsb = pr.sb("fsb", [128, 32], F32, ctx1)
        dcol = pr.sb("dcol", [128, 32], F32, ctx1)
        sil = pr.sb("sil", [128, 32], F32, ctx1)
        hcol = pr.sb("hcol", [128, 32], F32, ctx1)
        fnat = pr.sb("fnat", [128, D], BF16, ctx1)
        psU = pr.ps("psU", [128, 2048], F32, ctx1)
        psT = pr.ps("psT", [128, 128], BF16, ctx1)

        for tb in range(NDT):
            def qtk(q, tb=tb, vv=vm.v):
                q.wait_ge(vm.h, vv)
                s_tk.issue(q, tb, [(idxb[:], fi_i[tb], {}),
                                   (fsb[:], fs_i[tb], {})])
            block.sync(qtk)
            tk_t = (s_tk.sems[tb % 2].h, s_tk.sems[tb % 2].v)

            def tx(t, tb=tb, xv=xn_t, cv=const_t, pv=vm.v):
                t.wait_ge(*xv)
                t.wait_ge(*cv)
                t.wait_ge(vm.h, pv)
                for dt in range(NDT):
                    t.transpose(psT[:], xnT[dt][:, 128 * tb:128 * (tb + 1)], ident[:]).then_inc(pe.h, 1)
                    pe.inc()
                    t.wait_ge(sm.h, sm.v + 1)
                    sm.inc()
            block.tensor(tx)

            def sx(s, tb=tb, pv0=pe.v - NDT, sv0=sm.v - NDT):
                for dt in range(NDT):
                    s.wait_ge(pe.h, pv0 + dt + 1)
                    s.copy(xnat[:, 128 * dt:128 * (dt + 1)], psT[:])
                    s.drain()
                    s.nop().then_inc(sm.h, 1)
            block.scalar(sx)
            xnat_done = sm.v

            tu_mark = {}
            vh_mark = {}
            for k in range(32):
                def gk(g, tb=tb, k=k, tkt=tk_t):
                    g.wait_ge(*tkt)
                    if k >= 4:
                        g.wait_ge(pe.h, tu_mark[k - 4])
                    g.indirect_dma_start(
                        out=gbuf[k % 4][:], out_offset=None,
                        in_=ecat[:, :],
                        in_offset=IOA(ap=idxb[:, k:k + 1], axis=0),
                    ).then_inc(s_g.sems[k % 4].h, 16)
                    s_g.sems[k % 4].inc(16)
                block.gpsimd(gk)
                g_t = (s_g.sems[k % 4].h, s_g.sems[k % 4].v)

                def vd(v, tb=tb, k=k, gt=g_t, xd=xnat_done):
                    v.wait_ge(*gt)
                    v.wait_ge(sm.h, xd)
                    v.scalar_tensor_tensor(dscr[k % 2][:], gbuf[k % 4][:, 0:2048],
                                           1.0, xnat[:], OP.mult, OP.mult,
                                           accum_out=dcol[:, k:k + 1]
                                           ).then_inc(vm.h, 1)
                    vm.inc()
                block.vector(vd)

                def sk(s, k=k, vv=vm.v):
                    s.wait_ge(vm.h, vv)
                    s.activation(sil[:, k:k + 1], dcol[:, k:k + 1],
                                 AF.Silu).then_inc(sm.h, 1)
                    sm.inc()
                block.scalar(sk)

                def vh(v, k=k, sv=sm.v):
                    v.wait_ge(sm.h, sv)
                    if k >= 2:
                        v.wait_ge(pe.h, tu_mark[k - 2])
                    v.tensor_scalar(hcol[:, k:k + 1], fsb[:, k:k + 1], 0.0, None,
                                    OP.max)
                    v.drain()
                    v.tensor_tensor(hcol[:, k:k + 1], hcol[:, k:k + 1],
                                    sil[:, k:k + 1], OP.mult)
                    v.drain()
                    v.tensor_scalar(diag[k % 2][:], ident[:], hcol[:, k:k + 1],
                                    None, OP.mult).then_inc(vm.h, 1)
                    vm.inc()
                block.vector(vh)
                vh_mark[k] = vm.v

                def tu(t, tb=tb, k=k, vv=vm.v, gt=g_t):
                    t.wait_ge(vm.h, vv)
                    t.wait_ge(*gt)
                    for tc in range(4):
                        _mm = t.matmul(psU[:, 512 * tc:512 * (tc + 1)],
                                       diag[k % 2][:],
                                       gbuf[k % 4][:, 2048 + 512 * tc:
                                                   2048 + 512 * (tc + 1)],
                                       start=(k == 0), stop=(k == 31))
                    tmark(t, pe, _mm)
                block.tensor(tu)
                tu_mark[k] = pe.v

            def sf(s, tb=tb, pv=pe.v):
                s.wait_ge(pe.h, pv)
                s.copy(fnat[:], psU[:])
                mark(s, sm)
            block.scalar(sf)

            def tf2(t, tb=tb, sv=sm.v):
                t.wait_ge(sm.h, sv)
                for dt in range(NDT):
                    t.transpose(psT[:], fnat[:, 128 * dt:128 * (dt + 1)], ident[:]).then_inc(pe.h, 1)
                    pe.inc()
                    t.wait_ge(sm.h, sm.v + 1)
                    sm.inc()
            block.tensor(tf2)

            def sf2(s, tb=tb, pv0=pe.v - NDT, sv0=sm.v - NDT):
                for dt in range(NDT):
                    s.wait_ge(pe.h, pv0 + dt + 1)
                    s.copy(fT[dt][:, 128 * tb:128 * (tb + 1)], psT[:])
                    s.drain()
                    s.nop().then_inc(sm.h, 1)
            block.scalar(sf2)

            def vnext(v, sv=sm.v):
                v.wait_ge(sm.h, sv)
                mark(v, vm)
            block.vector(vnext)

        ctx1.close()
        ffn_done = vm.v

        ctx2 = ExitStack()
        ytile = [pr.sb(f"yt{i}", [128, 1024], BF16, ctx2) for i in range(NDT)]
        qbuf = [pr.sb(f"qbuf{i}", [128, 1024], BF16, ctx2) for i in range(2)]
        wbuf = [pr.sb(f"wb{i}", [128, NDT * 128], BF16, ctx2) for i in range(2)]
        xf = [pr.sb(f"xf{i}", [128, 1024], F32, ctx2) for i in range(2)]
        t1 = pr.sb("t1", [128, 1024], F32, ctx2)
        t2 = pr.sb("t2", [128, 1024], F32, ctx2)
        psW = pr.ps("psW", [128, 1024], F32, ctx2)

        w_t = {}

        def wload(g, e, buf):
            w_t[e] = s_w.issue(g, buf, [(
                wbuf[buf][:].rearrange("p (dt c) -> p dt c", c=128),
                woutT[:, 128 * e:128 * (e + 1)].rearrange("(dt p) c -> p dt c", p=128),
                {})])

        for half in range(2):
            hs = slice(1024 * half, 1024 * (half + 1))
            for dt in range(NDT):
                def qy(q, dt=dt, half=half, fv=ffn_done, vv=vm.v):
                    q.wait_ge(vm.h, vv if (half > 0 or dt >= 2) else fv)
                    s_q.issue(q, dt, [(qbuf[dt % 2][:], qT_i[128 * dt:128 * (dt + 1), hs], {})])
                block.sync(qy)
                q_t = (s_q.sems[dt % 2].h, s_q.sems[dt % 2].v)

                def vy(v, dt=dt, qt=q_t, av=a_done):
                    v.wait_ge(*qt)
                    v.wait_ge(vm.h, av)
                    v.tensor_scalar(ytile[dt][:], qbuf[dt % 2][:], acol[:, dt:dt + 1],
                                    None, OP.mult)
                    mark(v, vm)
                block.vector(vy)
            y_done = vm.v

            def g0(g, half=half, yv=y_done):
                g.wait_ge(vm.h, yv)
                wload(g, 0, 0)
                wload(g, 1, 1)
            block.gpsimd(g0)

            for d2 in range(NDT):
                def tw(t, d2=d2, yv=y_done):
                    t.wait_ge(vm.h, yv)
                    t.wait_ge(*w_t[d2])
                    if d2 > 0:
                        t.wait_ge(vm.h, vm.v)
                    for d1 in range(NDT):
                        for tc in range(2):
                            _mm = t.matmul(psW[:, 512 * tc:512 * (tc + 1)],
                                     wbuf[d2 % 2][:, d1 * 128:d1 * 128 + 128],
                                     ytile[d1][:, 512 * tc:512 * (tc + 1)],
                                     start=(d1 == 0), stop=(d1 == NDT - 1))
                    tmark(t, pe, _mm)
                block.tensor(tw)
                mmv = pe.v
                if d2 < NDT - 2:
                    def g2(g, d2=d2, mmv=mmv):
                        g.wait_ge(pe.h, mmv)
                        wload(g, d2 + 2, d2 % 2)
                    block.gpsimd(g2)

                def qx(q, d2=d2, half=half, vv=vm.v):
                    q.wait_ge(vm.h, vv)
                    s_xf.issue(q, d2, [(xf[d2 % 2][:],
                                        xT[128 * d2:128 * (d2 + 1), hs], {})])
                block.sync(qx)
                xf_t = (s_xf.sems[d2 % 2].h, s_xf.sems[d2 % 2].v)

                def vx(v, d2=d2, half=half, mmv=mmv, xt_=xf_t):
                    v.wait_ge(pe.h, mmv)
                    v.wait_ge(*xt_)
                    v.tensor_scalar(t1[:], psW[:], bout[:, d2:d2 + 1], None, OP.add)
                    v.drain()
                    v.tensor_tensor(t2[:], t1[:], xnT[d2][:, hs], OP.add)
                    v.drain()
                    v.tensor_tensor(t1[:], t2[:], fT[d2][:, hs], OP.add)
                    v.drain()
                    v.tensor_tensor(t2[:], t1[:], xf[d2 % 2][:], OP.add)
                    mark(v, vm)
                block.vector(vx)

                def qo(q, d2=d2, half=half, vv=vm.v):
                    q.wait_ge(vm.h, vv)
                    q.dma_start(out=x1T_o[128 * d2:128 * (d2 + 1), hs],
                                in_=t2[:]).then_inc(st.h, 16)
                    st.inc(16)
                    q.dma_start(out=x1s[128 * d2:128 * (d2 + 1), hs],
                                in_=t2[:]).then_inc(st.h, 16)
                    st.inc(16)
                block.sync(qo)
                stv = st.v

                def vnx(v, sv=stv):
                    v.wait_ge(st.h, sv)
                    mark(v, vm)
                block.vector(vnx)

        ctx2.close()

        emit_ln(pr, b, x1s, xnT, ms, eps=1e-5, tag="ln2")
        xn2_done = vm.v

        def qxn2(q):
            q.wait_ge(vm.h, xn2_done)
            for dt in range(NDT):
                q.dma_start(out=xn2T_o[128 * dt:128 * (dt + 1), :],
                            in_=xnT[dt][:]).then_inc(st.h, 16)
                st.inc(16)
        block.sync(qxn2)

        def qfin(q):
            q.wait_ge(st.h, st.v)
        block.sync(qfin)
        pr.finish = [(pe.h, pe.v), (vm.h, vm.v), (sm.h, sm.v), (st.h, st.v)]
        pr.ctx.pop_all()

    return pr


def l0b_inmaps(prep, l0a_res):
    maps = []
    for c in range(NCORE):
        part = c + 1 if c % 2 == 0 else c - 1
        maps.append({
            "xT": prep["xT"][c],
            "xnT_i": l0a_res[c]["xnT_o"],
            "qT_i": l0a_res[c]["qT_o"],
            "ap0": l0a_res[c]["apart_o"],
            "ap1": l0a_res[part]["apart_o"],
            "fi_i": l0a_res[c]["fi_o"],
            "fs_i": l0a_res[c]["fs_o"],
            "ecat": prep["e_cat"],
            "woutT": prep["woutT"],
            "bout_d": prep["bout"],
            "ident_d": prep["ident"],
            "ones1": prep["ones1"],
            "onescol": prep["onescol"],
        })
    return maps





def build_l1a(pr=None, io=None, gate=None):
    if pr is None:
        pr = Prog("l1a")
    nc = pr.nc
    ag = io is not None and "xn2f_ag" in io
    xn2f_d = io["xn2f_ag"] if ag else get_io(pr, io, "xn2f", [D, S], BF16)
    kvaT_d = get_io(pr, io, "kvaT", [D, 576], F32)
    lnw_d = get_io(pr, io, "lnw_d", [128, 4], F32)
    kvbn_d = get_io(pr, io, "kvbn", [512, 2048], F32)
    kvbv_d = get_io(pr, io, "kvbv", [512, 2048], F32)
    qpT_d = get_io(pr, io, "qpT", [D, 1536], F32)
    ones1 = get_io(pr, io, "ones1", [1, 128], BF16)
    onescol = get_io(pr, io, "onescol", [128, 1], BF16)

    kT_o = get_io(pr, io, "kT_o", [2048, S], BF16, out=True)
    krotT_o = get_io(pr, io, "krotT_o", [64, S], BF16, out=True)
    v_o = get_io(pr, io, "v_o", [S, 2048], BF16, out=True)
    q2T_o = get_io(pr, io, "q2T_o", [1536, S], BF16, out=True)

    with pr.ctx:
        o1 = pr.sb("o1", [1, 128], BF16)
        ocol = pr.sb("ocol", [128, 1], BF16)
        lnw = pr.sb("lnw", [128, 4], F32)
        cn = [pr.sb(f"cn{i}", [128, S], BF16) for i in range(4)]
        RNb = pr.sb("RNb", [128, S], BF16)
        ssq = pr.sb("ssq", [1, S], F32)
        rn16 = pr.sb("rn16", [1, S], BF16)

        pe = pr.sem("pe")
        vm = pr.sem("vm")
        sm = pr.sem("sm")
        st = pr.sem("st")
        s_const = Stream(pr, "cst", 1)
        s_xn = Stream(pr, "sxn", 1)
        s_w = Stream(pr, "sw", 2)

        if pr.block is None:
            pr.block = pr.ctx.enter_context(nc.Block())
        block = pr.block
        gate_all(block, gate)

        def qc(q):
            s_const.issue(q, 0, [
                (o1[:], ones1[:], {}),
                (ocol[:], onescol[:], {}),
                (lnw[:], lnw_d[:], {}),
            ])
        block.sync(qc)
        const_t = s_const.totals()[0]

        ctxA = ExitStack()
        xnf = [pr.sb(f"xnf{i}", [128, S], BF16, ctxA) for i in range(NDT)]
        wbuf = [pr.sb(f"wb{i}", [128, NDT * 128], BF16, ctxA) for i in range(2)]
        evb = pr.sb("evb", [128, 2048], BF16, ctxA)
        csq = evb
        psA = pr.ps("psA", [128, 2048], F32, ctxA)
        psR = pr.ps("psR", [128, 512], F32, ctxA)

        def qxn(q):
            for dt in range(NDT):
                if ag:
                    s_xn.issue(q, 0, [
                        (xnf[dt][:, 0:T], xn2f_d[128 * dt:128 * (dt + 1), :], {}),
                        (xnf[dt][:, T:S],
                         xn2f_d[D + 128 * dt:D + 128 * (dt + 1), :], {}),
                    ])
                else:
                    s_xn.issue(q, 0, [(xnf[dt][:],
                                       xn2f_d[128 * dt:128 * (dt + 1), :], {})])
        block.sync(qxn)
        xn_t = s_xn.totals()[0]

        w_t = {}

        def wload(g, dram, e, buf, ncol=128):
            w_t[(dram is qpT_d, e)] = s_w.issue(g, buf, [(
                wbuf[buf][:, 0:NDT * ncol].rearrange("p (dt c) -> p dt c", c=ncol),
                dram[:, ncol * e:ncol * (e + 1)].rearrange("(dt p) c -> p dt c", p=128),
                {})])

        def gq0(g):
            wload(g, qpT_d, 0, 0)
            wload(g, qpT_d, 1, 1)
        block.gpsimd(gq0)

        for mc in range(12):
            def tq(t, mc=mc, xt=xn_t, ct=const_t):
                t.wait_ge(*xt)
                t.wait_ge(*ct)
                t.wait_ge(*w_t[(True, mc)])
                if mc > 0:
                    t.wait_ge(vm.h, vm.v)
                for th in range(2):
                    for dt in range(NDT):
                        for tc in range(4):
                            _mm = t.matmul(psA[:, 512 * tc:512 * (tc + 1)],
                                     wbuf[mc % 2][:, dt * 128:dt * 128 + 128],
                                     xnf[dt][:, 2048 * th + 512 * tc:
                                              2048 * th + 512 * (tc + 1)],
                                     start=(dt == 0), stop=(dt == NDT - 1))
                    _mm.then_inc(pe.h, 1)
                    pe.inc()
                    t.wait_ge(vm.h, vm.v + th + 1)
            block.tensor(tq)
            if mc < 10:
                def gq(g, mc=mc, pv=pe.v):
                    g.wait_ge(pe.h, pv)
                    wload(g, qpT_d, mc + 2, mc % 2)
                block.gpsimd(gq)

            for th in range(2):
                def vq(v, mc=mc, th=th, pv=pe.v - 1 + th):
                    v.wait_ge(pe.h, pv)
                    v.tensor_scalar(evb[:], psA[:], QSCALE, None, OP.mult)
                    mark(v, vm)
                block.vector(vq)

                def qq(q, mc=mc, th=th, vv=vm.v):
                    q.wait_ge(vm.h, vv)
                    q.dma_start(out=q2T_o[128 * mc:128 * (mc + 1),
                                          2048 * th:2048 * (th + 1)],
                                in_=evb[:]).then_inc(st.h, 16)
                    st.inc(16)
                block.sync(qq)
                stv = st.v

                def vq2(v, sv=stv):
                    v.wait_ge(st.h, sv)
                    mark(v, vm)
                block.vector(vq2)
        q_done = vm.v

        def gc0(g, pv=pe.v):
            g.wait_ge(pe.h, pv)
            wload(g, kvaT_d, 0, 0)
            wload(g, kvaT_d, 1, 1)
        block.gpsimd(gc0)

        def vz(v, qd=q_done):
            v.wait_ge(vm.h, qd)
            v.memset(ssq[:], 0.0)
            mark(v, vm)
        block.vector(vz)

        for cc in range(5):
            ncol = 128 if cc < 4 else 64
            if cc >= 2 and cc < 4:
                pass
            for th in range(2):
                def tc_(t, cc=cc, th=th, ncol=ncol, qd=q_done):
                    t.wait_ge(vm.h, vm.v)
                    t.wait_ge(*w_t[(False, cc)])
                    for dt in range(NDT):
                        for tc in range(4):
                            _mm = t.matmul(psA[0:ncol, 512 * tc:512 * (tc + 1)],
                                     wbuf[cc % 2][:, dt * 128:dt * 128 + ncol],
                                     xnf[dt][:, 2048 * th + 512 * tc:
                                              2048 * th + 512 * (tc + 1)],
                                     start=(dt == 0), stop=(dt == NDT - 1))
                    tmark(t, pe, _mm)
                block.tensor(tc_)
                mmv = pe.v
                if th == 1 and cc < 3:
                    def gc(g, cc=cc, mmv=mmv):
                        g.wait_ge(pe.h, mmv)
                        if cc + 2 < 4:
                            wload(g, kvaT_d, cc + 2, cc % 2)
                        else:
                            w_t[(False, 4)] = s_w.issue(g, cc % 2, [(
                                wbuf[cc % 2][:].rearrange(
                                    "p (dt c) -> p dt c", c=128)[:, :, 0:64],
                                kvaT_d[:, 512:576].rearrange(
                                    "(dt p) c -> p dt c", p=128), {})])
                    block.gpsimd(gc)

                if cc < 4:
                    def sc(s, cc=cc, th=th, mmv=mmv):
                        s.wait_ge(pe.h, mmv)
                        s.copy(cn[cc][:, 2048 * th:2048 * (th + 1)], psA[:])
                        s.drain()
                        s.activation(csq[:], psA[:], AF.Square)
                        mark(s, sm)
                    block.scalar(sc)

                    def tsq(t, sv=sm.v):
                        t.wait_ge(sm.h, sv)
                        for tc in range(4):
                            _mm = t.matmul(psR[0:1, 0:512], ocol[:],
                                     csq[:, 512 * tc:512 * (tc + 1)],
                                     start=True, stop=True)
                            _mm.then_inc(pe.h, 1)
                            pe.inc()
                            t.wait_ge(vm.h, vm.v + tc + 1)
                    block.tensor(tsq)

                    for tc in range(4):
                        def vsq(v, th=th, tc=tc, pv=pe.v - 3 + tc):
                            v.wait_ge(pe.h, pv)
                            v.tensor_tensor(ssq[0:1, 2048 * th + 512 * tc:
                                                2048 * th + 512 * (tc + 1)],
                                            ssq[0:1, 2048 * th + 512 * tc:
                                                2048 * th + 512 * (tc + 1)],
                                            psR[0:1, 0:512], OP.add)
                            mark(v, vm)
                        block.vector(vsq)

                    def vcn(v, sv=sm.v):
                        v.wait_ge(sm.h, sv)
                        mark(v, vm)
                    block.vector(vcn)
                else:
                    def vrot(v, th=th, mmv=mmv):
                        v.wait_ge(pe.h, mmv)
                        v.tensor_copy(evb[0:64, 0:2048], psA[0:64, :])
                        mark(v, vm)
                    block.vector(vrot)

                    def qrot(q, th=th, vv=vm.v):
                        q.wait_ge(vm.h, vv)
                        q.dma_start(out=krotT_o[:, 2048 * th:2048 * (th + 1)],
                                    in_=evb[0:64, 0:2048]).then_inc(st.h, 16)
                        st.inc(16)
                    block.sync(qrot)
                    stv = st.v

                    def vrot2(v, sv=stv):
                        v.wait_ge(st.h, sv)
                        mark(v, vm)
                    block.vector(vrot2)
        c_done = vm.v

        def vrr(v, cd=c_done):
            v.wait_ge(vm.h, cd)
            v.tensor_scalar(ssq[:], ssq[:], 1.0 / KVR_, 1e-6, OP.mult, OP.add)
            mark(v, vm)
        block.vector(vrr)

        def srr(s, vv=vm.v):
            s.wait_ge(vm.h, vv)
            s.activation(ssq[:], ssq[:], AF.Ln)
            s.drain()
            s.activation(ssq[:], ssq[:], AF.Exp, scale=-0.5)
            mark(s, sm)
        block.scalar(srr)

        def vrr2(v, sv=sm.v):
            v.wait_ge(sm.h, sv)
            v.tensor_copy(rn16[:], ssq[:])
            mark(v, vm)
        block.vector(vrr2)

        def trb(t, vv=vm.v):
            t.wait_ge(vm.h, vv)
            for tc in range(8):
                _mm = t.matmul(psA[:, 512 * (tc % 4):512 * (tc % 4 + 1)], o1[:],
                         rn16[:, 512 * tc:512 * (tc + 1)], start=True, stop=True)
                _mm.then_inc(pe.h, 1)
                pe.inc()
                t.wait_ge(sm.h, sm.v + tc + 1)
        block.tensor(trb)

        for tc in range(8):
            def srb(s, tc=tc, pv=pe.v - 7 + tc):
                s.wait_ge(pe.h, pv)
                s.copy(RNb[:, 512 * tc:512 * (tc + 1)],
                       psA[:, 512 * (tc % 4):512 * (tc % 4 + 1)])
                s.drain()
                s.nop().then_inc(sm.h, 1)
                sm.inc()
            block.scalar(srb)
        rb_done = sm.v

        for cc in range(4):
            def vcs(v, cc=cc, rb=rb_done):
                v.wait_ge(sm.h, rb)
                v.tensor_tensor(cn[cc][:], cn[cc][:], RNb[:], OP.mult)
                v.drain()
                v.tensor_scalar(cn[cc][:], cn[cc][:], lnw[:, cc:cc + 1], None,
                                OP.mult)
                mark(v, vm)
            block.vector(vcs)
        cn_done = vm.v

        ctxA.close()

        ctxB = ExitStack()
        kwb = pr.sb("kwb", [128, 4 * 128], BF16, ctxB)
        vwb = pr.sb("vwb", [128, 4 * 2048], BF16, ctxB)
        ev2 = pr.sb("ev2", [128, 2048], BF16, ctxB)
        psK = pr.ps("psK", [128, 2048], F32, ctxB)
        s_kw = Stream(pr, "skw", 2)

        kw_t = {}

        def kwload(g, e, buf):
            kw_t[e] = s_kw.issue(g, buf, [(
                kwb[:, 0:4 * 128].rearrange("p (rt c) -> p rt c", c=128),
                kvbn_d[:, 128 * e:128 * (e + 1)].rearrange("(rt p) c -> p rt c", p=128),
                {})])

        def gk0(g, cd=cn_done):
            g.wait_ge(vm.h, cd)
            kwload(g, 0, 0)
            g.dma_start(out=vwb[:].rearrange("p (rt c) -> p rt c", c=2048),
                        in_=kvbv_d.rearrange("(rt p) c -> p rt c", p=128),
                        ).then_inc(s_kw.sems[1].h, 16)
            s_kw.sems[1].inc(16)
        block.gpsimd(gk0)
        vw_t = (s_kw.sems[1].h, s_kw.sems[1].v)

        for mc in range(16):
            for th in range(2):
                def tk(t, mc=mc, th=th, cd=cn_done):
                    t.wait_ge(vm.h, vm.v if (mc or th) else cd)
                    t.wait_ge(*kw_t[mc])
                    for rt in range(4):
                        for tc in range(4):
                            _mm = t.matmul(psK[:, 512 * tc:512 * (tc + 1)],
                                     kwb[:, rt * 128:rt * 128 + 128],
                                     cn[rt][:, 2048 * th + 512 * tc:
                                             2048 * th + 512 * (tc + 1)],
                                     start=(rt == 0), stop=(rt == 3))
                    tmark(t, pe, _mm)
                block.tensor(tk)
                mmv = pe.v

                def vk(v, mmv=mmv):
                    v.wait_ge(pe.h, mmv)
                    v.tensor_copy(ev2[:], psK[:])
                    mark(v, vm)
                block.vector(vk)

                def qk(q, mc=mc, th=th, vv=vm.v):
                    q.wait_ge(vm.h, vv)
                    q.dma_start(out=kT_o[128 * mc:128 * (mc + 1),
                                         2048 * th:2048 * (th + 1)],
                                in_=ev2[:]).then_inc(st.h, 16)
                    st.inc(16)
                block.sync(qk)
                stv = st.v

                def vk2(v, sv=stv):
                    v.wait_ge(st.h, sv)
                    mark(v, vm)
                block.vector(vk2)
            if mc < 15:
                def gkn(g, mc=mc, pv=pe.v):
                    g.wait_ge(pe.h, pv)
                    kwload(g, mc + 1, 0)
                block.gpsimd(gkn)

        for kc in range(32):
            def tv(t, kc=kc, vt=vw_t):
                t.wait_ge(*vt)
                t.wait_ge(vm.h, vm.v)
                for rt in range(4):
                    for tc in range(4):
                        _mm = t.matmul(psK[:, 512 * tc:512 * (tc + 1)],
                                 cn[rt][:, 128 * kc:128 * (kc + 1)],
                                 vwb[:, rt * 2048 + 512 * tc:
                                     rt * 2048 + 512 * (tc + 1)],
                                 start=(rt == 0), stop=(rt == 3))
                tmark(t, pe, _mm)
            block.tensor(tv)
            mmv = pe.v

            def vv2(v, mmv=mmv):
                v.wait_ge(pe.h, mmv)
                v.tensor_copy(ev2[:], psK[:])
                mark(v, vm)
            block.vector(vv2)

            def qv(q, kc=kc, vv=vm.v):
                q.wait_ge(vm.h, vv)
                q.dma_start(out=v_o[128 * kc:128 * (kc + 1), :],
                            in_=ev2[:]).then_inc(st.h, 16)
                st.inc(16)
            block.sync(qv)
            stv = st.v

            def vv3(v, sv=stv):
                v.wait_ge(st.h, sv)
                mark(v, vm)
            block.vector(vv3)

        ctxB.close()

        def qfin(q):
            q.wait_ge(st.h, st.v)
        block.sync(qfin)
        pr.finish = [(pe.h, pe.v), (vm.h, vm.v), (sm.h, sm.v), (st.h, st.v)]
        pr.ctx.pop_all()

    return pr


KVR_ = 512.0



def build_l1b(pr=None, io=None, gate=None):
    if pr is None:
        pr = Prog("l1b")
    nc = pr.nc
    kT_d = get_io(pr, io, "kT", [1024, S], BF16)
    krotT_d = get_io(pr, io, "krotT", [64, S], BF16)
    v_d = get_io(pr, io, "v", [S, 1024], BF16)
    qT_d = get_io(pr, io, "qT", [1536, S], BF16)
    opT_d = get_io(pr, io, "opT", [1024, D], F32)
    masks_d = get_io(pr, io, "masks", [4, 128, 512], F32)
    ones1 = get_io(pr, io, "ones1", [1, 128], BF16)
    onescol = get_io(pr, io, "onescol", [128, 1], BF16)

    op_o = get_io(pr, io, "op_o", [S, D], BF16, out=True)

    with pr.ctx:
        o1 = pr.sb("o1", [1, 128], BF16)
        ocol = pr.sb("ocol", [128, 1], BF16)
        maskb = pr.sb("maskb", [128, 4 * 512], F32)
        ob = pr.sb("ob", [128, 8 * 2048], BF16)
        krot = pr.sb("krot", [64, S], BF16)
        pvt = [pr.sb(f"pvt{i}", [128, 2048], BF16) for i in range(8)]

        pe = pr.sem("pe")
        vm = pr.sem("vm")
        sm = pr.sem("sm")
        st = pr.sem("st")
        s_const = Stream(pr, "cst", 1)
        s_kh = Stream(pr, "skh", 2)
        s_vh = Stream(pr, "svh", 2)
        s_qh = Stream(pr, "sqh", 2)

        if pr.block is None:
            pr.block = pr.ctx.enter_context(nc.Block())
        block = pr.block
        gate_all(block, gate)

        def qc(q):
            s_const.issue(q, 0, [
                (o1[:], ones1[:], {}),
                (ocol[:], onescol[:], {}),
                (maskb[:].rearrange("p (a c) -> p a c", c=512),
                 masks_d.rearrange("a b c -> b a c"), {}),
                (krot[:], krotT_d[:], {}),
            ])
        block.sync(qc)
        const_t = s_const.totals()[0]

        s_ob = Stream(pr, "sob", 1)

        def gob(g):
            s_ob.issue(g, 0, [(ob[:].rearrange("p (a c) -> p a c", c=2048),
                               opT_d.rearrange("(a p) c -> p a c", p=128), {})])
        block.gpsimd(gob)
        ob_t = s_ob.totals()[0]

        ctx1 = ExitStack()
        kh = [pr.sb(f"kh{i}", [128, S], BF16, ctx1) for i in range(2)]
        vh = [pr.sb(f"vh{i}", [128, 32 * 128], BF16, ctx1) for i in range(2)]
        qhn = [pr.sb(f"qhn{i}", [128, 2048], BF16, ctx1) for i in range(2)]
        qhr = [pr.sb(f"qhr{i}", [64, 2048], BF16, ctx1) for i in range(2)]
        pbuf = [pr.sb(f"pbuf{i}", [128, 512], BF16, ctx1) for i in range(2)]
        zrow = pr.sb("zrow", [1, 512], F32, ctx1)
        rz16 = pr.sb("rz16", [1, 512], BF16, ctx1)
        rzb = pr.sb("rzb", [128, 512], BF16, ctx1)
        onat = pr.sb("onat", [128, 2048], BF16, ctx1)
        psSb = [pr.ps(f"psS{i}", [128, 512], F32, ctx1) for i in range(2)]
        psZ = pr.ps("psZ", [128, 512], F32, ctx1)
        psP = pr.ps("psP", [128, 512], F32, ctx1)
        psO = pr.ps("psO", [128, 2048], F32, ctx1)

        for qhalf in range(2):
            for hl in range(8):
                def qh_(q, hl=hl, qhalf=qhalf, vv=vm.v):
                    q.wait_ge(vm.h, vv)
                    s_kh.issue(q, hl, [(kh[hl % 2][:],
                                        kT_d[128 * hl:128 * (hl + 1), :], {})])
                    s_vh.issue(q, hl, [(
                        vh[hl % 2][:].rearrange("p (a c) -> p a c", c=128),
                        v_d[:, 128 * hl:128 * (hl + 1)].rearrange(
                            "(a p) c -> p a c", p=128), {})])
                    s_qh.issue(q, hl, [
                        (qhn[hl % 2][:],
                         qT_d[192 * hl:192 * hl + 128,
                              2048 * qhalf:2048 * (qhalf + 1)], {}),
                        (qhr[hl % 2][0:64, :],
                         qT_d[192 * hl + 128:192 * hl + 192,
                              2048 * qhalf:2048 * (qhalf + 1)], {}),
                    ])
                block.sync(qh_)
                kh_t = (s_kh.sems[hl % 2].h, s_kh.sems[hl % 2].v)
                vh_t = (s_vh.sems[hl % 2].h, s_vh.sems[hl % 2].v)
                qh_t = (s_qh.sems[hl % 2].h, s_qh.sems[hl % 2].v)

                for g_ in range(4):
                    qg = qhalf * 4 + g_
                    nkt = 4 * qg + 4
                    exp_mark = {}
                    tz_mark = {}

                    def emit_ts(kt, g_=g_, qg=qg, hl=hl, kh_t=kh_t, qh_t=qh_t):
                        psS = psSb[kt % 2]

                        def ts_(t, kt=kt, psS=psS, vv=vm.v):
                            t.wait_ge(*kh_t)
                            t.wait_ge(*qh_t)
                            t.wait_ge(*const_t)
                            if kt >= 2:
                                t.wait_ge(sm.h, exp_mark[kt - 2])
                            elif kt == 0:
                                t.wait_ge(vm.h, vv)
                            _mm = t.matmul(psS[:], kh[hl % 2][:, 128 * kt:128 * (kt + 1)],
                                     qhn[hl % 2][:, 512 * g_:512 * (g_ + 1)],
                                     start=True, stop=False)
                            _mm = t.matmul(psS[:], krot[:, 128 * kt:128 * (kt + 1)],
                                     qhr[hl % 2][0:64, 512 * g_:512 * (g_ + 1)],
                                     start=False, stop=True)
                            tmark(t, pe, _mm)
                        block.tensor(ts_)
                        mmv = pe.v

                        diag_j = kt - 4 * qg
                        if diag_j >= 0:
                            def vmsk(v, j=diag_j, mmv=mmv, psS=psS):
                                v.wait_ge(pe.h, mmv)
                                v.tensor_tensor(psS[:], psS[:],
                                                maskb[:, 512 * j:512 * (j + 1)],
                                                OP.add)
                                mark(v, vm)
                            block.vector(vmsk)

                        def se_(s, kt=kt, mmv=mmv, masked=(diag_j >= 0),
                                vv=vm.v, psS=psS):
                            if masked:
                                s.wait_ge(vm.h, vv)
                            else:
                                s.wait_ge(pe.h, mmv)
                            if kt >= 2:
                                s.wait_ge(pe.h, tz_mark[kt - 2])
                            s.activation(pbuf[kt % 2][:], psS[:], AF.Exp)
                            mark(s, sm)
                        block.scalar(se_)
                        exp_mark[kt] = sm.v

                    def emit_tz(kt, nkt=nkt, hl=hl, vh_t=vh_t):
                        def tz(t, kt=kt):
                            t.wait_ge(sm.h, exp_mark[kt])
                            t.wait_ge(*vh_t)
                            _mm = t.matmul(psZ[0:1, :], ocol[:], pbuf[kt % 2][:],
                                     start=(kt == 0), stop=(kt == nkt - 1))
                            _mm = t.matmul(psP[:], vh[hl % 2][:, 128 * kt:128 * (kt + 1)],
                                     pbuf[kt % 2][:],
                                     start=(kt == 0), stop=(kt == nkt - 1))
                            tmark(t, pe, _mm)
                        block.tensor(tz)
                        tz_mark[kt] = pe.v

                    for kt in range(nkt):
                        emit_ts(kt)
                        if kt >= 1:
                            emit_tz(kt - 1)
                    emit_tz(nkt - 1)

                    def vz_(v, pv=pe.v):
                        v.wait_ge(pe.h, pv)
                        v.tensor_copy(zrow[:], psZ[0:1, :])
                        v.drain()
                        v.reciprocal(zrow[:], zrow[:])
                        v.drain()
                        v.tensor_copy(rz16[:], zrow[:])
                        mark(v, vm)
                    block.vector(vz_)

                    def trz(t, vv=vm.v):
                        t.wait_ge(vm.h, vv)
                        _mm = t.matmul(psSb[0][:], o1[:], rz16[:], start=True, stop=True)
                        tmark(t, pe, _mm)
                    block.tensor(trz)

                    def srz(s, pv=pe.v):
                        s.wait_ge(pe.h, pv)
                        s.copy(rzb[:], psSb[0][:])
                        mark(s, sm)
                    block.scalar(srz)

                    def vpv(v, hl=hl, g_=g_, sv=sm.v):
                        v.wait_ge(sm.h, sv)
                        v.tensor_tensor(pvt[hl][:, 512 * g_:512 * (g_ + 1)],
                                        psP[:], rzb[:], OP.mult)
                        mark(v, vm)
                    block.vector(vpv)

            pv_done = vm.v
            for qc_ in range(16):
                def to(t, qc_=qc_, qhalf=qhalf, pv=pv_done, obt=ob_t, vv=vm.v):
                    t.wait_ge(vm.h, pv)
                    t.wait_ge(*obt)
                    if qc_ > 0:
                        t.wait_ge(vm.h, vv)
                    for hl in range(8):
                        for tc in range(4):
                            _mm = t.matmul(psO[:, 512 * tc:512 * (tc + 1)],
                                     pvt[hl][:, 128 * qc_:128 * (qc_ + 1)],
                                     ob[:, 2048 * hl + 512 * tc:
                                        2048 * hl + 512 * (tc + 1)],
                                     start=(hl == 0), stop=(hl == 7))
                    tmark(t, pe, _mm)
                block.tensor(to)
                mmv = pe.v

                def vo(v, mmv=mmv):
                    v.wait_ge(pe.h, mmv)
                    v.tensor_copy(onat[:], psO[:])
                    mark(v, vm)
                block.vector(vo)

                def qo(q, qc_=qc_, qhalf=qhalf, vv=vm.v):
                    q.wait_ge(vm.h, vv)
                    q.dma_start(out=op_o[2048 * qhalf + 128 * qc_:
                                         2048 * qhalf + 128 * (qc_ + 1), :],
                                in_=onat[:]).then_inc(st.h, 16)
                    st.inc(16)
                block.sync(qo)
                stv = st.v

                def vo2(v, sv=stv):
                    v.wait_ge(st.h, sv)
                    mark(v, vm)
                block.vector(vo2)

        ctx1.close()

        def qfin(q):
            q.wait_ge(st.h, st.v)
        block.sync(qfin)
        pr.finish = [(pe.h, pe.v), (vm.h, vm.v), (sm.h, sm.v), (st.h, st.v)]
        pr.ctx.pop_all()

    return pr



def build_l1c(pr=None, io=None, gate=None):
    if pr is None:
        pr = Prog("l1c")
    nc = pr.nc
    summed = io is not None and "opb" not in io
    xn2T_d = get_io(pr, io, "xn2T", [D, T], BF16)
    x1T_d = get_io(pr, io, "x1T", [D, T], F32)
    opa_d = get_io(pr, io, "opa", [T, D], BF16)
    opb_d = None if summed else get_io(pr, io, "opb", [T, D], BF16)
    wgu_d = get_io(pr, io, "wgu", [8 * D, D], F32)
    wpr_d = get_io(pr, io, "wpr", [D, 4 * D], F32)
    ident_d = get_io(pr, io, "ident_d", [128, 128], BF16)
    ones1 = get_io(pr, io, "ones1", [1, 128], BF16)
    onescol = get_io(pr, io, "onescol", [128, 1], BF16)

    yT_o = get_io(pr, io, "yT_o", [D, T], F32, out=True)
    h_hbm = pr.dtmp("h_hbm", [4 * D, T], BF16)
    wqpT_hbm = pr.dtmp("wqpT_hbm", [4 * D, D], BF16)
    aT_hbm = pr.dtmp("aT_hbm", [D, T], BF16)

    NH = 4 * D
    with pr.ctx:
        o1 = pr.sb("o1", [1, 128], BF16)
        ocol = pr.sb("ocol", [128, 1], BF16)
        ident = pr.sb("ident", [128, 128], BF16)
        xn2T = [pr.sb(f"xn2T{i}", [128, T], BF16) for i in range(NDT)]
        ssq = pr.sb("ssq", [1, T], F32)
        rrow = pr.sb("rrow", [1, T], F32)
        rn16 = pr.sb("rn16", [1, T], BF16)
        RNb = pr.sb("RNb", [128, T], BF16)

        pe = pr.sem("pe")
        vm = pr.sem("vm")
        sm = pr.sem("sm")
        st = pr.sem("st")
        s_const = Stream(pr, "cst", 1)
        s_xn = Stream(pr, "sxn", 1)
        s_op = Stream(pr, "sop", 2)
        s_wg = Stream(pr, "swg", 4)
        s_hh = Stream(pr, "shh", 2)
        s_wp = Stream(pr, "swp", 2)
        s_x1 = Stream(pr, "sx1", 2)
        s_at = Stream(pr, "sat", 2)

        if pr.block is None:
            pr.block = pr.ctx.enter_context(nc.Block())
        block = pr.block
        gate_all(block, gate)

        def qc(q):
            s_const.issue(q, 0, [
                (o1[:], ones1[:], {}),
                (ocol[:], onescol[:], {}),
                (ident[:], ident_d[:], {}),
            ])
            for dt in range(NDT):
                s_xn.issue(q, 0, [(xn2T[dt][:],
                                   xn2T_d[128 * dt:128 * (dt + 1), :], {})])
        block.sync(qc)
        const_t = s_const.totals()[0]
        xn_t = s_xn.totals()[0]

        ctx0 = ExitStack()
        oa = [pr.sb(f"oa{i}", [128, D], BF16, ctx0) for i in range(2)]
        obb = [pr.sb(f"obb{i}", [128, D], BF16, ctx0) for i in range(2)]
        anat = pr.sb("anat", [128, D], BF16, ctx0)
        arow = [pr.sb(f"arow{i}", [128, 128], BF16, ctx0) for i in range(2)]
        psT = pr.ps("psT", [128, 128], BF16, ctx0)

        for tb in range(NDT):
            def qop(q, tb=tb, vv=vm.v):
                q.wait_ge(vm.h, vv)
                dmas = [(oa[tb % 2][:], opa_d[128 * tb:128 * (tb + 1), :], {})]
                if not summed:
                    dmas.append((obb[tb % 2][:],
                                 opb_d[128 * tb:128 * (tb + 1), :], {}))
                s_op.issue(q, tb, dmas)
            block.sync(qop)
            op_t = (s_op.sems[tb % 2].h, s_op.sems[tb % 2].v)

            def vad(v, tb=tb, ot=op_t):
                v.wait_ge(*ot)
                if summed:
                    v.tensor_copy(anat[:], oa[tb % 2][:])
                else:
                    v.tensor_tensor(anat[:], oa[tb % 2][:], obb[tb % 2][:], OP.add)
                mark(v, vm)
            block.vector(vad)
            ad_v = vm.v

            pe0, sm0 = pe.v, sm.v

            def tt_(t, tb=tb, vv=ad_v, ct=const_t, sm0=sm0):
                t.wait_ge(vm.h, vv)
                t.wait_ge(*ct)
                for dt in range(NDT):
                    if dt > 0:
                        t.wait_ge(sm.h, sm0 + dt)
                    t.transpose(psT[:], anat[:, 128 * dt:128 * (dt + 1)], ident[:]).then_inc(pe.h, 1)
            block.tensor(tt_)
            pe.inc(NDT)

            def sev(s, tb=tb, pe0=pe0):
                for dt in range(NDT):
                    s.wait_ge(pe.h, pe0 + dt + 1)
                    if dt >= 2:
                        s.wait_ge(s_at.sems[dt % 2].h, 16 * (8 * tb + dt // 2))
                    elif tb > 0:
                        s.wait_ge(s_at.sems[dt % 2].h, 16 * 8 * tb)
                    s.copy(arow[dt % 2][:], psT[:])
                    s.drain()
                    s.nop().then_inc(sm.h, 1)
            block.scalar(sev)
            sm.inc(NDT)

            def qat(q, tb=tb, sm0=sm0):
                for dt in range(NDT):
                    q.wait_ge(sm.h, sm0 + dt + 1)
                    s_at.issue(q, dt, [(aT_hbm[128 * dt:128 * (dt + 1),
                                               128 * tb:128 * (tb + 1)],
                                        arow[dt % 2][:], {})])
            block.sync(qat)

            def vnx(v, at=s_at.totals()):
                wait_all(v, at)
                mark(v, vm)
            block.vector(vnx)
        ctx0.close()
        at_done = vm.v

        ctx1 = ExitStack()
        wt = [pr.sb(f"wt{i}", [128, NH], BF16, ctx1) for i in range(2)]
        sums = pr.sb("sums", [128, 128], F32, ctx1)
        scal = pr.sb("scal", [128, 128], F32, ctx1)
        thr = pr.sb("thr", [128, 128], F32, ctx1)
        nthr = pr.sb("nthr", [128, 128], F32, ctx1)
        sc16 = pr.sb("sc16", [128, 128], BF16, ctx1)
        mm1 = pr.sb("mm1", [128, NH], BF16, ctx1)
        mm2 = pr.sb("mm2", [128, NH], BF16, ctx1)
        wqt = pr.sb("wqt", [128, NH], BF16, ctx1)
        tev = [pr.sb(f"tev{i}", [128, 128], BF16, ctx1) for i in range(2)]
        psT1 = pr.ps("psT1", [128, 128], BF16, ctx1)
        s_wq2 = Stream(pr, "swq2", 2)

        def quant_tile(v, wtile, ng):
            v.tensor_reduce(sums[:, 0:ng],
                            wtile[:].rearrange("p (g k) -> p g k", k=64),
                            axis=AXX, op=OP.add, apply_absolute_value=True)
            v.drain()
            v.tensor_scalar(scal[:, 0:ng], sums[:, 0:ng], 1.0 / 64, None, OP.mult)
            v.drain()
            v.tensor_copy(sc16[:, 0:ng], scal[:, 0:ng])
            v.drain()
            v.tensor_copy(scal[:, 0:ng], sc16[:, 0:ng])
            v.drain()
            v.tensor_scalar(scal[:, 0:ng], scal[:, 0:ng], 1e-8, None, OP.max)
            v.drain()
            v.tensor_scalar(thr[:, 0:ng], scal[:, 0:ng], 0.5, None, OP.mult)
            v.tensor_scalar(nthr[:, 0:ng], scal[:, 0:ng], -0.5, None, OP.mult)
            v.drain()
            v.tensor_tensor(mm1[:, 0:ng * 64].rearrange("p (g k) -> p g k", k=64),
                            wtile[:].rearrange("p (g k) -> p g k", k=64),
                            thr[:, 0:ng, None].broadcast_to((128, ng, 64)),
                            OP.is_gt)
            v.tensor_tensor(mm2[:, 0:ng * 64].rearrange("p (g k) -> p g k", k=64),
                            wtile[:].rearrange("p (g k) -> p g k", k=64),
                            nthr[:, 0:ng, None].broadcast_to((128, ng, 64)),
                            OP.is_lt)
            v.drain()
            v.tensor_tensor(mm1[:, 0:ng * 64], mm1[:, 0:ng * 64],
                            mm2[:, 0:ng * 64], OP.subtract)
            v.drain()
            v.tensor_tensor(wqt[:, 0:ng * 64].rearrange("p (g k) -> p g k", k=64),
                            mm1[:, 0:ng * 64].rearrange("p (g k) -> p g k", k=64),
                            scal[:, 0:ng, None].broadcast_to((128, ng, 64)),
                            OP.mult)

        tt2_mark = {}
        vq_mark = {}
        for wb_ in range(16):
            def gw(g, wb_=wb_, vv=vm.v):
                if wb_ >= 2:
                    g.wait_ge(vm.h, vq_mark[wb_ - 2])
                else:
                    g.wait_ge(vm.h, vv)
                s_wq2.issue(g, wb_, [(
                    wt[wb_ % 2][:], wpr_d[128 * wb_:128 * (wb_ + 1), :], {})])
            block.gpsimd(gw)
            w_tt = (s_wq2.sems[wb_ % 2].h, s_wq2.sems[wb_ % 2].v)

            def vq_(v, wb_=wb_, wtt=w_tt):
                v.wait_ge(*wtt)
                if wb_ >= 1:
                    v.wait_ge(pe.h, tt2_mark[wb_ - 1])
                quant_tile(v, wt[wb_ % 2], 128)
                v.drain()
                v.nop().then_inc(vm.h, 1)
                vm.inc()
            block.vector(vq_)
            q_v = vm.v
            vq_mark[wb_] = q_v

            pe0, sm0 = pe.v, sm.v

            def tt2(t, wb_=wb_, vv=q_v, ct=const_t, sm0=sm0):
                t.wait_ge(vm.h, vv)
                t.wait_ge(*ct)
                for hb in range(64):
                    if hb > 0:
                        t.wait_ge(sm.h, sm0 + hb)
                    t.transpose(psT1[:], wqt[:, 128 * hb:128 * (hb + 1)],
                                ident[:]).then_inc(pe.h, 1)
            block.tensor(tt2)
            pe.inc(64)
            tt2_mark[wb_] = pe.v

            def se2(s, wb_=wb_, pe0=pe0):
                for hb in range(64):
                    s.wait_ge(pe.h, pe0 + hb + 1)
                    if hb >= 2:
                        s.wait_ge(s_wp.sems[hb % 2].h, 16 * (32 * wb_ + hb // 2))
                    elif wb_ > 0:
                        s.wait_ge(s_wp.sems[hb % 2].h, 16 * 32 * wb_)
                    s.copy(tev[hb % 2][:], psT1[:])
                    s.drain()
                    s.nop().then_inc(sm.h, 1)
            block.scalar(se2)
            sm.inc(64)

            def qw2(q, wb_=wb_, sm0=sm0):
                for hb in range(64):
                    q.wait_ge(sm.h, sm0 + hb + 1)
                    s_wp.issue(q, hb, [(wqpT_hbm[128 * hb:128 * (hb + 1),
                                                 128 * wb_:128 * (wb_ + 1)],
                                        tev[hb % 2][:], {})])
            block.sync(qw2)

            def vnx2(v, at=s_wp.totals()):
                wait_all(v, at)
                mark(v, vm)
            block.vector(vnx2)
        wp_done = vm.v

        psG = pr.ps("psG", [128, 1024], F32, ctx1)
        psU2 = pr.ps("psU2", [128, 1024], F32, ctx1)
        psR = pr.ps("psR", [128, 512], F32, ctx1)
        wqg = pr.sb("wqg", [128, D], BF16, ctx1)
        wqu = pr.sb("wqu", [128, D], BF16, ctx1)
        wqgT = [pr.sb(f"wqgT{i}", [128, NDT * 128], BF16, ctx1) for i in range(2)]
        wquT = [pr.sb(f"wquT{i}", [128, NDT * 128], BF16, ctx1) for i in range(2)]
        sg = pr.sb("sg", [128, 1024], F32, ctx1)
        hj = pr.sb("hj", [128, T], BF16, ctx1)
        hsq = pr.sb("hsq", [128, T], BF16, ctx1)

        wtg = [wt[0][:, 0:D], wt[0][:, D:2 * D],
               wt[1][:, 0:D], wt[1][:, D:2 * D]]

        def vz0(v, wv=wp_done):
            v.wait_ge(vm.h, wv)
            v.memset(ssq[:], 0.0)
            mark(v, vm)
        block.vector(vz0)

        tt3_mark = {}
        vqg_mark = {}
        tg_mark = {}
        for jp in range(64):
            def gg(g, jp=jp, vv=vm.v):
                if jp >= 2:
                    g.wait_ge(vm.h, vqg_mark[jp - 2])
                else:
                    g.wait_ge(vm.h, vv)
                s_wg.issue(g, (jp % 2) * 2, [(wtg[(jp % 2) * 2],
                                   wgu_d[128 * jp:128 * (jp + 1), :], {})])
                s_wg.issue(g, (jp % 2) * 2 + 1, [(wtg[(jp % 2) * 2 + 1],
                                   wgu_d[128 * (64 + jp):128 * (64 + jp + 1), :],
                                   {})])
            block.gpsimd(gg)
            g_t = [(s_wg.sems[(jp % 2) * 2].h, s_wg.sems[(jp % 2) * 2].v),
                   (s_wg.sems[(jp % 2) * 2 + 1].h, s_wg.sems[(jp % 2) * 2 + 1].v)]

            def vqg(v, jp=jp, gt=g_t):
                v.wait_ge(*gt[0])
                if jp >= 1:
                    v.wait_ge(pe.h, tt3_mark[jp - 1])
                quant_tile(v, wtg[(jp % 2) * 2], 32)
                v.drain()
                v.tensor_copy(wqg[:], wqt[:, 0:D])
                v.drain()
                v.wait_ge(*gt[1])
                quant_tile(v, wtg[(jp % 2) * 2 + 1], 32)
                v.drain()
                v.tensor_copy(wqu[:], wqt[:, 0:D]).then_inc(vm.h, 1)
                vm.inc()
            block.vector(vqg)
            q_v = vm.v
            vqg_mark[jp] = q_v

            pe0, sm0 = pe.v, sm.v

            def tt3(t, jp=jp, vv=q_v, sm0=sm0):
                t.wait_ge(vm.h, vv)
                if jp >= 2:
                    t.wait_ge(pe.h, tg_mark[jp - 2])
                for z, src in enumerate([wqg, wqu]):
                    for dt in range(NDT):
                        idx = z * NDT + dt
                        if idx > 0:
                            t.wait_ge(sm.h, sm0 + idx)
                        t.transpose(psT1[:], src[:, 128 * dt:128 * (dt + 1)],
                                    ident[:]).then_inc(pe.h, 1)
            block.tensor(tt3)
            pe.inc(2 * NDT)
            tt3_mark[jp] = pe.v

            def se3(s, jp=jp, pe0=pe0):
                for z, dst in enumerate([wqgT[jp % 2], wquT[jp % 2]]):
                    for dt in range(NDT):
                        idx = z * NDT + dt
                        s.wait_ge(pe.h, pe0 + idx + 1)
                        s.copy(dst[:, 128 * dt:128 * (dt + 1)], psT1[:])
                        s.drain()
                        s.nop().then_inc(sm.h, 1)
            block.scalar(se3)
            sm.inc(2 * NDT)
            tr_done = sm.v

            for th in range(2):
                def tg(t, jp=jp, th=th, tv=tr_done, xt=xn_t, vv=vm.v):
                    t.wait_ge(sm.h, tv)
                    t.wait_ge(*xt)
                    if th > 0 or jp > 0:
                        t.wait_ge(vm.h, vv)
                    for dt in range(NDT):
                        for tc in range(2):
                            sl = slice(1024 * th + 512 * tc,
                                       1024 * th + 512 * (tc + 1))
                            _mm = t.matmul(psG[:, 512 * tc:512 * (tc + 1)],
                                           wqgT[jp % 2][:, 128 * dt:128 * dt + 128],
                                           xn2T[dt][:, sl],
                                           start=(dt == 0), stop=(dt == NDT - 1))
                            _mm = t.matmul(psU2[:, 512 * tc:512 * (tc + 1)],
                                           wquT[jp % 2][:, 128 * dt:128 * dt + 128],
                                           xn2T[dt][:, sl],
                                           start=(dt == 0), stop=(dt == NDT - 1))
                    tmark(t, pe, _mm)
                block.tensor(tg)
                mmv = pe.v
                tg_mark[jp] = mmv

                def sh(s, th=th, mmv=mmv):
                    s.wait_ge(pe.h, mmv)
                    s.activation(sg[:], psG[:], AF.Silu).then_inc(sm.h, 1)
                    sm.inc()
                block.scalar(sh)

                def vh_(v, th=th, sv=sm.v):
                    v.wait_ge(sm.h, sv)
                    v.tensor_tensor(hj[:, 1024 * th:1024 * (th + 1)], sg[:],
                                    psU2[:], OP.mult).then_inc(vm.h, 1)
                    vm.inc()
                block.vector(vh_)

                def sh2(s, th=th, vv=vm.v):
                    s.wait_ge(vm.h, vv)
                    s.activation(hsq[:, 1024 * th:1024 * (th + 1)],
                                 hj[:, 1024 * th:1024 * (th + 1)],
                                 AF.Square).then_inc(sm.h, 1)
                    sm.inc()
                block.scalar(sh2)

                def tsq2(t, th=th, sv=sm.v):
                    t.wait_ge(sm.h, sv)
                    for tc in range(2):
                        _mm = t.matmul(psR[0:1, 0:512], ocol[:],
                                       hsq[:, 1024 * th + 512 * tc:
                                           1024 * th + 512 * (tc + 1)],
                                       start=True, stop=True)
                        _mm.then_inc(pe.h, 1)
                        pe.inc()
                        t.wait_ge(vm.h, vm.v + tc + 1)
                block.tensor(tsq2)

                for tc in range(2):
                    def vsq2(v, th=th, tc=tc, pv=pe.v - 1 + tc):
                        v.wait_ge(pe.h, pv)
                        sl = slice(1024 * th + 512 * tc, 1024 * th + 512 * (tc + 1))
                        v.tensor_tensor(ssq[0:1, sl], ssq[0:1, sl],
                                        psR[0:1, 0:512], OP.add).then_inc(vm.h, 1)
                        vm.inc()
                    block.vector(vsq2)

            def qh2(q, jp=jp, vv=vm.v):
                q.wait_ge(vm.h, vv)
                s_hh.issue(q, jp, [(h_hbm[128 * jp:128 * (jp + 1), :], hj[:], {})])
            block.sync(qh2)
            hst = (s_hh.sems[jp % 2].h, s_hh.sems[jp % 2].v)

            def vhf(v, ht=hst):
                v.wait_ge(*ht)
                mark(v, vm)
            block.vector(vhf)
        gu_done = vm.v

        def vrr(v, gd=gu_done):
            v.wait_ge(vm.h, gd)
            v.tensor_scalar(rrow[:], ssq[:], 1.0 / NH, 1.1920929e-7,
                            OP.mult, OP.add)
            mark(v, vm)
        block.vector(vrr)

        def srr(s, vv=vm.v):
            s.wait_ge(vm.h, vv)
            s.activation(rrow[:], rrow[:], AF.Ln)
            s.drain()
            s.activation(rrow[:], rrow[:], AF.Exp, scale=-0.5)
            mark(s, sm)
        block.scalar(srr)

        def vrr2(v, sv=sm.v):
            v.wait_ge(sm.h, sv)
            v.tensor_copy(rn16[:], rrow[:])
            mark(v, vm)
        block.vector(vrr2)

        def trb2(t, vv=vm.v):
            t.wait_ge(vm.h, vv)
            for tc in range(4):
                _mm = t.matmul(psG[:, 0:512], o1[:], rn16[:, 512 * tc:512 * (tc + 1)],
                         start=True, stop=True)
                _mm.then_inc(pe.h, 1)
                pe.inc()
                t.wait_ge(sm.h, sm.v + tc + 1)
        block.tensor(trb2)

        for tc in range(4):
            def srb2(s, tc=tc, pv=pe.v - 3 + tc):
                s.wait_ge(pe.h, pv)
                s.copy(RNb[:, 512 * tc:512 * (tc + 1)], psG[:, 0:512])
                s.drain()
                s.nop().then_inc(sm.h, 1)
                sm.inc()
            block.scalar(srb2)
        rnb_done = sm.v

        ctx1.close()

        ctx2 = ExitStack()
        hn = [pr.sb(f"hn{i}", [128, 512], BF16, ctx2) for i in range(64)]
        wpb = [pr.sb(f"wpb{i}", [128, 64 * 128], BF16, ctx2) for i in range(2)]
        x1f = [pr.sb(f"x1f{i}", [128, 512], F32, ctx2) for i in range(2)]
        atb = [pr.sb(f"atb{i}", [128, 512], BF16, ctx2) for i in range(2)]
        fo = pr.sb("fo", [128, 512], F32, ctx2)
        psO = [pr.ps(f"psO{i}", [128, 512], F32, ctx2) for i in range(2)]
        s_hn = Stream(pr, "shn", 4)
        s_wpb = Stream(pr, "swpb", 2)

        for tcol in range(4):
            ts_ = slice(512 * tcol, 512 * (tcol + 1))
            for ht in range(64):
                def qhn(q, ht=ht, tcol=tcol, vv=vm.v, rd=rnb_done):
                    q.wait_ge(sm.h, rd)
                    if ht >= 4 or tcol > 0:
                        q.wait_ge(vm.h, vv)
                    s_hn.issue(q, ht, [(hn[ht][:],
                                        h_hbm[128 * ht:128 * (ht + 1), ts_], {})])
                block.sync(qhn)
                h_t = (s_hn.sems[ht % 4].h, s_hn.sems[ht % 4].v)

                def vhn(v, ht=ht, tcol=tcol, htt=h_t):
                    v.wait_ge(*htt)
                    v.tensor_tensor(hn[ht][:], hn[ht][:], RNb[:, ts_], OP.mult)
                    mark(v, vm)
                block.vector(vhn)
            hn_done = vm.v

            def gwp0(g, tcol=tcol, hd=hn_done):
                g.wait_ge(vm.h, hd)
                s_wpb.issue(g, 0, [(
                    wpb[0][:].rearrange("p (a c) -> p a c", c=128),
                    wqpT_hbm[:, 0:128].rearrange("(a p) c -> p a c", p=128), {})])
                s_wpb.issue(g, 1, [(
                    wpb[1][:].rearrange("p (a c) -> p a c", c=128),
                    wqpT_hbm[:, 128:256].rearrange("(a p) c -> p a c", p=128), {})])
            block.gpsimd(gwp0)
            wpb_t = {0: s_wpb.totals()[0], 1: s_wpb.totals()[1]}

            vf_mark = {}
            for dc in range(16):
                def tp(t, dc=dc, tcol=tcol, hd=hn_done, wt_=wpb_t[dc % 2], vv=vm.v):
                    t.wait_ge(vm.h, hd)
                    t.wait_ge(*wt_)
                    if dc >= 2:
                        t.wait_ge(vm.h, vf_mark[dc - 2])
                    for ht in range(64):
                        _mm = t.matmul(psO[dc % 2][:],
                                       wpb[dc % 2][:, 128 * ht:128 * (ht + 1)],
                                       hn[ht][:], start=(ht == 0), stop=(ht == 63))
                    tmark(t, pe, _mm)
                block.tensor(tp)
                mmv = pe.v
                if dc < 14:
                    def gwp(g, dc=dc, tcol=tcol, mmv=mmv):
                        g.wait_ge(pe.h, mmv)
                        tgt = s_wpb.issue(g, dc, [(
                            wpb[dc % 2][:].rearrange("p (a c) -> p a c", c=128),
                            wqpT_hbm[:, 128 * (dc + 2):128 * (dc + 3)].rearrange(
                                "(a p) c -> p a c", p=128), {})])
                        wpb_t[dc % 2] = tgt
                    block.gpsimd(gwp)

                def qx1(q, dc=dc, tcol=tcol, vv=vm.v):
                    q.wait_ge(vm.h, vv)
                    s_x1.issue(q, dc, [(x1f[dc % 2][:],
                                        x1T_d[128 * dc:128 * (dc + 1), ts_], {})])
                    s_at.issue(q, dc, [(atb[dc % 2][:],
                                        aT_hbm[128 * dc:128 * (dc + 1), ts_], {})])
                block.sync(qx1)
                x1_t = (s_x1.sems[dc % 2].h, s_x1.sems[dc % 2].v)
                at_t = (s_at.sems[dc % 2].h, s_at.sems[dc % 2].v)

                def vf(v, dc=dc, mmv=mmv, xt=x1_t, at=at_t):
                    v.wait_ge(pe.h, mmv)
                    v.wait_ge(*xt)
                    v.wait_ge(*at)
                    v.tensor_tensor(fo[:], psO[dc % 2][:], atb[dc % 2][:], OP.add)
                    v.drain()
                    v.tensor_tensor(fo[:], fo[:], x1f[dc % 2][:], OP.add)
                    mark(v, vm)
                block.vector(vf)

                def qy(q, dc=dc, tcol=tcol, vv=vm.v):
                    q.wait_ge(vm.h, vv)
                    q.dma_start(out=yT_o[128 * dc:128 * (dc + 1), ts_],
                                in_=fo[:]).then_inc(st.h, 16)
                    st.inc(16)
                block.sync(qy)
                stv = st.v

                def vf2(v, sv=stv):
                    v.wait_ge(st.h, sv)
                    mark(v, vm)
                block.vector(vf2)
                vf_mark[dc] = vm.v

        ctx2.close()

        def qfin(q):
            q.wait_ge(st.h, st.v)
        block.sync(qfin)
        pr.finish = [(pe.h, pe.v), (vm.h, vm.v), (sm.h, sm.v), (st.h, st.v)]
        pr.ctx.pop_all()

    return pr





RG_PAIR = [[0, 1], [2, 3], [4, 5], [6, 7]]


def build_fused(nphase=5, nocoll=False):
    pr = Prog("fz")
    nc = pr.nc
    io_sh = {
        "ones1": pr.din("ones1", [1, 128], BF16),
        "onescol": pr.din("onescol", [128, 1], BF16),
        "ident_d": pr.din("ident_d", [128, 128], BF16),
    }
    ins = {
        "xT": pr.din("xT", [D, T], F32),
        "winT": pr.din("winT", [D, 3 * D], F32),
        "binp": pr.din("binp", [128, 48], F32),
        "cosT": pr.din("cosT", [512, T], BF16),
        "sinT": pr.din("sinT", [512, T], BF16),
        "wq": pr.din("wq", [D, 512], F32),
        "keysT": pr.din("keysT", [8, 64, 128], F32),
        "iota64": pr.din("iota64", [128, 64], F32),
        "ecat": pr.din("ecat", [16384, 4096], F32),
        "woutT": pr.din("woutT", [D, D], F32),
        "bout_d": pr.din("bout_d", [128, 16], F32),
        "kvaT": pr.din("kvaT", [D, 576], F32),
        "lnw_d": pr.din("lnw_d", [128, 4], F32),
        "kvbn": pr.din("kvbn", [512, 2048], F32),
        "kvbv": pr.din("kvbv", [512, 2048], F32),
        "qpT": pr.din("qpT", [D, 1536], F32),
        "opT": pr.din("opT", [1024, D], F32),
        "masks": pr.din("masks", [4, 128, 512], F32),
        "wgu": pr.din("wgu", [8 * D, D], F32),
        "wpr": pr.din("wpr", [D, 4 * D], F32),
    }
    yT_o = pr.dout("yT_o", [D, T], F32)

    scr = {
        "xnT_s": pr.dtmp("xnT_s", [D, T], BF16),
        "qT_s": pr.dtmp("qT_s", [D, T], BF16),
        "apart_s": pr.dtmp("apart_s", [128, 16], F32),
        "apart_sum": pr.dtmp("apart_sum", [128, 16], F32),
        "fi_s": pr.dtmp("fi_s", [16, 128, 32], U32),
        "fs_s": pr.dtmp("fs_s", [16, 128, 32], F32),
        "x1T_s": pr.dtmp("x1T_s", [D, T], F32),
        "x1s2": pr.dtmp("x1s2", [D, T], F32),
        "xn2T_s": pr.dtmp("xn2T_s", [D, T], BF16),
        "xn2f_ag": pr.dtmp("xn2f_ag", [2 * D, T], BF16),
        "kT_s": pr.dtmp("kT_s", [2048, S], BF16),
        "krot_s": pr.dtmp("krot_s", [64, S], BF16),
        "v_s": pr.dtmp("v_s", [S, 2048], BF16),
        "q2T_s": pr.dtmp("q2T_s", [1536, S], BF16),
        "op_s": pr.dtmp("op_s", [S, D], BF16),
        "opsum_s": pr.dtmp("opsum_s", [T, D], BF16),
    }

    gbar = pr.sem("gbar")
    ccA = pr.sem("ccA")
    ccG = pr.sem("ccG")
    ccR = pr.sem("ccR")

    def phase(fn, io, coll=None):
        pr.pctx = ExitStack()
        fn(pr=pr, io=io, gate=(gbar.h, gbar.v) if gbar.v else None)
        pr.pctx.close()
        pr.pctx = None
        block = pr.block

        def qbar(q, fin=pr.finish):
            wait_all(q, fin)
            q.nop().then_inc(gbar.h, 1)
        block.sync(qbar)
        gbar.inc()

        if coll is not None and nocoll:
            coll = None
        if coll is not None:
            kind, op_, cin, cout, csem = coll

            def gcc(g, gv=gbar.v):
                g.wait_ge(gbar.h, gv)
                g.collective_compute(kind, op_, ins=[cin.opt()], outs=[cout.opt()],
                                     replica_groups=RG_PAIR).then_inc(csem.h, 1)
            block.gpsimd(gcc)
            csem.inc(1)

            def qcc(q, cs=(csem.h, csem.v)):
                q.wait_ge(*cs)
                q.nop().then_inc(gbar.h, 1)
            block.sync(qcc)
            gbar.inc()

    pr.prefix = "a_"
    phase(build_l0a,
          {**io_sh, **{k: ins[k] for k in ["xT", "winT", "binp", "cosT", "sinT",
                                           "wq", "keysT", "iota64"]},
           "xnT_o": scr["xnT_s"], "qT_o": scr["qT_s"], "apart_o": scr["apart_s"],
           "fi_o": scr["fi_s"], "fs_o": scr["fs_s"]},
          coll=("AllReduce", OP.add, scr["apart_s"], scr["apart_sum"], ccA))

    if nphase <= 1:
        return pr
    pr.prefix = "b_"
    phase(build_l0b,
          {**io_sh, "xT": ins["xT"], "xnT_i": scr["xnT_s"], "qT_i": scr["qT_s"],
           "ap0": scr["apart_sum"], "fi_i": scr["fi_s"], "fs_i": scr["fs_s"],
           "ecat": ins["ecat"], "woutT": ins["woutT"], "bout_d": ins["bout_d"],
           "x1T_o": scr["x1T_s"], "xn2T_o": scr["xn2T_s"], "x1s": scr["x1s2"]},
          coll=("AllGather", OP.bypass, scr["xn2T_s"], scr["xn2f_ag"], ccG))

    if nphase <= 2:
        return pr
    pr.prefix = "c_"
    phase(build_l1a,
          {**io_sh, "xn2f_ag": scr["xn2f_ag"],
           **{k: ins[k] for k in ["kvaT", "lnw_d", "kvbn", "kvbv", "qpT"]},
           "kT_o": scr["kT_s"], "krotT_o": scr["krot_s"], "v_o": scr["v_s"],
           "q2T_o": scr["q2T_s"]})

    if nphase <= 3:
        return pr
    pr.prefix = "d_"
    phase(build_l1b,
          {**io_sh, "kT": scr["kT_s"][0:1024, :], "krotT": scr["krot_s"],
           "v": scr["v_s"][:, 0:1024], "qT": scr["q2T_s"], "opT": ins["opT"],
           "masks": ins["masks"], "op_o": scr["op_s"]},
          coll=("ReduceScatter", OP.add, scr["op_s"], scr["opsum_s"], ccR))

    if nphase <= 4:
        return pr
    pr.prefix = "e_"
    phase(build_l1c,
          {**io_sh, "xn2T": scr["xn2T_s"], "x1T": scr["x1T_s"],
           "opa": scr["opsum_s"], "wgu": ins["wgu"], "wpr": ins["wpr"],
           "yT_o": yT_o})

    return pr


def fused_inmaps(prep):
    cos, sin = prep["cosT"], prep["sinT"]
    maps = []
    for c in range(NCORE):
        r = c % 2
        horder = list(range(r * 8, r * 8 + 8)) + list(range((1 - r) * 8,
                                                            (1 - r) * 8 + 8))
        cols = np.concatenate([np.arange(128) + 128 * h for h in horder])
        maps.append({
            "xT": prep["xT"][c],
            "winT": prep["winT"], "binp": prep["binp"],
            "cosT": np.ascontiguousarray(cos[:, r * T:(r + 1) * T]),
            "sinT": np.ascontiguousarray(sin[:, r * T:(r + 1) * T]),
            "wq": prep["wq"], "keysT": prep["keysT"], "iota64": prep["iota64"],
            "ecat": prep["e_cat"], "woutT": prep["woutT"], "bout_d": prep["bout"],
            "kvaT": prep["kvaT"], "lnw_d": prep["lnw"],
            "kvbn": np.ascontiguousarray(prep["kvbT_nope"][:, cols]),
            "kvbv": np.ascontiguousarray(prep["kvbT_v"][:, cols]),
            "qpT": np.ascontiguousarray(prep["qprojT"][:, r * 1536:(r + 1) * 1536]),
            "opT": np.ascontiguousarray(prep["oprojT"][r * 1024:(r + 1) * 1024]),
            "masks": prep["masks"],
            "wgu": prep["wgu"], "wpr": prep["wproj"],
            "ones1": prep["ones1"], "onescol": prep["onescol"],
            "ident_d": prep["ident"],
        })
    return maps


_PROGS = {}


def _prog(name):
    if name not in _PROGS:
        _PROGS[name] = {"l0a": build_l0a, "l0b": build_l0b, "l1a": build_l1a,
                        "l1b": build_l1b, "l1c": build_l1c}[name]()
    return _PROGS[name]


def _run(name, maps):
    pr = _prog(name)
    res = run_bass_kernel_spmd(pr.nc, maps, core_ids=list(range(NCORE)))
    return res.results


MODE = "fused"


def kernel(**inputs):
    prep = host_prep(inputs)
    if MODE == "fused":
        if "fused" not in _PROGS:
            _PROGS["fused"] = build_fused()
        pr = _PROGS["fused"]
        res = run_bass_kernel_spmd(pr.nc, fused_inmaps(prep),
                                   core_ids=list(range(NCORE)))
        out = np.empty((4, S, D), np.float32)
        for c in range(NCORE):
            b, r = c // 2, c % 2
            out[b, r * T:(r + 1) * T, :] = res.results[c]["yT_o"].T
        return out
    ra = _run("l0a", l0a_inmaps(prep))
    rb = _run("l0b", l0b_inmaps(prep, ra))

    maps = []
    for c in range(NCORE):
        base, r = (c // 2) * 2, c % 2
        maps.append({
            "xn2f": np.concatenate([rb[base]["xn2T_o"], rb[base + 1]["xn2T_o"]],
                                   axis=1),
            "kvaT": prep["kvaT"], "lnw_d": prep["lnw"],
            "kvbn": prep["kvbT_nope"], "kvbv": prep["kvbT_v"],
            "qpT": np.ascontiguousarray(prep["qprojT"][:, r * 1536:(r + 1) * 1536]),
            "ones1": prep["ones1"], "onescol": prep["onescol"]})
    r1a = _run("l1a", maps)

    maps = []
    for c in range(NCORE):
        r = c % 2
        maps.append({
            "kT": np.ascontiguousarray(r1a[c]["kT_o"][r * 1024:(r + 1) * 1024]),
            "krotT": r1a[c]["krotT_o"],
            "v": np.ascontiguousarray(r1a[c]["v_o"][:, r * 1024:(r + 1) * 1024]),
            "qT": r1a[c]["q2T_o"],
            "opT": np.ascontiguousarray(prep["oprojT"][r * 1024:(r + 1) * 1024]),
            "masks": prep["masks"],
            "ones1": prep["ones1"], "onescol": prep["onescol"]})
    r1b = _run("l1b", maps)

    maps = []
    for c in range(NCORE):
        r = c % 2
        part = c + 1 if r == 0 else c - 1
        maps.append({
            "xn2T": rb[c]["xn2T_o"], "x1T": rb[c]["x1T_o"],
            "opa": np.ascontiguousarray(r1b[c]["op_o"][r * T:(r + 1) * T]),
            "opb": np.ascontiguousarray(r1b[part]["op_o"][r * T:(r + 1) * T]),
            "wgu": prep["wgu"], "wpr": prep["wproj"],
            "ident_d": prep["ident"], "ones1": prep["ones1"],
            "onescol": prep["onescol"]})
    r1c = _run("l1c", maps)

    out = np.empty((4, S, D), np.float32)
    for c in range(NCORE):
        b, r = c // 2, c % 2
        out[b, r * T:(r + 1) * T, :] = r1c[c]["yT_o"].T
    return out


# version 10
# speedup vs baseline: 1.3844x; 1.0040x over previous
import numpy as np
import ml_dtypes as mld
from contextlib import ExitStack

import concourse.bass as bass
import concourse.mybir as mybir
from concourse.alu_op_type import AluOpType as OP
from concourse.bass_utils import run_bass_kernel_spmd

F32 = mybir.dt.float32
BF16 = mybir.dt.bfloat16
U32 = mybir.dt.uint32
AF = mybir.ActivationFunctionType
AXX = mybir.AxisListType.X
IOA = bass.IndirectOffsetOnAxis

NCORE = 8
D = 2048
T = 2048
S = 4096
NDT = D // 128
NTC = T // 512
ROT = 1024
NEG = -1e9
QSCALE = 192.0 ** -0.5



def _rope_perm(n):
    return np.concatenate([np.arange(0, ROT, 2), np.arange(1, ROT, 2),
                           np.arange(ROT, n)]).astype(np.int64)


def host_prep(inp):
    p = {}
    perm = _rope_perm(D)
    perm_all = np.concatenate([perm, perm + D, perm + 2 * D])
    W_in = np.asarray(inp["L0_attn_W_in"])
    p["winT"] = np.ascontiguousarray(W_in[perm_all].T)
    p["binp"] = np.ascontiguousarray(
        np.asarray(inp["L0_attn_b_in"])[perm_all].reshape(48, 128).T)
    W_out = np.asarray(inp["L0_attn_W_out"])
    p["woutT"] = np.ascontiguousarray(W_out[:, perm].T)
    p["bout"] = np.ascontiguousarray(
        np.asarray(inp["L0_attn_b_out"]).reshape(16, 128).T)
    p["wq"] = np.ascontiguousarray(np.asarray(inp["L0_ffn_Wq"]))
    keys = np.asarray(inp["L0_ffn_keys"])
    p["keysT"] = np.ascontiguousarray(
        keys.transpose(0, 2, 3, 1).reshape(8, 64, 128))
    p["e_cat"] = np.ascontiguousarray(
        np.concatenate([np.asarray(inp["L0_ffn_e_down"]),
                        np.asarray(inp["L0_ffn_e_up"])], axis=1))

    p["qprojT"] = np.ascontiguousarray(np.asarray(inp["L1_attn_q_proj_w"]).T)
    p["kvaT"] = np.ascontiguousarray(np.asarray(inp["L1_attn_kv_a_proj_w"]).T)
    p["lnw"] = np.ascontiguousarray(
        np.asarray(inp["L1_attn_kv_a_ln_w"]).reshape(4, 128).T)
    kvbT = np.asarray(inp["L1_attn_kv_b_proj_w"]).T
    nope_cols = (np.arange(16)[:, None] * 256 + np.arange(128)[None]).reshape(-1)
    p["kvbT_nope"] = np.ascontiguousarray(kvbT[:, nope_cols])
    p["kvbT_v"] = np.ascontiguousarray(kvbT[:, nope_cols + 128])
    p["oprojT"] = np.ascontiguousarray(np.asarray(inp["L1_attn_o_proj_w"]).T)
    p["wgu"] = np.ascontiguousarray(np.asarray(inp["L1_ffn_w_gate_up"]))
    p["wproj"] = np.ascontiguousarray(np.asarray(inp["L1_ffn_w_proj"]))

    X = np.asarray(inp["L0_attn_X"])
    p["xT"] = [np.ascontiguousarray(X[c // 2, (c % 2) * T:(c % 2 + 1) * T, :].T)
               for c in range(NCORE)]

    invf = 10000.0 ** (-(np.arange(512, dtype=np.float64) * 2.0 / ROT))
    ang = np.arange(S, dtype=np.float64)[None, :] * invf[:, None]
    p["cosT"] = np.cos(ang).astype(mld.bfloat16)
    p["sinT"] = np.sin(ang).astype(mld.bfloat16)
    p["ones1"] = np.ones((1, 128), dtype=mld.bfloat16)
    p["onescol"] = np.ones((128, 1), dtype=mld.bfloat16)
    p["ident"] = np.eye(128, dtype=np.float32).astype(mld.bfloat16)
    p["iota64"] = np.broadcast_to(np.arange(64, dtype=np.float32), (128, 64)).copy()
    masks = np.zeros((4, 128, 512), dtype=np.float32)
    for j in range(4):
        masks[j] = np.where(np.arange(128)[:, None] + 128 * j > np.arange(512)[None, :],
                            NEG, 0.0)
    p["masks"] = masks
    return p



class Sem:
    def __init__(self, nc, ctx, name):
        self.h = ctx.enter_context(nc.semaphore(name))
        self.v = 0

    def inc(self, n=1):
        self.v += n
        return self.v


class Prog:
    def __init__(self, name):
        self.nc = bass.Bass("TRN2", target_bir_lowering=False)
        self.name = name
        self.ctx = ExitStack()
        self.pctx = None
        self.prefix = ""
        self.block = None

    def sem(self, name):
        return Sem(self.nc, self.ctx, self.prefix + name)

    def sb(self, name, shape, dt, ctx=None):
        return (ctx or self.pctx or self.ctx).enter_context(
            self.nc.sbuf_tensor(self.prefix + name, list(shape), dt))

    def ps(self, name, shape, dt=F32, ctx=None):
        return (ctx or self.pctx or self.ctx).enter_context(
            self.nc.psum_tensor(self.prefix + name, list(shape), dt))

    def din(self, name, shape, dt):
        return self.nc.dram_tensor(name, list(shape), dt, kind="ExternalInput").ap()

    def dout(self, name, shape, dt):
        return self.nc.dram_tensor(name, list(shape), dt, kind="ExternalOutput").ap()

    def dtmp(self, name, shape, dt):
        return self.nc.dram_tensor(self.prefix + name, list(shape), dt,
                                   kind="Internal").ap()


def mark(eng, msem):
    eng.drain()
    eng.nop().then_inc(msem.h, 1)
    msem.inc()
    return msem.v


def tmark(t, msem, inst=None):
    (inst if inst is not None else t.nop()).then_inc(msem.h, 1)
    msem.inc()
    return msem.v


def get_io(pr, io, name, shape, dt, out=False):
    if io is not None and name in io:
        return io[name]
    return (pr.dout if out else pr.din)(name, list(shape), dt)


def gate_all(block, gate):
    if gate is None:
        return
    for emit in (block.sync, block.tensor, block.vector, block.scalar,
                 block.gpsimd):
        emit(lambda e, g=gate: e.wait_ge(g[0], g[1]))


class Stream:

    def __init__(self, pr, name, nslot):
        self.sems = [pr.sem(f"{name}{i}") for i in range(nslot)]
        self.n = nslot

    def issue(self, eng, slot, dmas):
        sem = self.sems[slot % self.n]
        for out, in_, kw in dmas:
            eng.dma_start(out=out, in_=in_, **kw).then_inc(sem.h, 16)
            sem.inc(16)
        return (sem.h, sem.v)

    def totals(self):
        return [(s.h, s.v) for s in self.sems]


def wait_all(eng, targets):
    for h, v in targets:
        if v > 0:
            eng.wait_ge(h, v)



def emit_ln(pr, b, src_dram, xnT, ms, eps=1e-5, tag="ln"):
    block = pr.block
    vm, sm, pe = ms["vm"], ms["sm"], ms["pe"]
    ctx = ExitStack()
    xf = [pr.sb(f"{tag}_xf{i}", [128, T], F32, ctx) for i in range(2)]
    xb = pr.sb(f"{tag}_xb", [128, T], BF16, ctx)
    xsq = pr.sb(f"{tag}_xsq", [128, T], BF16, ctx)
    m_row = pr.sb(f"{tag}_m", [1, T], F32, ctx)
    r_row = pr.sb(f"{tag}_r", [1, T], F32, ctx)
    sc_row = pr.sb(f"{tag}_sc", [1, T], F32, ctx)
    rb16 = pr.sb(f"{tag}_rb16", [1, T], BF16, ctx)
    mrb16 = pr.sb(f"{tag}_mrb16", [1, T], BF16, ctx)
    Rb = pr.sb(f"{tag}_Rb", [128, T], BF16, ctx)
    MRb = pr.sb(f"{tag}_MRb", [128, T], BF16, ctx)
    tmp = pr.sb(f"{tag}_tmp", [128, T], F32, ctx)
    psA = pr.ps(f"{tag}_psA", [128, 2048], F32, ctx)
    psB = pr.ps(f"{tag}_psB", [128, 2048], F32, ctx)
    s_x = Stream(pr, f"{tag}sx", 2)
    ocol, o1 = b["ocol"], b["o1"]

    cons = []
    for dt in range(NDT):
        def qf(q, dt=dt):
            if dt >= 2:
                q.wait_ge(vm.h, cons[dt - 2][0])
                q.wait_ge(sm.h, cons[dt - 2][1])
            s_x.issue(q, dt, [(xf[dt % 2][:],
                               src_dram[128 * dt:128 * (dt + 1), :], {})])
        block.sync(qf)
        lt = (s_x.sems[dt % 2].h, s_x.sems[dt % 2].v)

        def vf(v, dt=dt, lt=lt):
            v.wait_ge(*lt)
            if dt >= 1:
                v.wait_ge(pe.h, pe.v)
            v.tensor_copy(xb[:], xf[dt % 2][:])
            mark(v, vm)
        block.vector(vf)

        def sf(s, dt=dt, lt=lt):
            s.wait_ge(*lt)
            if dt >= 1:
                s.wait_ge(pe.h, pe.v)
            s.activation(xsq[:], xf[dt % 2][:], AF.Square)
            mark(s, sm)
        block.scalar(sf)
        cons.append((vm.v, sm.v))

        def tf(t, dt=dt, vv=vm.v, sv=sm.v):
            t.wait_ge(vm.h, vv)
            t.wait_ge(sm.h, sv)
            for tc in range(NTC):
                _mm = t.matmul(psA[0:1, 512 * tc:512 * (tc + 1)], ocol[:],
                         xb[:, 512 * tc:512 * (tc + 1)],
                         start=(dt == 0), stop=(dt == NDT - 1))
                _mm = t.matmul(psB[0:1, 512 * tc:512 * (tc + 1)], ocol[:],
                         xsq[:, 512 * tc:512 * (tc + 1)],
                         start=(dt == 0), stop=(dt == NDT - 1))
            tmark(t, pe, _mm)
        block.tensor(tf)

    def vrows(v, pv=pe.v):
        v.wait_ge(pe.h, pv)
        v.tensor_scalar(m_row[:], psA[0:1, :], 1.0 / D, None, OP.mult)
        v.tensor_scalar(sc_row[:], psB[0:1, :], 1.0 / D, None, OP.mult)
        v.drain()
        v.tensor_tensor(r_row[:], m_row[:], m_row[:], OP.mult)
        v.drain()
        v.tensor_tensor(sc_row[:], sc_row[:], r_row[:], OP.subtract)
        v.drain()
        v.tensor_scalar(sc_row[:], sc_row[:], eps, None, OP.add)
        mark(v, vm)
    block.vector(vrows)

    def srows(s, vv=vm.v):
        s.wait_ge(vm.h, vv)
        s.activation(r_row[:], sc_row[:], AF.Ln)
        s.drain()
        s.activation(r_row[:], r_row[:], AF.Exp, scale=-0.5)
        mark(s, sm)
    block.scalar(srows)

    def vmr(v, sv=sm.v):
        v.wait_ge(sm.h, sv)
        v.tensor_tensor(sc_row[:], m_row[:], r_row[:], OP.mult)
        v.drain()
        v.tensor_copy(rb16[:], r_row[:])
        v.tensor_copy(mrb16[:], sc_row[:])
        mark(v, vm)
    block.vector(vmr)

    def tbc(t, vv=vm.v):
        t.wait_ge(vm.h, vv)
        for tc in range(NTC):
            _mm = t.matmul(psA[:, 512 * tc:512 * (tc + 1)], o1[:],
                     rb16[:, 512 * tc:512 * (tc + 1)], start=True, stop=True)
            _mm = t.matmul(psB[:, 512 * tc:512 * (tc + 1)], o1[:],
                     mrb16[:, 512 * tc:512 * (tc + 1)], start=True, stop=True)
        tmark(t, pe, _mm)
    block.tensor(tbc)

    def vbc(v, pv=pe.v):
        v.wait_ge(pe.h, pv)
        v.tensor_copy(Rb[:], psA[:])
        v.tensor_copy(MRb[:], psB[:])
        mark(v, vm)
    block.vector(vbc)
    bc = vm.v

    marks = []
    for dt in range(NDT):
        def qf2(q, dt=dt):
            q.wait_ge(vm.h, bc if dt < 2 else marks[dt - 2])
            s_x.issue(q, dt, [(xf[dt % 2][:],
                               src_dram[128 * dt:128 * (dt + 1), :], {})])
        block.sync(qf2)
        lt = (s_x.sems[dt % 2].h, s_x.sems[dt % 2].v)

        def vf2(v, dt=dt, lt=lt):
            v.wait_ge(*lt)
            v.tensor_tensor(tmp[:], xf[dt % 2][:], Rb[:], OP.mult)
            v.drain()
            v.tensor_tensor(xnT[dt][:], tmp[:], MRb[:], OP.subtract)
            mark(v, vm)
        block.vector(vf2)
        marks.append(vm.v)

    ctx.close()
    return vm.v



def build_l0a(cut="all", pr=None, io=None, gate=None):
    if pr is None:
        pr = Prog("l0a")
    nc = pr.nc
    xT = get_io(pr, io, "xT", [D, T], F32)
    winT = get_io(pr, io, "winT", [D, 3 * D], F32)
    binp = get_io(pr, io, "binp", [128, 48], F32)
    cosT_d = get_io(pr, io, "cosT", [512, T], BF16)
    sinT_d = get_io(pr, io, "sinT", [512, T], BF16)
    wq_d = get_io(pr, io, "wq", [D, 512], F32)
    keysT = get_io(pr, io, "keysT", [8, 64, 128], F32)
    ones1 = get_io(pr, io, "ones1", [1, 128], BF16)
    onescol = get_io(pr, io, "onescol", [128, 1], BF16)
    iota64 = get_io(pr, io, "iota64", [128, 64], F32)

    xnT_o = get_io(pr, io, "xnT_o", [D, T], BF16, out=True)
    qT_o = get_io(pr, io, "qT_o", [D, T], BF16, out=True)
    apart_o = get_io(pr, io, "apart_o", [128, 16], F32, out=True)
    fi_o = get_io(pr, io, "fi_o", [16, 128, 32], U32, out=True)
    fs_o = get_io(pr, io, "fs_o", [16, 128, 32], F32, out=True)

    with pr.ctx:
        xnT = [pr.sb(f"xnT{i}", [128, T], BF16) for i in range(NDT)]
        o1 = pr.sb("o1", [1, 128], BF16)
        ocol = pr.sb("ocol", [128, 1], BF16)
        bcol = pr.sb("bcol", [128, 48], F32)
        acol = pr.sb("acol", [128, 16], F32)

        pe = pr.sem("pe")
        vm = pr.sem("vm")
        sm = pr.sem("sm")
        st = pr.sem("st")
        ms = {"pe": pe, "vm": vm, "sm": sm, "st": st}
        s_const = Stream(pr, "cst", 1)
        s_w = Stream(pr, "sw", 2)
        s_cs = Stream(pr, "scs", 2)
        s_wq = Stream(pr, "swq", 1)
        s_qst = Stream(pr, "sqst", 1)
        s_tk = Stream(pr, "stk", 1)

        if pr.block is None:
            pr.block = pr.ctx.enter_context(nc.Block())
        block = pr.block
        gate_all(block, gate)
        b = {"o1": o1, "ocol": ocol}

        def qconst(q):
            s_const.issue(q, 0, [
                (o1[:], ones1[:], {}),
                (ocol[:], onescol[:], {}),
                (bcol[:], binp[:], {}),
            ])
        block.sync(qconst)
        const_t = s_const.totals()[0]

        def vconst(v):
            v.wait_ge(*const_t)
            mark(v, vm)
        block.vector(vconst)

        emit_ln(pr, b, xT, xnT, ms)
        xn_done = vm.v

        def qxn(q):
            q.wait_ge(vm.h, xn_done)
            for dt in range(NDT):
                q.dma_start(out=xnT_o[128 * dt:128 * (dt + 1), :],
                            in_=xnT[dt][:]).then_inc(st.h, 16)
                st.inc(16)
        block.sync(qxn)

        if cut == "ln":
            def qfin0(q):
                q.wait_ge(st.h, st.v)
            block.sync(qfin0)
            pr.ctx.pop_all()
            return pr

        ctx2 = ExitStack()
        ktT = [pr.sb(f"ktT{i}", [128, T], BF16, ctx2) for i in range(NDT)]
        cosb = [pr.sb(f"cosb{i}", [128, T], BF16, ctx2) for i in range(2)]
        sinb = [pr.sb(f"sinb{i}", [128, T], BF16, ctx2) for i in range(2)]
        p1 = pr.sb("p1", [128, T], F32, ctx2)
        p2 = pr.sb("p2", [128, T], F32, ctx2)
        sqt = pr.sb("sqt", [128, T], BF16, ctx2)
        vtile = pr.sb("vtile", [128, T], BF16, ctx2)
        fkt = pr.sb("fkt", [128, T], BF16, ctx2)
        rrow = pr.sb("rrow", [1, T], F32, ctx2)
        rn16 = pr.sb("rn16", [1, T], BF16, ctx2)
        RNb = pr.sb("RNb", [128, T], BF16, ctx2)
        scr = pr.sb("scr", [128, T], BF16, ctx2)
        qtile = pr.sb("qtile", [128, T], BF16, ctx2)
        wbuf = [pr.sb(f"wb{i}", [128, NDT * 128], BF16, ctx2) for i in range(2)]
        psA = pr.ps("psA", [128, 2048], F32, ctx2)
        psB = pr.ps("psB", [128, 2048], F32, ctx2)

        cs_t = {}

        def csload(q, i, buf):
            cs_t[i] = s_cs.issue(q, buf, [
                (cosb[buf][:], cosT_d[128 * i:128 * (i + 1), :], {}),
                (sinb[buf][:], sinT_d[128 * i:128 * (i + 1), :], {}),
            ])

        def qcs(q, xv=xn_done):
            q.wait_ge(vm.h, xv)
            csload(q, 0, 0)
            csload(q, 1, 1)
        block.sync(qcs)

        w_t = {}

        def wload(g, e, buf):
            w_t[e] = s_w.issue(g, buf, [(
                wbuf[buf][:].rearrange("p (dt c) -> p dt c", c=128),
                winT[:, 128 * e:128 * (e + 1)].rearrange("(dt p) c -> p dt c", p=128),
                {})])

        def mm_echunk(t, e, psX, buf):
            t.wait_ge(*w_t[e])
            for dt in range(NDT):
                for tc in range(NTC):
                    _mm = t.matmul(psX[:, 512 * tc:512 * (tc + 1)],
                             wbuf[buf][:, dt * 128:dt * 128 + 128],
                             xnT[dt][:, 512 * tc:512 * (tc + 1)],
                             start=(dt == 0), stop=(dt == NDT - 1))
            return tmark(t, pe, _mm)

        def gk0(g, xv=xn_done):
            g.wait_ge(vm.h, xv)
            wload(g, 16, 0)
            wload(g, 20, 1)
        block.gpsimd(gk0)

        for i in range(4):
            def tk(t, i=i):
                if i > 0:
                    t.wait_ge(vm.h, vm.v)
                mm_echunk(t, 16 + i, psA, 0)
                mm_echunk(t, 20 + i, psB, 1)
            block.tensor(tk)
            pair_mm = pe.v

            if i < 3:
                def gk(g, i=i, pm=pair_mm):
                    g.wait_ge(pe.h, pm)
                    wload(g, 16 + i + 1, 0)
                    wload(g, 20 + i + 1, 1)
                block.gpsimd(gk)

            def vk(v, i=i, pm=pair_mm):
                v.wait_ge(pe.h, pm)
                v.wait_ge(*cs_t[i])
                cb, sb_ = cosb[i % 2], sinb[i % 2]
                v.scalar_tensor_tensor(p1[:], psA[:], bcol[:, 16 + i:17 + i], cb[:],
                                       OP.add, OP.mult)
                v.scalar_tensor_tensor(p2[:], psB[:], bcol[:, 20 + i:21 + i], sb_[:],
                                       OP.add, OP.mult)
                v.drain()
                v.tensor_tensor(ktT[i][:], p1[:], p2[:], OP.subtract)
                v.drain()
                v.scalar_tensor_tensor(p1[:], psB[:], bcol[:, 20 + i:21 + i], cb[:],
                                       OP.add, OP.mult)
                v.scalar_tensor_tensor(p2[:], psA[:], bcol[:, 16 + i:17 + i], sb_[:],
                                       OP.add, OP.mult)
                v.drain()
                v.tensor_tensor(ktT[4 + i][:], p1[:], p2[:], OP.add)
                mark(v, vm)
            block.vector(vk)

            if 0 < i < 3:
                def qcs2(q, i=i, vv=vm.v):
                    q.wait_ge(vm.h, vv)
                    csload(q, i + 1, (i + 1) % 2)
                block.sync(qcs2)

        if cut == "kpair":
            def qfin2(q, vv=vm.v):
                q.wait_ge(vm.h, vv)
                q.wait_ge(st.h, st.v)
            block.sync(qfin2)
            pr.ctx.pop_all()
            return pr

        def gkp(g, pv=pe.v):
            g.wait_ge(pe.h, pv)
            wload(g, 24, 0)
            wload(g, 25, 1)
        block.gpsimd(gkp)

        for j in range(8):
            def tkp(t, j=j, vv=vm.v):
                t.wait_ge(vm.h, vv if j == 0 else vm.v)
                mm_echunk(t, 24 + j, psA if j % 2 == 0 else psB, j % 2)
            block.tensor(tkp)
            mmv = pe.v
            if j < 6:
                def gkp2(g, j=j, mmv=mmv):
                    g.wait_ge(pe.h, mmv)
                    wload(g, 26 + j, j % 2)
                block.gpsimd(gkp2)

            def vkp(v, j=j, mmv=mmv):
                v.wait_ge(pe.h, mmv)
                v.tensor_scalar(ktT[8 + j][:], (psA if j % 2 == 0 else psB)[:],
                                bcol[:, 24 + j:25 + j], None, OP.add)
                mark(v, vm)
            block.vector(vkp)
        k_done = vm.v

        for dt in range(NDT):
            def sq(s, dt=dt, kd=k_done):
                s.wait_ge(vm.h, kd)
                if dt > 0:
                    s.wait_ge(pe.h, pe.v)
                s.activation(sqt[:], ktT[dt][:], AF.Square)
                mark(s, sm)
            block.scalar(sq)

            def tn(t, dt=dt, sv=sm.v):
                t.wait_ge(sm.h, sv)
                for tc in range(NTC):
                    _mm = t.matmul(psA[0:1, 512 * tc:512 * (tc + 1)], ocol[:],
                             sqt[:, 512 * tc:512 * (tc + 1)],
                             start=(dt == 0), stop=(dt == NDT - 1))
                tmark(t, pe, _mm)
            block.tensor(tn)

        def vnorm(v, pv=pe.v):
            v.wait_ge(pe.h, pv)
            v.tensor_scalar(rrow[:], psA[0:1, :], 1.0, 1e-24, OP.mult, OP.max)
            mark(v, vm)
        block.vector(vnorm)

        def snorm(s, vv=vm.v):
            s.wait_ge(vm.h, vv)
            s.activation(rrow[:], rrow[:], AF.Ln)
            s.drain()
            s.activation(rrow[:], rrow[:], AF.Exp, scale=-0.5)
            mark(s, sm)
        block.scalar(snorm)

        def vnorm2(v, sv=sm.v):
            v.wait_ge(sm.h, sv)
            v.tensor_scalar(rrow[:], rrow[:], 1e12, None, OP.min)
            v.drain()
            v.tensor_copy(rn16[:], rrow[:])
            mark(v, vm)
        block.vector(vnorm2)

        def trb(t, vv=vm.v):
            t.wait_ge(vm.h, vv)
            for tc in range(NTC):
                _mm = t.matmul(psA[:, 512 * tc:512 * (tc + 1)], o1[:],
                         rn16[:, 512 * tc:512 * (tc + 1)], start=True, stop=True)
            tmark(t, pe, _mm)
        block.tensor(trb)

        def vrb(v, pv=pe.v):
            v.wait_ge(pe.h, pv)
            v.tensor_copy(RNb[:], psA[:])
            v.memset(acol[:], 0.0)
            mark(v, vm)
        block.vector(vrb)
        rn_done = vm.v

        def gv0(g, pv=pe.v):
            g.wait_ge(pe.h, pv)
            wload(g, 32, 0)
            wload(g, 33, 1)
        block.gpsimd(gv0)

        for j in range(NDT):
            def tv(t, j=j, rd=rn_done):
                t.wait_ge(vm.h, rd if j == 0 else vm.v)
                mm_echunk(t, 32 + j, psB, j % 2)
            block.tensor(tv)
            mmv = pe.v
            if j < NDT - 2:
                def gv(g, j=j, mmv=mmv):
                    g.wait_ge(pe.h, mmv)
                    wload(g, 34 + j, j % 2)
                block.gpsimd(gv)

            def vv(v, j=j, mmv=mmv):
                v.wait_ge(pe.h, mmv)
                v.tensor_scalar(vtile[:], psB[:], bcol[:, 32 + j:33 + j], None, OP.add)
                v.drain()
                v.tensor_tensor(fkt[:], ktT[j][:], RNb[:], OP.mult)
                v.drain()
                v.scalar_tensor_tensor(scr[:], fkt[:], 1.0, vtile[:],
                                       OP.mult, OP.mult,
                                       accum_out=acol[:, j:j + 1])
                mark(v, vm)
            block.vector(vv)
        a_done = vm.v

        def qa(q):
            q.wait_ge(vm.h, a_done)
            q.dma_start(out=apart_o[:], in_=acol[:]).then_inc(st.h, 16)
            st.inc(16)
        block.sync(qa)

        def gq0(g, pv=pe.v):
            g.wait_ge(pe.h, pv)
            wload(g, 0, 0)
            wload(g, 4, 1)
        block.gpsimd(gq0)

        def qcsq(q, av=a_done):
            q.wait_ge(vm.h, av)
            csload(q, 0, 0)
            csload(q, 1, 1)
        block.sync(qcsq)

        for i in range(4):
            def tq(t, i=i, av=a_done):
                t.wait_ge(vm.h, av if i == 0 else vm.v)
                mm_echunk(t, i, psA, 0)
                mm_echunk(t, 4 + i, psB, 1)
            block.tensor(tq)
            pair_mm = pe.v
            if i < 3:
                def gq(g, i=i, pm=pair_mm):
                    g.wait_ge(pe.h, pm)
                    wload(g, i + 1, 0)
                    wload(g, 4 + i + 1, 1)
                block.gpsimd(gq)

            def vq(v, i=i, pm=pair_mm):
                v.wait_ge(pe.h, pm)
                v.wait_ge(*cs_t[i])
                cb, sb_ = cosb[i % 2], sinb[i % 2]
                v.scalar_tensor_tensor(p1[:], psA[:], bcol[:, i:i + 1], cb[:],
                                       OP.add, OP.mult)
                v.scalar_tensor_tensor(p2[:], psB[:], bcol[:, 4 + i:5 + i], sb_[:],
                                       OP.add, OP.mult)
                v.drain()
                v.tensor_tensor(qtile[:], p1[:], p2[:], OP.subtract)
                mark(v, vm)
            block.vector(vq)

            def qq(q, i=i, ev=vm.v):
                q.wait_ge(vm.h, ev)
                s_qst.issue(q, 0, [(qT_o[128 * i:128 * (i + 1), :], qtile[:], {})])
            block.sync(qq)
            qst_t = s_qst.totals()[0]

            def vq2(v, i=i, qt=qst_t):
                v.wait_ge(*qt)
                cb, sb_ = cosb[i % 2], sinb[i % 2]
                v.scalar_tensor_tensor(p1[:], psB[:], bcol[:, 4 + i:5 + i], cb[:],
                                       OP.add, OP.mult)
                v.scalar_tensor_tensor(p2[:], psA[:], bcol[:, i:i + 1], sb_[:],
                                       OP.add, OP.mult)
                v.drain()
                v.tensor_tensor(qtile[:], p1[:], p2[:], OP.add)
                mark(v, vm)
            block.vector(vq2)

            def qq2(q, i=i, ev=vm.v):
                q.wait_ge(vm.h, ev)
                s_qst.issue(q, 0, [(qT_o[128 * (4 + i):128 * (5 + i), :], qtile[:], {})])
            block.sync(qq2)
            qst_t2 = s_qst.totals()[0]

            def vq3(v, qt=qst_t2):
                v.wait_ge(*qt)
                mark(v, vm)
            block.vector(vq3)

            if 0 < i < 3:
                def qcsq2(q, i=i, vv=vm.v):
                    q.wait_ge(vm.h, vv)
                    csload(q, i + 1, (i + 1) % 2)
                block.sync(qcsq2)

        def gqp(g, pv=pe.v):
            g.wait_ge(pe.h, pv)
            wload(g, 8, 0)
            wload(g, 9, 1)
        block.gpsimd(gqp)

        for j in range(8):
            def tqp(t, j=j):
                t.wait_ge(vm.h, vm.v)
                mm_echunk(t, 8 + j, psA if j % 2 == 0 else psB, j % 2)
            block.tensor(tqp)
            mmv = pe.v
            if j < 6:
                def gqp2(g, j=j, mmv=mmv):
                    g.wait_ge(pe.h, mmv)
                    wload(g, 10 + j, j % 2)
                block.gpsimd(gqp2)

            def vqp(v, j=j, mmv=mmv):
                v.wait_ge(pe.h, mmv)
                v.tensor_scalar(qtile[:], (psA if j % 2 == 0 else psB)[:],
                                bcol[:, 8 + j:9 + j], None, OP.add)
                mark(v, vm)
            block.vector(vqp)

            def qqp(q, j=j, ev=vm.v):
                q.wait_ge(vm.h, ev)
                s_qst.issue(q, 0, [(qT_o[128 * (8 + j):128 * (9 + j), :], qtile[:], {})])
            block.sync(qqp)
            qst_t = s_qst.totals()[0]

            def vqp2(v, qt=qst_t):
                v.wait_ge(*qt)
                mark(v, vm)
            block.vector(vqp2)

        ctx2.close()
        qkv_end = vm.v

        if cut == "qkv":
            def qfin1(q):
                q.wait_ge(vm.h, qkv_end)
                q.wait_ge(st.h, st.v)
                wait_all(q, s_qst.totals())
            block.sync(qfin1)
            pr.ctx.pop_all()
            return pr

        ctx3 = ExitStack()
        qfT = [pr.sb(f"qfT{i}", [128, T], BF16, ctx3) for i in range(4)]
        wqb = pr.sb("wqb", [128, NDT * 512], BF16, ctx3)
        keysb = pr.sb("keysb", [128, 8 * 128], BF16, ctx3)
        iob = pr.sb("iob", [128, 64], F32, ctx3)
        s1 = pr.sb("s1", [128, 8], F32, ctx3)
        s2 = pr.sb("s2", [128, 8], F32, ctx3)
        i1 = pr.sb("i1", [128, 8], U32, ctx3)
        i2 = pr.sb("i2", [128, 8], U32, ctx3)
        i1f = pr.sb("i1f", [128, 8], F32, ctx3)
        i2f = pr.sb("i2f", [128, 8], F32, ctx3)
        i1m = pr.sb("i1m", [128, 8], F32, ctx3)
        asc = pr.sb("asc", [128, 64], F32, ctx3)
        aidx = pr.sb("aidx", [128, 64], F32, ctx3)
        fs_t = pr.sb("fs_t", [128, 32], F32, ctx3)
        fsh = pr.sb("fsh", [128, 8], F32, ctx3)
        pk_t = pr.sb("pk_t", [128, 8], U32, ctx3)
        pkf = pr.sb("pkf", [128, 8], F32, ctx3)
        fi_f = pr.sb("fi_f", [128, 32], F32, ctx3)
        fi_u = pr.sb("fi_u", [128, 32], U32, ctx3)
        eqm = pr.sb("eqm", [128, 64], F32, ctx3)
        prodm = pr.sb("prodm", [128, 64], F32, ctx3)
        psF = pr.ps("psF", [128, 2048], F32, ctx3)
        psS = pr.ps("psS", [128, 2048], F32, ctx3)

        def gw(g, vv=qkv_end):
            g.wait_ge(vm.h, vv)
            s_wq.issue(g, 0, [
                (wqb[:].rearrange("p (dt c) -> p dt c", c=512),
                 wq_d.rearrange("(dt p) c -> p dt c", p=128), {}),
                (keysb[0:64, :].rearrange("p (a c) -> p a c", c=128),
                 keysT.rearrange("a b c -> b a c"), {}),
                (keysb[64:128, :].rearrange("p (a c) -> p a c", c=128),
                 keysT.rearrange("a b c -> b a c"), {}),
                (iob[:], iota64[:], {}),
            ])
        block.gpsimd(gw)
        wq_t = s_wq.totals()[0]

        for cc in range(4):
            def tqf(t, cc=cc, wt=wq_t, vv=qkv_end):
                t.wait_ge(*wt)
                t.wait_ge(vm.h, vv)
                if cc > 0:
                    t.wait_ge(sm.h, sm.v)
                for dt in range(NDT):
                    for tc in range(NTC):
                        _mm = t.matmul(psF[:, 512 * tc:512 * (tc + 1)],
                                 wqb[:, dt * 512 + cc * 128:dt * 512 + cc * 128 + 128],
                                 xnT[dt][:, 512 * tc:512 * (tc + 1)],
                                 start=(dt == 0), stop=(dt == NDT - 1))
                tmark(t, pe, _mm)
            block.tensor(tqf)

            def sqf(s, cc=cc, pv=pe.v):
                s.wait_ge(pe.h, pv)
                s.copy(qfT[cc][:], psF[:])
                mark(s, sm)
            block.scalar(sqf)
        qf_done = sm.v

        if cut == "qf":
            def qfinq(q):
                q.wait_ge(sm.h, qf_done)
                q.wait_ge(st.h, st.v)
            block.sync(qfinq)
            pr.ctx.pop_all()
            return pr

        for tb in range(NDT):
            def tsim(t, tb=tb, qd=qf_done, wt=wq_t):
                t.wait_ge(sm.h, qd)
                t.wait_ge(*wt)
                if tb > 0:
                    t.wait_ge(vm.h, vm.v)
                for h in range(4):
                    for w in range(2):
                        b_ = 2 * h + w
                        psX = psF if b_ < 4 else psS
                        _mm = t.matmul(psX[:, 512 * (b_ % 4):512 * (b_ % 4) + 128],
                                 qfT[h][64 * w:64 * (w + 1), 128 * tb:128 * (tb + 1)],
                                 keysb[64 * w:64 * (w + 1),
                                       (2 * h + w) * 128:(2 * h + w + 1) * 128],
                                 start=True, stop=True)
                tmark(t, pe, _mm)
            block.tensor(tsim)

            def vtk(v, tb=tb, pv=pe.v):
                v.wait_ge(pe.h, pv)
                for h in range(4):
                    b1, b2 = 2 * h, 2 * h + 1
                    sl1 = (psF if b1 < 4 else psS)[:, 512 * (b1 % 4):512 * (b1 % 4) + 128]
                    sl2 = (psF if b2 < 4 else psS)[:, 512 * (b2 % 4):512 * (b2 % 4) + 128]
                    v.max(s1[:], sl1)
                    v.max(s2[:], sl2)
                    v.drain()
                    v.max_index(i1[:], s1[:], sl1)
                    v.max_index(i2[:], s2[:], sl2)
                    v.drain()
                    v.tensor_copy(i1f[:], i1[:])
                    v.tensor_copy(i2f[:], i2[:])
                    v.drain()
                    v.tensor_scalar(i1m[:], i1f[:], 128.0, None, OP.mult)
                    v.drain()
                    for j in range(8):
                        v.tensor_scalar(asc[:, 8 * j:8 * j + 8], s2[:],
                                        s1[:, j:j + 1], None, OP.add)
                        v.tensor_scalar(aidx[:, 8 * j:8 * j + 8], i2f[:],
                                        i1m[:, j:j + 1], None, OP.add)
                    v.drain()
                    v.max(fsh[:], asc[:])
                    v.drain()
                    v.max_index(pk_t[:], fsh[:], asc[:])
                    v.tensor_copy(fs_t[:, 8 * h:8 * h + 8], fsh[:])
                    v.drain()
                    v.tensor_copy(pkf[:], pk_t[:])
                    v.drain()
                    for j in range(8):
                        v.tensor_scalar(eqm[:], iob[:], pkf[:, j:j + 1],
                                        None, OP.is_equal)
                        v.drain()
                        v.tensor_tensor(prodm[:], eqm[:], aidx[:], OP.mult)
                        v.drain()
                        v.tensor_reduce(fi_f[:, 8 * h + j:8 * h + j + 1], prodm[:],
                                        axis=AXX, op=OP.add)
                        v.drain()
                v.tensor_copy(fi_u[:], fi_f[:])
                mark(v, vm)
            block.vector(vtk)

            def qtk(q, tb=tb, ev=vm.v):
                q.wait_ge(vm.h, ev)
                s_tk.issue(q, 0, [(fi_o[tb], fi_u[:], {}),
                                  (fs_o[tb], fs_t[:], {})])
            block.sync(qtk)
            tk_t = s_tk.totals()[0]

            def vtk2(v, tt=tk_t):
                v.wait_ge(*tt)
                mark(v, vm)
            block.vector(vtk2)

        ctx3.close()

        def qfin(q):
            q.wait_ge(st.h, st.v)
            wait_all(q, s_qst.totals())
            wait_all(q, s_tk.totals())
        block.sync(qfin)
        pr.finish = [(pe.h, pe.v), (vm.h, vm.v), (sm.h, sm.v), (st.h, st.v)]
        pr.ctx.pop_all()

    return pr



def l0a_inmaps(prep):
    cos, sin = prep["cosT"], prep["sinT"]
    maps = []
    for c in range(NCORE):
        r = c % 2
        maps.append({
            "xT": prep["xT"][c],
            "winT": prep["winT"],
            "binp": prep["binp"],
            "cosT": np.ascontiguousarray(cos[:, r * T:(r + 1) * T]),
            "sinT": np.ascontiguousarray(sin[:, r * T:(r + 1) * T]),
            "wq": prep["wq"],
            "keysT": prep["keysT"],
            "ones1": prep["ones1"],
            "onescol": prep["onescol"],
            "iota64": prep["iota64"],
        })
    return maps





def build_l0b(pr=None, io=None, gate=None):
    if pr is None:
        pr = Prog("l0b")
    nc = pr.nc
    summed = io is not None and "ap1" not in io
    xT = get_io(pr, io, "xT", [D, T], F32)
    xnT_i = get_io(pr, io, "xnT_i", [D, T], BF16)
    qT_i = get_io(pr, io, "qT_i", [D, T], BF16)
    ap0 = get_io(pr, io, "ap0", [128, 16], F32)
    ap1 = None if summed else get_io(pr, io, "ap1", [128, 16], F32)
    fi_i = get_io(pr, io, "fi_i", [16, 128, 32], U32)
    fs_i = get_io(pr, io, "fs_i", [16, 128, 32], F32)
    ecat = get_io(pr, io, "ecat", [16384, 4096], F32)
    woutT = get_io(pr, io, "woutT", [D, D], F32)
    bout_d = get_io(pr, io, "bout_d", [128, 16], F32)
    ident_d = get_io(pr, io, "ident_d", [128, 128], BF16)
    ones1 = get_io(pr, io, "ones1", [1, 128], BF16)
    onescol = get_io(pr, io, "onescol", [128, 1], BF16)

    x1T_o = get_io(pr, io, "x1T_o", [D, T], F32, out=True)
    xn2T_o = get_io(pr, io, "xn2T_o", [D, T], BF16, out=True)
    x1s = io["x1s"] if (io is not None and "x1s" in io) else pr.dtmp("x1s", [D, T], F32)

    with pr.ctx:
        xnT = [pr.sb(f"xnT{i}", [128, T], BF16) for i in range(NDT)]
        fT = [pr.sb(f"fT{i}", [128, T], BF16) for i in range(NDT)]
        o1 = pr.sb("o1", [1, 128], BF16)
        ocol = pr.sb("ocol", [128, 1], BF16)
        ident = pr.sb("ident", [128, 128], BF16)
        bout = pr.sb("bout", [128, 16], F32)
        acol = pr.sb("acol", [128, 16], F32)
        aps = [pr.sb(f"aps{i}", [128, 16], F32) for i in range(2)]

        pe = pr.sem("pe")
        vm = pr.sem("vm")
        sm = pr.sem("sm")
        st = pr.sem("st")
        ms = {"pe": pe, "vm": vm, "sm": sm, "st": st}
        s_const = Stream(pr, "cst", 1)
        s_xn = Stream(pr, "sxn", 1)
        s_g = Stream(pr, "sg", 4)
        s_tk = Stream(pr, "stk", 2)
        s_w = Stream(pr, "sw", 2)
        s_q = Stream(pr, "sq", 2)
        s_xf = Stream(pr, "sxf", 2)

        if pr.block is None:
            pr.block = pr.ctx.enter_context(nc.Block())
        block = pr.block
        gate_all(block, gate)
        b = {"o1": o1, "ocol": ocol}

        def qc(q):
            dmas = [
                (o1[:], ones1[:], {}),
                (ocol[:], onescol[:], {}),
                (ident[:], ident_d[:], {}),
                (bout[:], bout_d[:], {}),
                (aps[0][:], ap0[:], {}),
            ]
            if not summed:
                dmas.append((aps[1][:], ap1[:], {}))
            s_const.issue(q, 0, dmas)
            for dt in range(NDT):
                s_xn.issue(q, 0, [(xnT[dt][:], xnT_i[128 * dt:128 * (dt + 1), :], {})])
        block.sync(qc)
        const_t = s_const.totals()[0]
        xn_t = s_xn.totals()[0]

        def va(v):
            v.wait_ge(*const_t)
            if summed:
                v.tensor_copy(acol[:], aps[0][:])
            else:
                v.tensor_tensor(acol[:], aps[0][:], aps[1][:], OP.add)
            mark(v, vm)
        block.vector(va)
        a_done = vm.v

        ctx1 = ExitStack()
        xnat = pr.sb("xnat", [128, D], BF16, ctx1)
        gbuf = [pr.sb(f"gbuf{i}", [128, 4096], BF16, ctx1) for i in range(4)]
        dscr = [pr.sb(f"dscr{i}", [128, D], BF16, ctx1) for i in range(2)]
        diag = [pr.sb(f"diag{i}", [128, 128], BF16, ctx1) for i in range(2)]
        idxb = pr.sb("idxb", [128, 32], U32, ctx1)
        fsb = pr.sb("fsb", [128, 32], F32, ctx1)
        dcol = pr.sb("dcol", [128, 32], F32, ctx1)
        sil = pr.sb("sil", [128, 32], F32, ctx1)
        hcol = pr.sb("hcol", [128, 32], F32, ctx1)
        fnat = pr.sb("fnat", [128, D], BF16, ctx1)
        psU = pr.ps("psU", [128, 2048], F32, ctx1)
        psT = pr.ps("psT", [128, 128], BF16, ctx1)

        for tb in range(NDT):
            def qtk(q, tb=tb, vv=vm.v):
                q.wait_ge(vm.h, vv)
                s_tk.issue(q, tb, [(idxb[:], fi_i[tb], {}),
                                   (fsb[:], fs_i[tb], {})])
            block.sync(qtk)
            tk_t = (s_tk.sems[tb % 2].h, s_tk.sems[tb % 2].v)

            def tx(t, tb=tb, xv=xn_t, cv=const_t, pv=vm.v):
                t.wait_ge(*xv)
                t.wait_ge(*cv)
                t.wait_ge(vm.h, pv)
                for dt in range(NDT):
                    t.transpose(psT[:], xnT[dt][:, 128 * tb:128 * (tb + 1)], ident[:]).then_inc(pe.h, 1)
                    pe.inc()
                    t.wait_ge(sm.h, sm.v + 1)
                    sm.inc()
            block.tensor(tx)

            def sx(s, tb=tb, pv0=pe.v - NDT, sv0=sm.v - NDT):
                for dt in range(NDT):
                    s.wait_ge(pe.h, pv0 + dt + 1)
                    s.copy(xnat[:, 128 * dt:128 * (dt + 1)], psT[:])
                    s.drain()
                    s.nop().then_inc(sm.h, 1)
            block.scalar(sx)
            xnat_done = sm.v

            tu_mark = {}
            vh_mark = {}
            for k in range(32):
                def gk(g, tb=tb, k=k, tkt=tk_t):
                    g.wait_ge(*tkt)
                    if k >= 4:
                        g.wait_ge(pe.h, tu_mark[k - 4])
                    g.indirect_dma_start(
                        out=gbuf[k % 4][:], out_offset=None,
                        in_=ecat[:, :],
                        in_offset=IOA(ap=idxb[:, k:k + 1], axis=0),
                    ).then_inc(s_g.sems[k % 4].h, 16)
                    s_g.sems[k % 4].inc(16)
                block.gpsimd(gk)
                g_t = (s_g.sems[k % 4].h, s_g.sems[k % 4].v)

                def vd(v, tb=tb, k=k, gt=g_t, xd=xnat_done):
                    v.wait_ge(*gt)
                    v.wait_ge(sm.h, xd)
                    v.scalar_tensor_tensor(dscr[k % 2][:], gbuf[k % 4][:, 0:2048],
                                           1.0, xnat[:], OP.mult, OP.mult,
                                           accum_out=dcol[:, k:k + 1]
                                           ).then_inc(vm.h, 1)
                    vm.inc()
                block.vector(vd)

                def sk(s, k=k, vv=vm.v):
                    s.wait_ge(vm.h, vv)
                    s.activation(sil[:, k:k + 1], dcol[:, k:k + 1],
                                 AF.Silu).then_inc(sm.h, 1)
                    sm.inc()
                block.scalar(sk)

                def vh(v, k=k, sv=sm.v):
                    v.wait_ge(sm.h, sv)
                    if k >= 2:
                        v.wait_ge(pe.h, tu_mark[k - 2])
                    v.tensor_scalar(hcol[:, k:k + 1], fsb[:, k:k + 1], 0.0, None,
                                    OP.max)
                    v.drain()
                    v.tensor_tensor(hcol[:, k:k + 1], hcol[:, k:k + 1],
                                    sil[:, k:k + 1], OP.mult)
                    v.drain()
                    v.tensor_scalar(diag[k % 2][:], ident[:], hcol[:, k:k + 1],
                                    None, OP.mult).then_inc(vm.h, 1)
                    vm.inc()
                block.vector(vh)
                vh_mark[k] = vm.v

                def tu(t, tb=tb, k=k, vv=vm.v, gt=g_t):
                    t.wait_ge(vm.h, vv)
                    t.wait_ge(*gt)
                    for tc in range(4):
                        _mm = t.matmul(psU[:, 512 * tc:512 * (tc + 1)],
                                       diag[k % 2][:],
                                       gbuf[k % 4][:, 2048 + 512 * tc:
                                                   2048 + 512 * (tc + 1)],
                                       start=(k == 0), stop=(k == 31))
                    tmark(t, pe, _mm)
                block.tensor(tu)
                tu_mark[k] = pe.v

            def sf(s, tb=tb, pv=pe.v):
                s.wait_ge(pe.h, pv)
                s.copy(fnat[:], psU[:])
                mark(s, sm)
            block.scalar(sf)

            def tf2(t, tb=tb, sv=sm.v):
                t.wait_ge(sm.h, sv)
                for dt in range(NDT):
                    t.transpose(psT[:], fnat[:, 128 * dt:128 * (dt + 1)], ident[:]).then_inc(pe.h, 1)
                    pe.inc()
                    t.wait_ge(sm.h, sm.v + 1)
                    sm.inc()
            block.tensor(tf2)

            def sf2(s, tb=tb, pv0=pe.v - NDT, sv0=sm.v - NDT):
                for dt in range(NDT):
                    s.wait_ge(pe.h, pv0 + dt + 1)
                    s.copy(fT[dt][:, 128 * tb:128 * (tb + 1)], psT[:])
                    s.drain()
                    s.nop().then_inc(sm.h, 1)
            block.scalar(sf2)

            def vnext(v, sv=sm.v):
                v.wait_ge(sm.h, sv)
                mark(v, vm)
            block.vector(vnext)

        ctx1.close()
        ffn_done = vm.v

        ctx2 = ExitStack()
        ytile = [pr.sb(f"yt{i}", [128, 1024], BF16, ctx2) for i in range(NDT)]
        qbuf = [pr.sb(f"qbuf{i}", [128, 1024], BF16, ctx2) for i in range(2)]
        wbuf = [pr.sb(f"wb{i}", [128, NDT * 128], BF16, ctx2) for i in range(2)]
        xf = [pr.sb(f"xf{i}", [128, 1024], F32, ctx2) for i in range(2)]
        t1 = pr.sb("t1", [128, 1024], F32, ctx2)
        t2 = pr.sb("t2", [128, 1024], F32, ctx2)
        psW = pr.ps("psW", [128, 1024], F32, ctx2)

        w_t = {}

        def wload(g, e, buf):
            w_t[e] = s_w.issue(g, buf, [(
                wbuf[buf][:].rearrange("p (dt c) -> p dt c", c=128),
                woutT[:, 128 * e:128 * (e + 1)].rearrange("(dt p) c -> p dt c", p=128),
                {})])

        for half in range(2):
            hs = slice(1024 * half, 1024 * (half + 1))
            for dt in range(NDT):
                def qy(q, dt=dt, half=half, fv=ffn_done, vv=vm.v):
                    q.wait_ge(vm.h, vv if (half > 0 or dt >= 2) else fv)
                    s_q.issue(q, dt, [(qbuf[dt % 2][:], qT_i[128 * dt:128 * (dt + 1), hs], {})])
                block.sync(qy)
                q_t = (s_q.sems[dt % 2].h, s_q.sems[dt % 2].v)

                def vy(v, dt=dt, qt=q_t, av=a_done):
                    v.wait_ge(*qt)
                    v.wait_ge(vm.h, av)
                    v.tensor_scalar(ytile[dt][:], qbuf[dt % 2][:], acol[:, dt:dt + 1],
                                    None, OP.mult)
                    mark(v, vm)
                block.vector(vy)
            y_done = vm.v

            def g0(g, half=half, yv=y_done):
                g.wait_ge(vm.h, yv)
                wload(g, 0, 0)
                wload(g, 1, 1)
            block.gpsimd(g0)

            for d2 in range(NDT):
                def tw(t, d2=d2, yv=y_done):
                    t.wait_ge(vm.h, yv)
                    t.wait_ge(*w_t[d2])
                    if d2 > 0:
                        t.wait_ge(vm.h, vm.v)
                    for d1 in range(NDT):
                        for tc in range(2):
                            _mm = t.matmul(psW[:, 512 * tc:512 * (tc + 1)],
                                     wbuf[d2 % 2][:, d1 * 128:d1 * 128 + 128],
                                     ytile[d1][:, 512 * tc:512 * (tc + 1)],
                                     start=(d1 == 0), stop=(d1 == NDT - 1))
                    tmark(t, pe, _mm)
                block.tensor(tw)
                mmv = pe.v
                if d2 < NDT - 2:
                    def g2(g, d2=d2, mmv=mmv):
                        g.wait_ge(pe.h, mmv)
                        wload(g, d2 + 2, d2 % 2)
                    block.gpsimd(g2)

                def qx(q, d2=d2, half=half, vv=vm.v):
                    q.wait_ge(vm.h, vv)
                    s_xf.issue(q, d2, [(xf[d2 % 2][:],
                                        xT[128 * d2:128 * (d2 + 1), hs], {})])
                block.sync(qx)
                xf_t = (s_xf.sems[d2 % 2].h, s_xf.sems[d2 % 2].v)

                def vx(v, d2=d2, half=half, mmv=mmv, xt_=xf_t):
                    v.wait_ge(pe.h, mmv)
                    v.wait_ge(*xt_)
                    v.tensor_scalar(t1[:], psW[:], bout[:, d2:d2 + 1], None, OP.add)
                    v.drain()
                    v.tensor_tensor(t2[:], t1[:], xnT[d2][:, hs], OP.add)
                    v.drain()
                    v.tensor_tensor(t1[:], t2[:], fT[d2][:, hs], OP.add)
                    v.drain()
                    v.tensor_tensor(t2[:], t1[:], xf[d2 % 2][:], OP.add)
                    mark(v, vm)
                block.vector(vx)

                def qo(q, d2=d2, half=half, vv=vm.v):
                    q.wait_ge(vm.h, vv)
                    q.dma_start(out=x1T_o[128 * d2:128 * (d2 + 1), hs],
                                in_=t2[:]).then_inc(st.h, 16)
                    st.inc(16)
                    q.dma_start(out=x1s[128 * d2:128 * (d2 + 1), hs],
                                in_=t2[:]).then_inc(st.h, 16)
                    st.inc(16)
                block.sync(qo)
                stv = st.v

                def vnx(v, sv=stv):
                    v.wait_ge(st.h, sv)
                    mark(v, vm)
                block.vector(vnx)

        ctx2.close()

        emit_ln(pr, b, x1s, xnT, ms, eps=1e-5, tag="ln2")
        xn2_done = vm.v

        def qxn2(q):
            q.wait_ge(vm.h, xn2_done)
            for dt in range(NDT):
                q.dma_start(out=xn2T_o[128 * dt:128 * (dt + 1), :],
                            in_=xnT[dt][:]).then_inc(st.h, 16)
                st.inc(16)
        block.sync(qxn2)

        def qfin(q):
            q.wait_ge(st.h, st.v)
        block.sync(qfin)
        pr.finish = [(pe.h, pe.v), (vm.h, vm.v), (sm.h, sm.v), (st.h, st.v)]
        pr.ctx.pop_all()

    return pr


def l0b_inmaps(prep, l0a_res):
    maps = []
    for c in range(NCORE):
        part = c + 1 if c % 2 == 0 else c - 1
        maps.append({
            "xT": prep["xT"][c],
            "xnT_i": l0a_res[c]["xnT_o"],
            "qT_i": l0a_res[c]["qT_o"],
            "ap0": l0a_res[c]["apart_o"],
            "ap1": l0a_res[part]["apart_o"],
            "fi_i": l0a_res[c]["fi_o"],
            "fs_i": l0a_res[c]["fs_o"],
            "ecat": prep["e_cat"],
            "woutT": prep["woutT"],
            "bout_d": prep["bout"],
            "ident_d": prep["ident"],
            "ones1": prep["ones1"],
            "onescol": prep["onescol"],
        })
    return maps





def build_l1a(pr=None, io=None, gate=None):
    if pr is None:
        pr = Prog("l1a")
    nc = pr.nc
    ag = io is not None and "xn2f_ag" in io
    xn2f_d = io["xn2f_ag"] if ag else get_io(pr, io, "xn2f", [D, S], BF16)
    kvaT_d = get_io(pr, io, "kvaT", [D, 576], F32)
    lnw_d = get_io(pr, io, "lnw_d", [128, 4], F32)
    kvbn_d = get_io(pr, io, "kvbn", [512, 2048], F32)
    kvbv_d = get_io(pr, io, "kvbv", [512, 2048], F32)
    qpT_d = get_io(pr, io, "qpT", [D, 1536], F32)
    ones1 = get_io(pr, io, "ones1", [1, 128], BF16)
    onescol = get_io(pr, io, "onescol", [128, 1], BF16)

    kT_o = get_io(pr, io, "kT_o", [2048, S], BF16, out=True)
    krotT_o = get_io(pr, io, "krotT_o", [64, S], BF16, out=True)
    v_o = get_io(pr, io, "v_o", [S, 2048], BF16, out=True)
    q2T_o = get_io(pr, io, "q2T_o", [1536, S], BF16, out=True)

    with pr.ctx:
        o1 = pr.sb("o1", [1, 128], BF16)
        ocol = pr.sb("ocol", [128, 1], BF16)
        lnw = pr.sb("lnw", [128, 4], F32)
        cn = [pr.sb(f"cn{i}", [128, S], BF16) for i in range(4)]
        RNb = pr.sb("RNb", [128, S], BF16)
        ssq = pr.sb("ssq", [1, S], F32)
        rn16 = pr.sb("rn16", [1, S], BF16)

        pe = pr.sem("pe")
        vm = pr.sem("vm")
        sm = pr.sem("sm")
        st = pr.sem("st")
        s_const = Stream(pr, "cst", 1)
        s_xn = Stream(pr, "sxn", 1)
        s_w = Stream(pr, "sw", 2)

        if pr.block is None:
            pr.block = pr.ctx.enter_context(nc.Block())
        block = pr.block
        gate_all(block, gate)

        def qc(q):
            s_const.issue(q, 0, [
                (o1[:], ones1[:], {}),
                (ocol[:], onescol[:], {}),
                (lnw[:], lnw_d[:], {}),
            ])
        block.sync(qc)
        const_t = s_const.totals()[0]

        ctxA = ExitStack()
        xnf = [pr.sb(f"xnf{i}", [128, S], BF16, ctxA) for i in range(NDT)]
        wbuf = [pr.sb(f"wb{i}", [128, NDT * 128], BF16, ctxA) for i in range(2)]
        evb = pr.sb("evb", [128, 2048], BF16, ctxA)
        csq = evb
        psA = pr.ps("psA", [128, 2048], F32, ctxA)
        psR = pr.ps("psR", [128, 512], F32, ctxA)

        def qxn(q):
            for dt in range(NDT):
                if ag:
                    s_xn.issue(q, 0, [
                        (xnf[dt][:, 0:T], xn2f_d[128 * dt:128 * (dt + 1), :], {}),
                        (xnf[dt][:, T:S],
                         xn2f_d[D + 128 * dt:D + 128 * (dt + 1), :], {}),
                    ])
                else:
                    s_xn.issue(q, 0, [(xnf[dt][:],
                                       xn2f_d[128 * dt:128 * (dt + 1), :], {})])
        block.sync(qxn)
        xn_t = s_xn.totals()[0]

        w_t = {}

        def wload(g, dram, e, buf, ncol=128):
            w_t[(dram is qpT_d, e)] = s_w.issue(g, buf, [(
                wbuf[buf][:, 0:NDT * ncol].rearrange("p (dt c) -> p dt c", c=ncol),
                dram[:, ncol * e:ncol * (e + 1)].rearrange("(dt p) c -> p dt c", p=128),
                {})])

        def gq0(g):
            wload(g, qpT_d, 0, 0)
            wload(g, qpT_d, 1, 1)
        block.gpsimd(gq0)

        for mc in range(12):
            def tq(t, mc=mc, xt=xn_t, ct=const_t):
                t.wait_ge(*xt)
                t.wait_ge(*ct)
                t.wait_ge(*w_t[(True, mc)])
                if mc > 0:
                    t.wait_ge(vm.h, vm.v)
                for th in range(2):
                    for dt in range(NDT):
                        for tc in range(4):
                            _mm = t.matmul(psA[:, 512 * tc:512 * (tc + 1)],
                                     wbuf[mc % 2][:, dt * 128:dt * 128 + 128],
                                     xnf[dt][:, 2048 * th + 512 * tc:
                                              2048 * th + 512 * (tc + 1)],
                                     start=(dt == 0), stop=(dt == NDT - 1))
                    _mm.then_inc(pe.h, 1)
                    pe.inc()
                    t.wait_ge(vm.h, vm.v + th + 1)
            block.tensor(tq)
            if mc < 10:
                def gq(g, mc=mc, pv=pe.v):
                    g.wait_ge(pe.h, pv)
                    wload(g, qpT_d, mc + 2, mc % 2)
                block.gpsimd(gq)

            for th in range(2):
                def vq(v, mc=mc, th=th, pv=pe.v - 1 + th):
                    v.wait_ge(pe.h, pv)
                    v.tensor_scalar(evb[:], psA[:], QSCALE, None, OP.mult)
                    mark(v, vm)
                block.vector(vq)

                def qq(q, mc=mc, th=th, vv=vm.v):
                    q.wait_ge(vm.h, vv)
                    q.dma_start(out=q2T_o[128 * mc:128 * (mc + 1),
                                          2048 * th:2048 * (th + 1)],
                                in_=evb[:]).then_inc(st.h, 16)
                    st.inc(16)
                block.sync(qq)
                stv = st.v

                def vq2(v, sv=stv):
                    v.wait_ge(st.h, sv)
                    mark(v, vm)
                block.vector(vq2)
        q_done = vm.v

        def gc0(g, pv=pe.v):
            g.wait_ge(pe.h, pv)
            wload(g, kvaT_d, 0, 0)
            wload(g, kvaT_d, 1, 1)
        block.gpsimd(gc0)

        def vz(v, qd=q_done):
            v.wait_ge(vm.h, qd)
            v.memset(ssq[:], 0.0)
            mark(v, vm)
        block.vector(vz)

        for cc in range(5):
            ncol = 128 if cc < 4 else 64
            if cc >= 2 and cc < 4:
                pass
            for th in range(2):
                def tc_(t, cc=cc, th=th, ncol=ncol, qd=q_done):
                    t.wait_ge(vm.h, vm.v)
                    t.wait_ge(*w_t[(False, cc)])
                    for dt in range(NDT):
                        for tc in range(4):
                            _mm = t.matmul(psA[0:ncol, 512 * tc:512 * (tc + 1)],
                                     wbuf[cc % 2][:, dt * 128:dt * 128 + ncol],
                                     xnf[dt][:, 2048 * th + 512 * tc:
                                              2048 * th + 512 * (tc + 1)],
                                     start=(dt == 0), stop=(dt == NDT - 1))
                    tmark(t, pe, _mm)
                block.tensor(tc_)
                mmv = pe.v
                if th == 1 and cc < 3:
                    def gc(g, cc=cc, mmv=mmv):
                        g.wait_ge(pe.h, mmv)
                        if cc + 2 < 4:
                            wload(g, kvaT_d, cc + 2, cc % 2)
                        else:
                            w_t[(False, 4)] = s_w.issue(g, cc % 2, [(
                                wbuf[cc % 2][:].rearrange(
                                    "p (dt c) -> p dt c", c=128)[:, :, 0:64],
                                kvaT_d[:, 512:576].rearrange(
                                    "(dt p) c -> p dt c", p=128), {})])
                    block.gpsimd(gc)

                if cc < 4:
                    def sc(s, cc=cc, th=th, mmv=mmv):
                        s.wait_ge(pe.h, mmv)
                        s.copy(cn[cc][:, 2048 * th:2048 * (th + 1)], psA[:])
                        s.drain()
                        s.activation(csq[:], psA[:], AF.Square)
                        mark(s, sm)
                    block.scalar(sc)

                    def tsq(t, sv=sm.v):
                        t.wait_ge(sm.h, sv)
                        for tc in range(4):
                            _mm = t.matmul(psR[0:1, 0:512], ocol[:],
                                     csq[:, 512 * tc:512 * (tc + 1)],
                                     start=True, stop=True)
                            _mm.then_inc(pe.h, 1)
                            pe.inc()
                            t.wait_ge(vm.h, vm.v + tc + 1)
                    block.tensor(tsq)

                    for tc in range(4):
                        def vsq(v, th=th, tc=tc, pv=pe.v - 3 + tc):
                            v.wait_ge(pe.h, pv)
                            v.tensor_tensor(ssq[0:1, 2048 * th + 512 * tc:
                                                2048 * th + 512 * (tc + 1)],
                                            ssq[0:1, 2048 * th + 512 * tc:
                                                2048 * th + 512 * (tc + 1)],
                                            psR[0:1, 0:512], OP.add)
                            mark(v, vm)
                        block.vector(vsq)

                    def vcn(v, sv=sm.v):
                        v.wait_ge(sm.h, sv)
                        mark(v, vm)
                    block.vector(vcn)
                else:
                    def vrot(v, th=th, mmv=mmv):
                        v.wait_ge(pe.h, mmv)
                        v.tensor_copy(evb[0:64, 0:2048], psA[0:64, :])
                        mark(v, vm)
                    block.vector(vrot)

                    def qrot(q, th=th, vv=vm.v):
                        q.wait_ge(vm.h, vv)
                        q.dma_start(out=krotT_o[:, 2048 * th:2048 * (th + 1)],
                                    in_=evb[0:64, 0:2048]).then_inc(st.h, 16)
                        st.inc(16)
                    block.sync(qrot)
                    stv = st.v

                    def vrot2(v, sv=stv):
                        v.wait_ge(st.h, sv)
                        mark(v, vm)
                    block.vector(vrot2)
        c_done = vm.v

        def vrr(v, cd=c_done):
            v.wait_ge(vm.h, cd)
            v.tensor_scalar(ssq[:], ssq[:], 1.0 / KVR_, 1e-6, OP.mult, OP.add)
            mark(v, vm)
        block.vector(vrr)

        def srr(s, vv=vm.v):
            s.wait_ge(vm.h, vv)
            s.activation(ssq[:], ssq[:], AF.Ln)
            s.drain()
            s.activation(ssq[:], ssq[:], AF.Exp, scale=-0.5)
            mark(s, sm)
        block.scalar(srr)

        def vrr2(v, sv=sm.v):
            v.wait_ge(sm.h, sv)
            v.tensor_copy(rn16[:], ssq[:])
            mark(v, vm)
        block.vector(vrr2)

        def trb(t, vv=vm.v):
            t.wait_ge(vm.h, vv)
            for tc in range(8):
                _mm = t.matmul(psA[:, 512 * (tc % 4):512 * (tc % 4 + 1)], o1[:],
                         rn16[:, 512 * tc:512 * (tc + 1)], start=True, stop=True)
                _mm.then_inc(pe.h, 1)
                pe.inc()
                t.wait_ge(sm.h, sm.v + tc + 1)
        block.tensor(trb)

        for tc in range(8):
            def srb(s, tc=tc, pv=pe.v - 7 + tc):
                s.wait_ge(pe.h, pv)
                s.copy(RNb[:, 512 * tc:512 * (tc + 1)],
                       psA[:, 512 * (tc % 4):512 * (tc % 4 + 1)])
                s.drain()
                s.nop().then_inc(sm.h, 1)
                sm.inc()
            block.scalar(srb)
        rb_done = sm.v

        for cc in range(4):
            def vcs(v, cc=cc, rb=rb_done):
                v.wait_ge(sm.h, rb)
                v.tensor_tensor(cn[cc][:], cn[cc][:], RNb[:], OP.mult)
                v.drain()
                v.tensor_scalar(cn[cc][:], cn[cc][:], lnw[:, cc:cc + 1], None,
                                OP.mult)
                mark(v, vm)
            block.vector(vcs)
        cn_done = vm.v

        ctxA.close()

        ctxB = ExitStack()
        kwb = pr.sb("kwb", [128, 4 * 128], BF16, ctxB)
        vwb = pr.sb("vwb", [128, 4 * 2048], BF16, ctxB)
        ev2 = pr.sb("ev2", [128, 2048], BF16, ctxB)
        psK = pr.ps("psK", [128, 2048], F32, ctxB)
        s_kw = Stream(pr, "skw", 2)

        kw_t = {}

        def kwload(g, e, buf):
            kw_t[e] = s_kw.issue(g, buf, [(
                kwb[:, 0:4 * 128].rearrange("p (rt c) -> p rt c", c=128),
                kvbn_d[:, 128 * e:128 * (e + 1)].rearrange("(rt p) c -> p rt c", p=128),
                {})])

        def gk0(g, cd=cn_done):
            g.wait_ge(vm.h, cd)
            kwload(g, 0, 0)
            g.dma_start(out=vwb[:].rearrange("p (rt c) -> p rt c", c=2048),
                        in_=kvbv_d.rearrange("(rt p) c -> p rt c", p=128),
                        ).then_inc(s_kw.sems[1].h, 16)
            s_kw.sems[1].inc(16)
        block.gpsimd(gk0)
        vw_t = (s_kw.sems[1].h, s_kw.sems[1].v)

        for mc in range(16):
            for th in range(2):
                def tk(t, mc=mc, th=th, cd=cn_done):
                    t.wait_ge(vm.h, vm.v if (mc or th) else cd)
                    t.wait_ge(*kw_t[mc])
                    for rt in range(4):
                        for tc in range(4):
                            _mm = t.matmul(psK[:, 512 * tc:512 * (tc + 1)],
                                     kwb[:, rt * 128:rt * 128 + 128],
                                     cn[rt][:, 2048 * th + 512 * tc:
                                             2048 * th + 512 * (tc + 1)],
                                     start=(rt == 0), stop=(rt == 3))
                    tmark(t, pe, _mm)
                block.tensor(tk)
                mmv = pe.v

                def vk(v, mmv=mmv):
                    v.wait_ge(pe.h, mmv)
                    v.tensor_copy(ev2[:], psK[:])
                    mark(v, vm)
                block.vector(vk)

                def qk(q, mc=mc, th=th, vv=vm.v):
                    q.wait_ge(vm.h, vv)
                    q.dma_start(out=kT_o[128 * mc:128 * (mc + 1),
                                         2048 * th:2048 * (th + 1)],
                                in_=ev2[:]).then_inc(st.h, 16)
                    st.inc(16)
                block.sync(qk)
                stv = st.v

                def vk2(v, sv=stv):
                    v.wait_ge(st.h, sv)
                    mark(v, vm)
                block.vector(vk2)
            if mc < 15:
                def gkn(g, mc=mc, pv=pe.v):
                    g.wait_ge(pe.h, pv)
                    kwload(g, mc + 1, 0)
                block.gpsimd(gkn)

        for kc in range(32):
            def tv(t, kc=kc, vt=vw_t):
                t.wait_ge(*vt)
                t.wait_ge(vm.h, vm.v)
                for rt in range(4):
                    for tc in range(4):
                        _mm = t.matmul(psK[:, 512 * tc:512 * (tc + 1)],
                                 cn[rt][:, 128 * kc:128 * (kc + 1)],
                                 vwb[:, rt * 2048 + 512 * tc:
                                     rt * 2048 + 512 * (tc + 1)],
                                 start=(rt == 0), stop=(rt == 3))
                tmark(t, pe, _mm)
            block.tensor(tv)
            mmv = pe.v

            def vv2(v, mmv=mmv):
                v.wait_ge(pe.h, mmv)
                v.tensor_copy(ev2[:], psK[:])
                mark(v, vm)
            block.vector(vv2)

            def qv(q, kc=kc, vv=vm.v):
                q.wait_ge(vm.h, vv)
                q.dma_start(out=v_o[128 * kc:128 * (kc + 1), :],
                            in_=ev2[:]).then_inc(st.h, 16)
                st.inc(16)
            block.sync(qv)
            stv = st.v

            def vv3(v, sv=stv):
                v.wait_ge(st.h, sv)
                mark(v, vm)
            block.vector(vv3)

        ctxB.close()

        def qfin(q):
            q.wait_ge(st.h, st.v)
        block.sync(qfin)
        pr.finish = [(pe.h, pe.v), (vm.h, vm.v), (sm.h, sm.v), (st.h, st.v)]
        pr.ctx.pop_all()

    return pr


KVR_ = 512.0



def build_l1b(pr=None, io=None, gate=None):
    if pr is None:
        pr = Prog("l1b")
    nc = pr.nc
    kT_d = get_io(pr, io, "kT", [1024, S], BF16)
    krotT_d = get_io(pr, io, "krotT", [64, S], BF16)
    v_d = get_io(pr, io, "v", [S, 1024], BF16)
    qT_d = get_io(pr, io, "qT", [1536, S], BF16)
    opT_d = get_io(pr, io, "opT", [1024, D], F32)
    masks_d = get_io(pr, io, "masks", [4, 128, 512], F32)
    ones1 = get_io(pr, io, "ones1", [1, 128], BF16)
    onescol = get_io(pr, io, "onescol", [128, 1], BF16)

    op_o = get_io(pr, io, "op_o", [S, D], BF16, out=True)

    with pr.ctx:
        o1 = pr.sb("o1", [1, 128], BF16)
        ocol = pr.sb("ocol", [128, 1], BF16)
        maskb = pr.sb("maskb", [128, 4 * 512], F32)
        ob = pr.sb("ob", [128, 8 * 2048], BF16)
        krot = pr.sb("krot", [64, S], BF16)
        pvt = [pr.sb(f"pvt{i}", [128, 2048], BF16) for i in range(8)]

        pe = pr.sem("pe")
        vm = pr.sem("vm")
        sm = pr.sem("sm")
        st = pr.sem("st")
        s_const = Stream(pr, "cst", 1)
        s_kh = Stream(pr, "skh", 2)
        s_vh = Stream(pr, "svh", 2)
        s_qh = Stream(pr, "sqh", 2)

        if pr.block is None:
            pr.block = pr.ctx.enter_context(nc.Block())
        block = pr.block
        gate_all(block, gate)

        def qc(q):
            s_const.issue(q, 0, [
                (o1[:], ones1[:], {}),
                (ocol[:], onescol[:], {}),
                (maskb[:].rearrange("p (a c) -> p a c", c=512),
                 masks_d.rearrange("a b c -> b a c"), {}),
                (krot[:], krotT_d[:], {}),
            ])
        block.sync(qc)
        const_t = s_const.totals()[0]

        s_ob = Stream(pr, "sob", 1)

        def gob(g):
            s_ob.issue(g, 0, [(ob[:].rearrange("p (a c) -> p a c", c=2048),
                               opT_d.rearrange("(a p) c -> p a c", p=128), {})])
        block.gpsimd(gob)
        ob_t = s_ob.totals()[0]

        ctx1 = ExitStack()
        kh = [pr.sb(f"kh{i}", [128, S], BF16, ctx1) for i in range(2)]
        vh = [pr.sb(f"vh{i}", [128, 32 * 128], BF16, ctx1) for i in range(2)]
        qhn = [pr.sb(f"qhn{i}", [128, 2048], BF16, ctx1) for i in range(2)]
        qhr = [pr.sb(f"qhr{i}", [64, 2048], BF16, ctx1) for i in range(2)]
        pbuf = [pr.sb(f"pbuf{i}", [128, 512], BF16, ctx1) for i in range(2)]
        zrow = pr.sb("zrow", [1, 512], F32, ctx1)
        rz16 = pr.sb("rz16", [1, 512], BF16, ctx1)
        rzb = pr.sb("rzb", [128, 512], BF16, ctx1)
        onat = pr.sb("onat", [128, 2048], BF16, ctx1)
        psSb = [pr.ps(f"psS{i}", [128, 512], F32, ctx1) for i in range(2)]
        psZ = pr.ps("psZ", [128, 512], F32, ctx1)
        psP = pr.ps("psP", [128, 512], F32, ctx1)
        psO = pr.ps("psO", [128, 2048], F32, ctx1)

        for qhalf in range(2):
            for hl in range(8):
                def qh_(q, hl=hl, qhalf=qhalf, vv=vm.v):
                    q.wait_ge(vm.h, vv)
                    s_kh.issue(q, hl, [(kh[hl % 2][:],
                                        kT_d[128 * hl:128 * (hl + 1), :], {})])
                    s_vh.issue(q, hl, [(
                        vh[hl % 2][:].rearrange("p (a c) -> p a c", c=128),
                        v_d[:, 128 * hl:128 * (hl + 1)].rearrange(
                            "(a p) c -> p a c", p=128), {})])
                    s_qh.issue(q, hl, [
                        (qhn[hl % 2][:],
                         qT_d[192 * hl:192 * hl + 128,
                              2048 * qhalf:2048 * (qhalf + 1)], {}),
                        (qhr[hl % 2][0:64, :],
                         qT_d[192 * hl + 128:192 * hl + 192,
                              2048 * qhalf:2048 * (qhalf + 1)], {}),
                    ])
                block.sync(qh_)
                kh_t = (s_kh.sems[hl % 2].h, s_kh.sems[hl % 2].v)
                vh_t = (s_vh.sems[hl % 2].h, s_vh.sems[hl % 2].v)
                qh_t = (s_qh.sems[hl % 2].h, s_qh.sems[hl % 2].v)

                for g_ in range(4):
                    qg = qhalf * 4 + g_
                    nkt = 4 * qg + 4
                    exp_mark = {}
                    tz_mark = {}

                    def emit_ts(kt, g_=g_, qg=qg, hl=hl, kh_t=kh_t, qh_t=qh_t):
                        psS = psSb[kt % 2]

                        def ts_(t, kt=kt, psS=psS, vv=vm.v):
                            t.wait_ge(*kh_t)
                            t.wait_ge(*qh_t)
                            t.wait_ge(*const_t)
                            if kt >= 2:
                                t.wait_ge(sm.h, exp_mark[kt - 2])
                            elif kt == 0:
                                t.wait_ge(vm.h, vv)
                            _mm = t.matmul(psS[:], kh[hl % 2][:, 128 * kt:128 * (kt + 1)],
                                     qhn[hl % 2][:, 512 * g_:512 * (g_ + 1)],
                                     start=True, stop=False)
                            _mm = t.matmul(psS[:], krot[:, 128 * kt:128 * (kt + 1)],
                                     qhr[hl % 2][0:64, 512 * g_:512 * (g_ + 1)],
                                     start=False, stop=True)
                            tmark(t, pe, _mm)
                        block.tensor(ts_)
                        mmv = pe.v

                        diag_j = kt - 4 * qg
                        if diag_j >= 0:
                            def vmsk(v, j=diag_j, mmv=mmv, psS=psS):
                                v.wait_ge(pe.h, mmv)
                                v.tensor_tensor(psS[:], psS[:],
                                                maskb[:, 512 * j:512 * (j + 1)],
                                                OP.add)
                                mark(v, vm)
                            block.vector(vmsk)

                        def se_(s, kt=kt, mmv=mmv, masked=(diag_j >= 0),
                                vv=vm.v, psS=psS):
                            if masked:
                                s.wait_ge(vm.h, vv)
                            else:
                                s.wait_ge(pe.h, mmv)
                            if kt >= 2:
                                s.wait_ge(pe.h, tz_mark[kt - 2])
                            s.activation(pbuf[kt % 2][:], psS[:], AF.Exp)
                            mark(s, sm)
                        block.scalar(se_)
                        exp_mark[kt] = sm.v

                    def emit_tz(kt, nkt=nkt, hl=hl, vh_t=vh_t):
                        def tz(t, kt=kt):
                            t.wait_ge(sm.h, exp_mark[kt])
                            t.wait_ge(*vh_t)
                            _mm = t.matmul(psZ[0:1, :], ocol[:], pbuf[kt % 2][:],
                                     start=(kt == 0), stop=(kt == nkt - 1))
                            _mm = t.matmul(psP[:], vh[hl % 2][:, 128 * kt:128 * (kt + 1)],
                                     pbuf[kt % 2][:],
                                     start=(kt == 0), stop=(kt == nkt - 1))
                            tmark(t, pe, _mm)
                        block.tensor(tz)
                        tz_mark[kt] = pe.v

                    for kt in range(nkt):
                        emit_ts(kt)
                        if kt >= 1:
                            emit_tz(kt - 1)
                    emit_tz(nkt - 1)

                    def vz_(v, pv=pe.v):
                        v.wait_ge(pe.h, pv)
                        v.tensor_copy(zrow[:], psZ[0:1, :])
                        v.drain()
                        v.reciprocal(zrow[:], zrow[:])
                        v.drain()
                        v.tensor_copy(rz16[:], zrow[:])
                        mark(v, vm)
                    block.vector(vz_)

                    def trz(t, vv=vm.v):
                        t.wait_ge(vm.h, vv)
                        _mm = t.matmul(psSb[0][:], o1[:], rz16[:], start=True, stop=True)
                        tmark(t, pe, _mm)
                    block.tensor(trz)

                    def srz(s, pv=pe.v):
                        s.wait_ge(pe.h, pv)
                        s.copy(rzb[:], psSb[0][:])
                        mark(s, sm)
                    block.scalar(srz)

                    def vpv(v, hl=hl, g_=g_, sv=sm.v):
                        v.wait_ge(sm.h, sv)
                        v.tensor_tensor(pvt[hl][:, 512 * g_:512 * (g_ + 1)],
                                        psP[:], rzb[:], OP.mult)
                        mark(v, vm)
                    block.vector(vpv)

            pv_done = vm.v
            for qc_ in range(16):
                def to(t, qc_=qc_, qhalf=qhalf, pv=pv_done, obt=ob_t, vv=vm.v):
                    t.wait_ge(vm.h, pv)
                    t.wait_ge(*obt)
                    if qc_ > 0:
                        t.wait_ge(vm.h, vv)
                    for hl in range(8):
                        for tc in range(4):
                            _mm = t.matmul(psO[:, 512 * tc:512 * (tc + 1)],
                                     pvt[hl][:, 128 * qc_:128 * (qc_ + 1)],
                                     ob[:, 2048 * hl + 512 * tc:
                                        2048 * hl + 512 * (tc + 1)],
                                     start=(hl == 0), stop=(hl == 7))
                    tmark(t, pe, _mm)
                block.tensor(to)
                mmv = pe.v

                def vo(v, mmv=mmv):
                    v.wait_ge(pe.h, mmv)
                    v.tensor_copy(onat[:], psO[:])
                    mark(v, vm)
                block.vector(vo)

                def qo(q, qc_=qc_, qhalf=qhalf, vv=vm.v):
                    q.wait_ge(vm.h, vv)
                    q.dma_start(out=op_o[2048 * qhalf + 128 * qc_:
                                         2048 * qhalf + 128 * (qc_ + 1), :],
                                in_=onat[:]).then_inc(st.h, 16)
                    st.inc(16)
                block.sync(qo)
                stv = st.v

                def vo2(v, sv=stv):
                    v.wait_ge(st.h, sv)
                    mark(v, vm)
                block.vector(vo2)

        ctx1.close()

        def qfin(q):
            q.wait_ge(st.h, st.v)
        block.sync(qfin)
        pr.finish = [(pe.h, pe.v), (vm.h, vm.v), (sm.h, sm.v), (st.h, st.v)]
        pr.ctx.pop_all()

    return pr



def build_l1c(pr=None, io=None, gate=None):
    if pr is None:
        pr = Prog("l1c")
    nc = pr.nc
    summed = io is not None and "opb" not in io
    xn2T_d = get_io(pr, io, "xn2T", [D, T], BF16)
    x1T_d = get_io(pr, io, "x1T", [D, T], F32)
    opa_d = get_io(pr, io, "opa", [T, D], BF16)
    opb_d = None if summed else get_io(pr, io, "opb", [T, D], BF16)
    wgu_d = get_io(pr, io, "wgu", [8 * D, D], F32)
    wpr_d = get_io(pr, io, "wpr", [D, 4 * D], F32)
    ident_d = get_io(pr, io, "ident_d", [128, 128], BF16)
    ones1 = get_io(pr, io, "ones1", [1, 128], BF16)
    onescol = get_io(pr, io, "onescol", [128, 1], BF16)

    yT_o = get_io(pr, io, "yT_o", [D, T], F32, out=True)
    h_hbm = pr.dtmp("h_hbm", [4 * D, T], BF16)
    wqpT_hbm = pr.dtmp("wqpT_hbm", [4 * D, D], BF16)
    aT_hbm = pr.dtmp("aT_hbm", [D, T], BF16)

    NH = 4 * D
    with pr.ctx:
        o1 = pr.sb("o1", [1, 128], BF16)
        ocol = pr.sb("ocol", [128, 1], BF16)
        ident = pr.sb("ident", [128, 128], BF16)
        xn2T = [pr.sb(f"xn2T{i}", [128, T], BF16) for i in range(NDT)]
        ssq = pr.sb("ssq", [1, T], F32)
        rrow = pr.sb("rrow", [1, T], F32)
        rn16 = pr.sb("rn16", [1, T], BF16)
        RNb = pr.sb("RNb", [128, T], BF16)

        pe = pr.sem("pe")
        vm = pr.sem("vm")
        sm = pr.sem("sm")
        st = pr.sem("st")
        s_const = Stream(pr, "cst", 1)
        s_xn = Stream(pr, "sxn", 1)
        s_op = Stream(pr, "sop", 2)
        s_wg = Stream(pr, "swg", 4)
        s_hh = Stream(pr, "shh", 2)
        s_wp = Stream(pr, "swp", 2)
        s_x1 = Stream(pr, "sx1", 2)
        s_at = Stream(pr, "sat", 2)

        if pr.block is None:
            pr.block = pr.ctx.enter_context(nc.Block())
        block = pr.block
        gate_all(block, gate)

        def qc(q):
            s_const.issue(q, 0, [
                (o1[:], ones1[:], {}),
                (ocol[:], onescol[:], {}),
                (ident[:], ident_d[:], {}),
            ])
            for dt in range(NDT):
                s_xn.issue(q, 0, [(xn2T[dt][:],
                                   xn2T_d[128 * dt:128 * (dt + 1), :], {})])
        block.sync(qc)
        const_t = s_const.totals()[0]
        xn_t = s_xn.totals()[0]

        ctx0 = ExitStack()
        oa = [pr.sb(f"oa{i}", [128, D], BF16, ctx0) for i in range(2)]
        obb = [pr.sb(f"obb{i}", [128, D], BF16, ctx0) for i in range(2)]
        anat = pr.sb("anat", [128, D], BF16, ctx0)
        arow = [pr.sb(f"arow{i}", [128, 128], BF16, ctx0) for i in range(2)]
        psT = [pr.ps(f"psT{i}", [128, 1024], BF16, ctx0)[:, 0:128]
               for i in range(2)]

        for tb in range(NDT):
            def qop(q, tb=tb, vv=vm.v):
                q.wait_ge(vm.h, vv)
                dmas = [(oa[tb % 2][:], opa_d[128 * tb:128 * (tb + 1), :], {})]
                if not summed:
                    dmas.append((obb[tb % 2][:],
                                 opb_d[128 * tb:128 * (tb + 1), :], {}))
                s_op.issue(q, tb, dmas)
            block.sync(qop)
            op_t = (s_op.sems[tb % 2].h, s_op.sems[tb % 2].v)

            def vad(v, tb=tb, ot=op_t):
                v.wait_ge(*ot)
                if summed:
                    v.tensor_copy(anat[:], oa[tb % 2][:])
                else:
                    v.tensor_tensor(anat[:], oa[tb % 2][:], obb[tb % 2][:], OP.add)
                mark(v, vm)
            block.vector(vad)
            ad_v = vm.v

            pe0, sm0 = pe.v, sm.v

            def tt_(t, tb=tb, vv=ad_v, ct=const_t, sm0=sm0):
                t.wait_ge(vm.h, vv)
                t.wait_ge(*ct)
                for dt in range(NDT):
                    if dt >= 2:
                        t.wait_ge(sm.h, sm0 + dt - 1)
                    t.transpose(psT[dt % 2][:], anat[:, 128 * dt:128 * (dt + 1)],
                                ident[:]).then_inc(pe.h, 1)
            block.tensor(tt_)
            pe.inc(NDT)

            def sev(s, tb=tb, pe0=pe0):
                for dt in range(NDT):
                    s.wait_ge(pe.h, pe0 + dt + 1)
                    if dt >= 2:
                        s.wait_ge(s_at.sems[dt % 2].h, 16 * (8 * tb + dt // 2))
                    elif tb > 0:
                        s.wait_ge(s_at.sems[dt % 2].h, 16 * 8 * tb)
                    s.copy(arow[dt % 2][:], psT[dt % 2][:])
                    s.drain()
                    s.nop().then_inc(sm.h, 1)
            block.scalar(sev)
            sm.inc(NDT)

            def qat(q, tb=tb, sm0=sm0):
                for dt in range(NDT):
                    q.wait_ge(sm.h, sm0 + dt + 1)
                    s_at.issue(q, dt, [(aT_hbm[128 * dt:128 * (dt + 1),
                                               128 * tb:128 * (tb + 1)],
                                        arow[dt % 2][:], {})])
            block.sync(qat)

            def vnx(v, at=s_at.totals()):
                wait_all(v, at)
                mark(v, vm)
            block.vector(vnx)
        ctx0.close()
        at_done = vm.v

        ctx1 = ExitStack()
        wt = [pr.sb(f"wt{i}", [128, NH], BF16, ctx1) for i in range(2)]
        sums = pr.sb("sums", [128, 128], F32, ctx1)
        scal = pr.sb("scal", [128, 128], F32, ctx1)
        thr = pr.sb("thr", [128, 128], BF16, ctx1)
        nthr = pr.sb("nthr", [128, 128], BF16, ctx1)
        sc16 = pr.sb("sc16", [128, 128], BF16, ctx1)
        mm1 = pr.sb("mm1", [128, NH], BF16, ctx1)
        mm2 = pr.sb("mm2", [128, NH], BF16, ctx1)
        wqt = pr.sb("wqt", [128, NH], BF16, ctx1)
        tev = [pr.sb(f"tev{i}", [128, 128], BF16, ctx1) for i in range(2)]
        psT1 = [pr.ps(f"psT1{i}", [128, 1024], BF16, ctx1)[:, 0:128]
                for i in range(2)]
        s_wq2 = Stream(pr, "swq2", 2)

        def quant_tile(v, wtile, ng):
            v.tensor_reduce(sums[:, 0:ng],
                            wtile[:].rearrange("p (g k) -> p g k", k=64),
                            axis=AXX, op=OP.add, apply_absolute_value=True)
            v.drain()
            v.tensor_scalar(scal[:, 0:ng], sums[:, 0:ng], 1.0 / 64, None, OP.mult)
            v.drain()
            v.tensor_copy(sc16[:, 0:ng], scal[:, 0:ng])
            v.drain()
            v.tensor_copy(scal[:, 0:ng], sc16[:, 0:ng])
            v.drain()
            v.tensor_scalar(scal[:, 0:ng], scal[:, 0:ng], 1e-8, None, OP.max)
            v.drain()
            v.tensor_scalar(thr[:, 0:ng], scal[:, 0:ng], 0.5, None, OP.mult)
            v.tensor_scalar(nthr[:, 0:ng], scal[:, 0:ng], -0.5, None, OP.mult)
            v.drain()
            v.tensor_tensor(mm1[:, 0:ng * 64].rearrange("p (g k) -> p g k", k=64),
                            wtile[:].rearrange("p (g k) -> p g k", k=64),
                            thr[:, 0:ng, None].broadcast_to((128, ng, 64)),
                            OP.is_gt)
            v.tensor_tensor(mm2[:, 0:ng * 64].rearrange("p (g k) -> p g k", k=64),
                            wtile[:].rearrange("p (g k) -> p g k", k=64),
                            nthr[:, 0:ng, None].broadcast_to((128, ng, 64)),
                            OP.is_lt)
            v.drain()
            v.tensor_tensor(mm1[:, 0:ng * 64], mm1[:, 0:ng * 64],
                            mm2[:, 0:ng * 64], OP.subtract)
            v.drain()
            v.tensor_tensor(wqt[:, 0:ng * 64].rearrange("p (g k) -> p g k", k=64),
                            mm1[:, 0:ng * 64].rearrange("p (g k) -> p g k", k=64),
                            scal[:, 0:ng, None].broadcast_to((128, ng, 64)),
                            OP.mult)

        tt2_mark = {}
        vq_mark = {}
        for wb_ in range(16):
            def gw(g, wb_=wb_, vv=vm.v):
                if wb_ >= 2:
                    g.wait_ge(vm.h, vq_mark[wb_ - 2])
                else:
                    g.wait_ge(vm.h, vv)
                s_wq2.issue(g, wb_, [(
                    wt[wb_ % 2][:], wpr_d[128 * wb_:128 * (wb_ + 1), :], {})])
            block.gpsimd(gw)
            w_tt = (s_wq2.sems[wb_ % 2].h, s_wq2.sems[wb_ % 2].v)

            def vq_(v, wb_=wb_, wtt=w_tt):
                v.wait_ge(*wtt)
                if wb_ >= 1:
                    v.wait_ge(pe.h, tt2_mark[wb_ - 1])
                quant_tile(v, wt[wb_ % 2], 128)
                v.drain()
                v.nop().then_inc(vm.h, 1)
                vm.inc()
            block.vector(vq_)
            q_v = vm.v
            vq_mark[wb_] = q_v

            pe0, sm0 = pe.v, sm.v

            def tt2(t, wb_=wb_, vv=q_v, ct=const_t, sm0=sm0):
                t.wait_ge(vm.h, vv)
                t.wait_ge(*ct)
                for hb in range(64):
                    if hb >= 2:
                        t.wait_ge(sm.h, sm0 + hb - 1)
                    t.transpose(psT1[hb % 2][:], wqt[:, 128 * hb:128 * (hb + 1)],
                                ident[:]).then_inc(pe.h, 1)
            block.tensor(tt2)
            pe.inc(64)
            tt2_mark[wb_] = pe.v

            def se2(s, wb_=wb_, pe0=pe0):
                for hb in range(64):
                    s.wait_ge(pe.h, pe0 + hb + 1)
                    if hb >= 2:
                        s.wait_ge(s_wp.sems[hb % 2].h, 16 * (32 * wb_ + hb // 2))
                    elif wb_ > 0:
                        s.wait_ge(s_wp.sems[hb % 2].h, 16 * 32 * wb_)
                    s.copy(tev[hb % 2][:], psT1[hb % 2][:])
                    s.drain()
                    s.nop().then_inc(sm.h, 1)
            block.scalar(se2)
            sm.inc(64)

            def qw2(q, wb_=wb_, sm0=sm0):
                for hb in range(64):
                    q.wait_ge(sm.h, sm0 + hb + 1)
                    s_wp.issue(q, hb, [(wqpT_hbm[128 * hb:128 * (hb + 1),
                                                 128 * wb_:128 * (wb_ + 1)],
                                        tev[hb % 2][:], {})])
            block.sync(qw2)

            def vnx2(v, at=s_wp.totals()):
                wait_all(v, at)
                mark(v, vm)
            block.vector(vnx2)
        wp_done = vm.v

        psG = pr.ps("psG", [128, 1024], F32, ctx1)
        psU2 = pr.ps("psU2", [128, 1024], F32, ctx1)
        psR = pr.ps("psR", [128, 512], F32, ctx1)
        wqg = pr.sb("wqg", [128, D], BF16, ctx1)
        wqu = pr.sb("wqu", [128, D], BF16, ctx1)
        wqgT = [pr.sb(f"wqgT{i}", [128, NDT * 128], BF16, ctx1) for i in range(2)]
        wquT = [pr.sb(f"wquT{i}", [128, NDT * 128], BF16, ctx1) for i in range(2)]
        sg = pr.sb("sg", [128, 1024], F32, ctx1)
        hj = pr.sb("hj", [128, T], BF16, ctx1)
        hsq = pr.sb("hsq", [128, T], BF16, ctx1)

        wtg = [wt[0][:, 0:D], wt[0][:, D:2 * D],
               wt[1][:, 0:D], wt[1][:, D:2 * D]]

        def vz0(v, wv=wp_done):
            v.wait_ge(vm.h, wv)
            v.memset(ssq[:], 0.0)
            mark(v, vm)
        block.vector(vz0)

        tt3_mark = {}
        vqg_mark = {}
        tg_mark = {}
        for jp in range(64):
            def gg(g, jp=jp, vv=vm.v):
                if jp >= 2:
                    g.wait_ge(vm.h, vqg_mark[jp - 2])
                else:
                    g.wait_ge(vm.h, vv)
                s_wg.issue(g, (jp % 2) * 2, [(wtg[(jp % 2) * 2],
                                   wgu_d[128 * jp:128 * (jp + 1), :], {})])
                s_wg.issue(g, (jp % 2) * 2 + 1, [(wtg[(jp % 2) * 2 + 1],
                                   wgu_d[128 * (64 + jp):128 * (64 + jp + 1), :],
                                   {})])
            block.gpsimd(gg)
            g_t = [(s_wg.sems[(jp % 2) * 2].h, s_wg.sems[(jp % 2) * 2].v),
                   (s_wg.sems[(jp % 2) * 2 + 1].h, s_wg.sems[(jp % 2) * 2 + 1].v)]

            def vqg(v, jp=jp, gt=g_t):
                v.wait_ge(*gt[0])
                if jp >= 1:
                    v.wait_ge(pe.h, tt3_mark[jp - 1])
                quant_tile(v, wtg[(jp % 2) * 2], 32)
                v.drain()
                v.tensor_copy(wqg[:], wqt[:, 0:D])
                v.drain()
                v.wait_ge(*gt[1])
                quant_tile(v, wtg[(jp % 2) * 2 + 1], 32)
                v.drain()
                v.tensor_copy(wqu[:], wqt[:, 0:D]).then_inc(vm.h, 1)
                vm.inc()
            block.vector(vqg)
            q_v = vm.v
            vqg_mark[jp] = q_v

            pe0, sm0 = pe.v, sm.v

            def tt3(t, jp=jp, vv=q_v, sm0=sm0):
                t.wait_ge(vm.h, vv)
                if jp >= 2:
                    t.wait_ge(pe.h, tg_mark[jp - 2])
                for z, src in enumerate([wqg, wqu]):
                    for dt in range(NDT):
                        idx = z * NDT + dt
                        if idx >= 2:
                            t.wait_ge(sm.h, sm0 + idx - 1)
                        t.transpose(psT1[idx % 2][:],
                                    src[:, 128 * dt:128 * (dt + 1)],
                                    ident[:]).then_inc(pe.h, 1)
            block.tensor(tt3)
            pe.inc(2 * NDT)
            tt3_mark[jp] = pe.v

            def se3(s, jp=jp, pe0=pe0):
                for z, dst in enumerate([wqgT[jp % 2], wquT[jp % 2]]):
                    for dt in range(NDT):
                        idx = z * NDT + dt
                        s.wait_ge(pe.h, pe0 + idx + 1)
                        s.copy(dst[:, 128 * dt:128 * (dt + 1)], psT1[idx % 2][:])
                        s.drain()
                        s.nop().then_inc(sm.h, 1)
            block.scalar(se3)
            sm.inc(2 * NDT)
            tr_done = sm.v

            for th in range(2):
                def tg(t, jp=jp, th=th, tv=tr_done, xt=xn_t, vv=vm.v):
                    t.wait_ge(sm.h, tv)
                    t.wait_ge(*xt)
                    if th > 0 or jp > 0:
                        t.wait_ge(vm.h, vv)
                    for dt in range(NDT):
                        for tc in range(2):
                            sl = slice(1024 * th + 512 * tc,
                                       1024 * th + 512 * (tc + 1))
                            _mm = t.matmul(psG[:, 512 * tc:512 * (tc + 1)],
                                           wqgT[jp % 2][:, 128 * dt:128 * dt + 128],
                                           xn2T[dt][:, sl],
                                           start=(dt == 0), stop=(dt == NDT - 1))
                            _mm = t.matmul(psU2[:, 512 * tc:512 * (tc + 1)],
                                           wquT[jp % 2][:, 128 * dt:128 * dt + 128],
                                           xn2T[dt][:, sl],
                                           start=(dt == 0), stop=(dt == NDT - 1))
                    tmark(t, pe, _mm)
                block.tensor(tg)
                mmv = pe.v
                tg_mark[jp] = mmv

                def sh(s, th=th, mmv=mmv):
                    s.wait_ge(pe.h, mmv)
                    s.activation(sg[:], psG[:], AF.Silu).then_inc(sm.h, 1)
                    sm.inc()
                block.scalar(sh)

                def vh_(v, th=th, sv=sm.v):
                    v.wait_ge(sm.h, sv)
                    v.tensor_tensor(hj[:, 1024 * th:1024 * (th + 1)], sg[:],
                                    psU2[:], OP.mult).then_inc(vm.h, 1)
                    vm.inc()
                block.vector(vh_)

                def sh2(s, th=th, vv=vm.v):
                    s.wait_ge(vm.h, vv)
                    s.activation(hsq[:, 1024 * th:1024 * (th + 1)],
                                 hj[:, 1024 * th:1024 * (th + 1)],
                                 AF.Square).then_inc(sm.h, 1)
                    sm.inc()
                block.scalar(sh2)

                def tsq2(t, th=th, sv=sm.v):
                    t.wait_ge(sm.h, sv)
                    for tc in range(2):
                        _mm = t.matmul(psR[0:1, 0:512], ocol[:],
                                       hsq[:, 1024 * th + 512 * tc:
                                           1024 * th + 512 * (tc + 1)],
                                       start=True, stop=True)
                        _mm.then_inc(pe.h, 1)
                        pe.inc()
                        t.wait_ge(vm.h, vm.v + tc + 1)
                block.tensor(tsq2)

                for tc in range(2):
                    def vsq2(v, th=th, tc=tc, pv=pe.v - 1 + tc):
                        v.wait_ge(pe.h, pv)
                        sl = slice(1024 * th + 512 * tc, 1024 * th + 512 * (tc + 1))
                        v.tensor_tensor(ssq[0:1, sl], ssq[0:1, sl],
                                        psR[0:1, 0:512], OP.add).then_inc(vm.h, 1)
                        vm.inc()
                    block.vector(vsq2)

            def qh2(q, jp=jp, vv=vm.v):
                q.wait_ge(vm.h, vv)
                s_hh.issue(q, jp, [(h_hbm[128 * jp:128 * (jp + 1), :], hj[:], {})])
            block.sync(qh2)
            hst = (s_hh.sems[jp % 2].h, s_hh.sems[jp % 2].v)

            def vhf(v, ht=hst):
                v.wait_ge(*ht)
                mark(v, vm)
            block.vector(vhf)
        gu_done = vm.v

        def vrr(v, gd=gu_done):
            v.wait_ge(vm.h, gd)
            v.tensor_scalar(rrow[:], ssq[:], 1.0 / NH, 1.1920929e-7,
                            OP.mult, OP.add)
            mark(v, vm)
        block.vector(vrr)

        def srr(s, vv=vm.v):
            s.wait_ge(vm.h, vv)
            s.activation(rrow[:], rrow[:], AF.Ln)
            s.drain()
            s.activation(rrow[:], rrow[:], AF.Exp, scale=-0.5)
            mark(s, sm)
        block.scalar(srr)

        def vrr2(v, sv=sm.v):
            v.wait_ge(sm.h, sv)
            v.tensor_copy(rn16[:], rrow[:])
            mark(v, vm)
        block.vector(vrr2)

        def trb2(t, vv=vm.v):
            t.wait_ge(vm.h, vv)
            for tc in range(4):
                _mm = t.matmul(psG[:, 0:512], o1[:], rn16[:, 512 * tc:512 * (tc + 1)],
                         start=True, stop=True)
                _mm.then_inc(pe.h, 1)
                pe.inc()
                t.wait_ge(sm.h, sm.v + tc + 1)
        block.tensor(trb2)

        for tc in range(4):
            def srb2(s, tc=tc, pv=pe.v - 3 + tc):
                s.wait_ge(pe.h, pv)
                s.copy(RNb[:, 512 * tc:512 * (tc + 1)], psG[:, 0:512])
                s.drain()
                s.nop().then_inc(sm.h, 1)
                sm.inc()
            block.scalar(srb2)
        rnb_done = sm.v

        ctx1.close()

        ctx2 = ExitStack()
        hn = [pr.sb(f"hn{i}", [128, 512], BF16, ctx2) for i in range(64)]
        wpb = [pr.sb(f"wpb{i}", [128, 64 * 128], BF16, ctx2) for i in range(2)]
        x1f = [pr.sb(f"x1f{i}", [128, 512], F32, ctx2) for i in range(2)]
        atb = [pr.sb(f"atb{i}", [128, 512], BF16, ctx2) for i in range(2)]
        fo = pr.sb("fo", [128, 512], F32, ctx2)
        psO = [pr.ps(f"psO{i}", [128, 512], F32, ctx2) for i in range(2)]
        s_hn = Stream(pr, "shn", 4)
        s_wpb = Stream(pr, "swpb", 2)

        for tcol in range(4):
            ts_ = slice(512 * tcol, 512 * (tcol + 1))
            for ht in range(64):
                def qhn(q, ht=ht, tcol=tcol, vv=vm.v, rd=rnb_done):
                    q.wait_ge(sm.h, rd)
                    if ht >= 4 or tcol > 0:
                        q.wait_ge(vm.h, vv)
                    s_hn.issue(q, ht, [(hn[ht][:],
                                        h_hbm[128 * ht:128 * (ht + 1), ts_], {})])
                block.sync(qhn)
                h_t = (s_hn.sems[ht % 4].h, s_hn.sems[ht % 4].v)

                def vhn(v, ht=ht, tcol=tcol, htt=h_t):
                    v.wait_ge(*htt)
                    v.tensor_tensor(hn[ht][:], hn[ht][:], RNb[:, ts_], OP.mult)
                    mark(v, vm)
                block.vector(vhn)
            hn_done = vm.v

            def gwp0(g, tcol=tcol, hd=hn_done):
                g.wait_ge(vm.h, hd)
                s_wpb.issue(g, 0, [(
                    wpb[0][:].rearrange("p (a c) -> p a c", c=128),
                    wqpT_hbm[:, 0:128].rearrange("(a p) c -> p a c", p=128), {})])
                s_wpb.issue(g, 1, [(
                    wpb[1][:].rearrange("p (a c) -> p a c", c=128),
                    wqpT_hbm[:, 128:256].rearrange("(a p) c -> p a c", p=128), {})])
            block.gpsimd(gwp0)
            wpb_t = {0: s_wpb.totals()[0], 1: s_wpb.totals()[1]}

            vf_mark = {}
            for dc in range(16):
                def tp(t, dc=dc, tcol=tcol, hd=hn_done, wt_=wpb_t[dc % 2], vv=vm.v):
                    t.wait_ge(vm.h, hd)
                    t.wait_ge(*wt_)
                    if dc >= 2:
                        t.wait_ge(vm.h, vf_mark[dc - 2])
                    for ht in range(64):
                        _mm = t.matmul(psO[dc % 2][:],
                                       wpb[dc % 2][:, 128 * ht:128 * (ht + 1)],
                                       hn[ht][:], start=(ht == 0), stop=(ht == 63))
                    tmark(t, pe, _mm)
                block.tensor(tp)
                mmv = pe.v
                if dc < 14:
                    def gwp(g, dc=dc, tcol=tcol, mmv=mmv):
                        g.wait_ge(pe.h, mmv)
                        tgt = s_wpb.issue(g, dc, [(
                            wpb[dc % 2][:].rearrange("p (a c) -> p a c", c=128),
                            wqpT_hbm[:, 128 * (dc + 2):128 * (dc + 3)].rearrange(
                                "(a p) c -> p a c", p=128), {})])
                        wpb_t[dc % 2] = tgt
                    block.gpsimd(gwp)

                def qx1(q, dc=dc, tcol=tcol, vv=vm.v):
                    q.wait_ge(vm.h, vv)
                    s_x1.issue(q, dc, [(x1f[dc % 2][:],
                                        x1T_d[128 * dc:128 * (dc + 1), ts_], {})])
                    s_at.issue(q, dc, [(atb[dc % 2][:],
                                        aT_hbm[128 * dc:128 * (dc + 1), ts_], {})])
                block.sync(qx1)
                x1_t = (s_x1.sems[dc % 2].h, s_x1.sems[dc % 2].v)
                at_t = (s_at.sems[dc % 2].h, s_at.sems[dc % 2].v)

                def vf(v, dc=dc, mmv=mmv, xt=x1_t, at=at_t):
                    v.wait_ge(pe.h, mmv)
                    v.wait_ge(*xt)
                    v.wait_ge(*at)
                    v.tensor_tensor(fo[:], psO[dc % 2][:], atb[dc % 2][:], OP.add)
                    v.drain()
                    v.tensor_tensor(fo[:], fo[:], x1f[dc % 2][:], OP.add)
                    mark(v, vm)
                block.vector(vf)

                def qy(q, dc=dc, tcol=tcol, vv=vm.v):
                    q.wait_ge(vm.h, vv)
                    q.dma_start(out=yT_o[128 * dc:128 * (dc + 1), ts_],
                                in_=fo[:]).then_inc(st.h, 16)
                    st.inc(16)
                block.sync(qy)
                stv = st.v

                def vf2(v, sv=stv):
                    v.wait_ge(st.h, sv)
                    mark(v, vm)
                block.vector(vf2)
                vf_mark[dc] = vm.v

        ctx2.close()

        def qfin(q):
            q.wait_ge(st.h, st.v)
        block.sync(qfin)
        pr.finish = [(pe.h, pe.v), (vm.h, vm.v), (sm.h, sm.v), (st.h, st.v)]
        pr.ctx.pop_all()

    return pr





RG_PAIR = [[0, 1], [2, 3], [4, 5], [6, 7]]


def build_fused(nphase=5, nocoll=False):
    pr = Prog("fz")
    nc = pr.nc
    io_sh = {
        "ones1": pr.din("ones1", [1, 128], BF16),
        "onescol": pr.din("onescol", [128, 1], BF16),
        "ident_d": pr.din("ident_d", [128, 128], BF16),
    }
    ins = {
        "xT": pr.din("xT", [D, T], F32),
        "winT": pr.din("winT", [D, 3 * D], F32),
        "binp": pr.din("binp", [128, 48], F32),
        "cosT": pr.din("cosT", [512, T], BF16),
        "sinT": pr.din("sinT", [512, T], BF16),
        "wq": pr.din("wq", [D, 512], F32),
        "keysT": pr.din("keysT", [8, 64, 128], F32),
        "iota64": pr.din("iota64", [128, 64], F32),
        "ecat": pr.din("ecat", [16384, 4096], F32),
        "woutT": pr.din("woutT", [D, D], F32),
        "bout_d": pr.din("bout_d", [128, 16], F32),
        "kvaT": pr.din("kvaT", [D, 576], F32),
        "lnw_d": pr.din("lnw_d", [128, 4], F32),
        "kvbn": pr.din("kvbn", [512, 2048], F32),
        "kvbv": pr.din("kvbv", [512, 2048], F32),
        "qpT": pr.din("qpT", [D, 1536], F32),
        "opT": pr.din("opT", [1024, D], F32),
        "masks": pr.din("masks", [4, 128, 512], F32),
        "wgu": pr.din("wgu", [8 * D, D], F32),
        "wpr": pr.din("wpr", [D, 4 * D], F32),
    }
    yT_o = pr.dout("yT_o", [D, T], F32)

    scr = {
        "xnT_s": pr.dtmp("xnT_s", [D, T], BF16),
        "qT_s": pr.dtmp("qT_s", [D, T], BF16),
        "apart_s": pr.dtmp("apart_s", [128, 16], F32),
        "apart_sum": pr.dtmp("apart_sum", [128, 16], F32),
        "fi_s": pr.dtmp("fi_s", [16, 128, 32], U32),
        "fs_s": pr.dtmp("fs_s", [16, 128, 32], F32),
        "x1T_s": pr.dtmp("x1T_s", [D, T], F32),
        "x1s2": pr.dtmp("x1s2", [D, T], F32),
        "xn2T_s": pr.dtmp("xn2T_s", [D, T], BF16),
        "xn2f_ag": pr.dtmp("xn2f_ag", [2 * D, T], BF16),
        "kT_s": pr.dtmp("kT_s", [2048, S], BF16),
        "krot_s": pr.dtmp("krot_s", [64, S], BF16),
        "v_s": pr.dtmp("v_s", [S, 2048], BF16),
        "q2T_s": pr.dtmp("q2T_s", [1536, S], BF16),
        "op_s": pr.dtmp("op_s", [S, D], BF16),
        "opsum_s": pr.dtmp("opsum_s", [T, D], BF16),
    }

    scr["ecat16"] = pr.dtmp("ecat16", [16384, 4096], BF16)
    gbar = pr.sem("gbar")
    ccA = pr.sem("ccA")
    ccG = pr.sem("ccG")
    ccR = pr.sem("ccR")
    cast_s = pr.sem("cast_s")

    pr.block = pr.ctx.enter_context(pr.nc.Block())

    def gcast(g):
        for ch in range(16):
            g.dma_start(out=scr["ecat16"][1024 * ch:1024 * (ch + 1), :],
                        in_=ins["ecat"][1024 * ch:1024 * (ch + 1), :]
                        ).then_inc(cast_s.h, 16)
            cast_s.inc(16)
    pr.block.gpsimd(gcast)
    cast_t = (cast_s.h, cast_s.v)

    def phase(fn, io, coll=None, extra_fin=None):
        pr.pctx = ExitStack()
        fn(pr=pr, io=io, gate=(gbar.h, gbar.v) if gbar.v else None)
        pr.pctx.close()
        pr.pctx = None
        block = pr.block

        def qbar(q, fin=pr.finish + (extra_fin or [])):
            wait_all(q, fin)
            q.nop().then_inc(gbar.h, 1)
        block.sync(qbar)
        gbar.inc()

        if coll is not None and nocoll:
            coll = None
        if coll is not None:
            kind, op_, cin, cout, csem = coll

            def gcc(g, gv=gbar.v):
                g.wait_ge(gbar.h, gv)
                g.collective_compute(kind, op_, ins=[cin.opt()], outs=[cout.opt()],
                                     replica_groups=RG_PAIR).then_inc(csem.h, 1)
            block.gpsimd(gcc)
            csem.inc(1)

            def qcc(q, cs=(csem.h, csem.v)):
                q.wait_ge(*cs)
                q.nop().then_inc(gbar.h, 1)
            block.sync(qcc)
            gbar.inc()

    pr.prefix = "a_"
    phase(build_l0a,
          {**io_sh, **{k: ins[k] for k in ["xT", "winT", "binp", "cosT", "sinT",
                                           "wq", "keysT", "iota64"]},
           "xnT_o": scr["xnT_s"], "qT_o": scr["qT_s"], "apart_o": scr["apart_s"],
           "fi_o": scr["fi_s"], "fs_o": scr["fs_s"]},
          coll=("AllReduce", OP.add, scr["apart_s"], scr["apart_sum"], ccA),
          extra_fin=[cast_t])

    if nphase <= 1:
        return pr
    pr.prefix = "b_"
    phase(build_l0b,
          {**io_sh, "xT": ins["xT"], "xnT_i": scr["xnT_s"], "qT_i": scr["qT_s"],
           "ap0": scr["apart_sum"], "fi_i": scr["fi_s"], "fs_i": scr["fs_s"],
           "ecat": scr["ecat16"], "woutT": ins["woutT"], "bout_d": ins["bout_d"],
           "x1T_o": scr["x1T_s"], "xn2T_o": scr["xn2T_s"], "x1s": scr["x1s2"]},
          coll=("AllGather", OP.bypass, scr["xn2T_s"], scr["xn2f_ag"], ccG))

    if nphase <= 2:
        return pr
    pr.prefix = "c_"
    phase(build_l1a,
          {**io_sh, "xn2f_ag": scr["xn2f_ag"],
           **{k: ins[k] for k in ["kvaT", "lnw_d", "kvbn", "kvbv", "qpT"]},
           "kT_o": scr["kT_s"], "krotT_o": scr["krot_s"], "v_o": scr["v_s"],
           "q2T_o": scr["q2T_s"]})

    if nphase <= 3:
        return pr
    pr.prefix = "d_"
    phase(build_l1b,
          {**io_sh, "kT": scr["kT_s"][0:1024, :], "krotT": scr["krot_s"],
           "v": scr["v_s"][:, 0:1024], "qT": scr["q2T_s"], "opT": ins["opT"],
           "masks": ins["masks"], "op_o": scr["op_s"]},
          coll=("ReduceScatter", OP.add, scr["op_s"], scr["opsum_s"], ccR))

    if nphase <= 4:
        return pr
    pr.prefix = "e_"
    phase(build_l1c,
          {**io_sh, "xn2T": scr["xn2T_s"], "x1T": scr["x1T_s"],
           "opa": scr["opsum_s"], "wgu": ins["wgu"], "wpr": ins["wpr"],
           "yT_o": yT_o})

    return pr


def fused_inmaps(prep):
    cos, sin = prep["cosT"], prep["sinT"]
    maps = []
    for c in range(NCORE):
        r = c % 2
        horder = list(range(r * 8, r * 8 + 8)) + list(range((1 - r) * 8,
                                                            (1 - r) * 8 + 8))
        cols = np.concatenate([np.arange(128) + 128 * h for h in horder])
        maps.append({
            "xT": prep["xT"][c],
            "winT": prep["winT"], "binp": prep["binp"],
            "cosT": np.ascontiguousarray(cos[:, r * T:(r + 1) * T]),
            "sinT": np.ascontiguousarray(sin[:, r * T:(r + 1) * T]),
            "wq": prep["wq"], "keysT": prep["keysT"], "iota64": prep["iota64"],
            "ecat": prep["e_cat"], "woutT": prep["woutT"], "bout_d": prep["bout"],
            "kvaT": prep["kvaT"], "lnw_d": prep["lnw"],
            "kvbn": np.ascontiguousarray(prep["kvbT_nope"][:, cols]),
            "kvbv": np.ascontiguousarray(prep["kvbT_v"][:, cols]),
            "qpT": np.ascontiguousarray(prep["qprojT"][:, r * 1536:(r + 1) * 1536]),
            "opT": np.ascontiguousarray(prep["oprojT"][r * 1024:(r + 1) * 1024]),
            "masks": prep["masks"],
            "wgu": prep["wgu"], "wpr": prep["wproj"],
            "ones1": prep["ones1"], "onescol": prep["onescol"],
            "ident_d": prep["ident"],
        })
    return maps


_PROGS = {}


def _prog(name):
    if name not in _PROGS:
        _PROGS[name] = {"l0a": build_l0a, "l0b": build_l0b, "l1a": build_l1a,
                        "l1b": build_l1b, "l1c": build_l1c}[name]()
    return _PROGS[name]


def _run(name, maps):
    pr = _prog(name)
    res = run_bass_kernel_spmd(pr.nc, maps, core_ids=list(range(NCORE)))
    return res.results


MODE = "fused"


def kernel(**inputs):
    prep = host_prep(inputs)
    if MODE == "fused":
        if "fused" not in _PROGS:
            _PROGS["fused"] = build_fused()
        pr = _PROGS["fused"]
        res = run_bass_kernel_spmd(pr.nc, fused_inmaps(prep),
                                   core_ids=list(range(NCORE)))
        out = np.empty((4, S, D), np.float32)
        for c in range(NCORE):
            b, r = c // 2, c % 2
            out[b, r * T:(r + 1) * T, :] = res.results[c]["yT_o"].T
        return out
    ra = _run("l0a", l0a_inmaps(prep))
    rb = _run("l0b", l0b_inmaps(prep, ra))

    maps = []
    for c in range(NCORE):
        base, r = (c // 2) * 2, c % 2
        maps.append({
            "xn2f": np.concatenate([rb[base]["xn2T_o"], rb[base + 1]["xn2T_o"]],
                                   axis=1),
            "kvaT": prep["kvaT"], "lnw_d": prep["lnw"],
            "kvbn": prep["kvbT_nope"], "kvbv": prep["kvbT_v"],
            "qpT": np.ascontiguousarray(prep["qprojT"][:, r * 1536:(r + 1) * 1536]),
            "ones1": prep["ones1"], "onescol": prep["onescol"]})
    r1a = _run("l1a", maps)

    maps = []
    for c in range(NCORE):
        r = c % 2
        maps.append({
            "kT": np.ascontiguousarray(r1a[c]["kT_o"][r * 1024:(r + 1) * 1024]),
            "krotT": r1a[c]["krotT_o"],
            "v": np.ascontiguousarray(r1a[c]["v_o"][:, r * 1024:(r + 1) * 1024]),
            "qT": r1a[c]["q2T_o"],
            "opT": np.ascontiguousarray(prep["oprojT"][r * 1024:(r + 1) * 1024]),
            "masks": prep["masks"],
            "ones1": prep["ones1"], "onescol": prep["onescol"]})
    r1b = _run("l1b", maps)

    maps = []
    for c in range(NCORE):
        r = c % 2
        part = c + 1 if r == 0 else c - 1
        maps.append({
            "xn2T": rb[c]["xn2T_o"], "x1T": rb[c]["x1T_o"],
            "opa": np.ascontiguousarray(r1b[c]["op_o"][r * T:(r + 1) * T]),
            "opb": np.ascontiguousarray(r1b[part]["op_o"][r * T:(r + 1) * T]),
            "wgu": prep["wgu"], "wpr": prep["wproj"],
            "ident_d": prep["ident"], "ones1": prep["ones1"],
            "onescol": prep["onescol"]})
    r1c = _run("l1c", maps)

    out = np.empty((4, S, D), np.float32)
    for c in range(NCORE):
        b, r = c // 2, c % 2
        out[b, r * T:(r + 1) * T, :] = r1c[c]["yT_o"].T
    return out
